# version 10 (final)
# baseline (speedup 1.0000x reference)
import numpy as np
import ml_dtypes
import concourse.bass as bass
import concourse.mybir as mybir
from concourse.bass_utils import run_bass_kernel_spmd

F32 = mybir.dt.float32
F16 = mybir.dt.float16
BF16 = mybir.dt.bfloat16
I32 = mybir.dt.int32
AF = mybir.ActivationFunctionType
ALU = mybir.AluOpType
AXX = mybir.AxisListType.X

D = 1024
NH = 16
L = 128
W = 32
NSUB = 4
EPS = 1e-5
KSL = 8

KT_OFF = [0, 32, 96, 192]
KT_SZ = [32, 64, 96, 128]
NA = 321
DELTA_C = 320
NB = 256
ECOLS = NA + NB


def build_constants():
    ind = np.triu(np.ones((L, L), np.float32))
    UA = np.zeros((L, NA), np.float32)
    for i in range(NSUB):
        ri = i * W - 1
        for s in range(W * (i + 1)):
            UA[:, KT_OFF[i] + s] = ind[:, s] - (ind[:, ri] if ri >= 0 else 0.0)
    UA[:, DELTA_C] = -1.0
    UB = np.zeros((L, NB), np.float32)
    for i in range(NSUB):
        ri = i * W - 1
        for a in range(W):
            t = i * W + a
            UB[:, 32 * i + a] = (ind[:, ri] if ri >= 0 else 0.0) - ind[:, t]
    for t in range(L):
        UB[:, 128 + t] = -ind[:, t]
    UBr = np.zeros((L, NB), np.float32)
    for i in range(NSUB):
        for a in range(W):
            UBr[i * W + a, 32 * i + a] = 1.0
    for t in range(L):
        UBr[t, 128 + t] = 1.0
    UKH = np.zeros((L, L), np.float32)
    for s in range(L):
        UKH[:, s] = ind[:, s] - ind[:, 127]
    MSK = np.zeros((L, NSUB * W), np.float32)
    for i in range(NSUB):
        MSK[: i * W, 32 * i: 32 * (i + 1)] = 1.0
        blk = (np.arange(W)[:, None] <= np.arange(W)[None, :]).astype(np.float32)
        MSK[i * W: (i + 1) * W, 32 * i: 32 * (i + 1)] = blk
    return dict(
        c_UA=UA.astype(np.float16), c_UB=UB.astype(np.float16),
        c_UBr=UBr.astype(np.float16), c_UKH=UKH.astype(np.float16),
        c_MSK=MSK.astype(ml_dtypes.bfloat16),
        c_ID=np.eye(128, dtype=np.float32).astype(ml_dtypes.bfloat16),
    )


class P:
    ENGS = ("sync", "tensor", "vector", "scalar", "gpsimd")

    def __init__(self, nc, ctx):
        self.nc = nc
        self.ctx = ctx
        self.streams = {e: [] for e in self.ENGS}
        self.h = {}
        self.cnt = {}
        self.marks = {}
        self.use = {}
        for name in ("sT", "sV", "sS", "sP"):
            self.h[name] = ctx.enter_context(nc.semaphore(name))
            self.cnt[name] = 0

    def mksem(self, name):
        if name not in self.h:
            self.h[name] = self.ctx.enter_context(self.nc.semaphore(name))
            self.cnt[name] = 0

    def op(self, eng, fn, wait=(), inc=None, inc_n=1):
        waits = []
        for tok in wait:
            if tok is None:
                continue
            if isinstance(tok, list):
                for t2 in tok:
                    if t2 is not None:
                        waits.append(t2)
                continue
            waits.append(tok)
        dedup = []
        for sem, val in waits:
            key = (eng, sem)
            if self.marks.get(key, 0) < val:
                dedup.append((sem, val))
                self.marks[key] = val
        tok_out = None
        if inc is not None:
            self.cnt[inc] += inc_n
            tok_out = (inc, self.cnt[inc])

        def thunk(e, waits=tuple(dedup), fn=fn, inc=inc, inc_n=inc_n):
            for sem, val in waits:
                e.wait_ge(self.h[sem], val)
            ins = fn(e)
            if inc is not None:
                ins.then_inc(self.h[inc], inc_n)
        self.streams[eng].append(thunk)
        return tok_out

    def dma(self, eng, out, in_, wait=(), q=None, slow=False):
        q = q or ("qS" if eng == "sync" else "qG")
        self.mksem(q)
        k = self.cnt[q] // 16 + 1
        gate = [(q, 16 * (k - 1))] if k >= 2 else []
        return self.op(eng, lambda e, o=out, i=in_, s=slow: e.dma_start(
            out=o, in_=i, allow_slow_non_contiguous=s),
            wait=list(wait) + gate, inc=q, inc_n=16)

    def rd(self, name, tok):
        self.use.setdefault(name, []).append(tok)
        return tok

    def wr(self, name):
        toks = self.use.pop(name, [])
        return toks

    def finish(self, block):
        def run(eng, thunks):
            for t in thunks:
                t(eng)
        for e in self.ENGS:
            getattr(block, e)(lambda eng, th=tuple(self.streams[e]): run(eng, th))


ALLOC_LOG = []


def build(T=2048, dbg=False, stage=4, sub=9, v604=0):
    ALLOC_LOG.clear()
    NCH = T // L
    nc = bass.Bass("TRN2", target_bir_lowering=False,
                   dynamic_dma_scratch_size=2048)
    from contextlib import ExitStack
    ctx = ExitStack()

    def din(name, shape, dt=F32):
        return nc.dram_tensor(name, shape, dt, kind="ExternalInput").ap()

    x = din("x", [T, D])
    ln_g, ln_b, bw = din("ln_g", [D]), din("ln_b", [D]), din("bw", [D])
    Wx, Ww = din("Wx", [D, D]), din("Ww", [D, D])
    Wk, Wv, Wr, Wo = (din(n, [D, D]) for n in ("Wk", "Wv", "Wr", "Wo"))
    g2, b2v = din("g2", [D]), din("b2v", [D])
    q1, s1, b1 = din("q1", [D, 4096], I32), din("s1", [8, 4096]), din("b1", [4096])
    q2, s2, b2f = din("q2", [4096, D], I32), din("s2", [32, D]), din("b2f", [D])
    c_UA, c_UB = din("c_UA", [L, NA], F16), din("c_UB", [L, NB], F16)
    c_UBr, c_UKH = din("c_UBr", [L, NB], F16), din("c_UKH", [L, L], F16)
    c_MSK, c_ID = din("c_MSK", [L, 128], BF16), din("c_ID", [128, 128], BF16)

    out = nc.dram_tensor("out", [T, D], F32, kind="ExternalOutput").ap()
    ikind = "ExternalOutput" if dbg else "Internal"
    xmid_d = nc.dram_tensor("xmid_d", [T, D], F32, kind=ikind).ap()
    xn2_d = nc.dram_tensor("xn2_d", [T, D], BF16, kind=ikind).ap()
    z2a_d = nc.dram_tensor("z2a_d", [T, D], F32, kind=ikind).ap()
    s1b_d = nc.dram_tensor("s1b_d", [8, 4096], BF16, kind=ikind).ap()
    s2b_d = nc.dram_tensor("s2b_d", [32, D], BF16, kind=ikind).ap()
    ckrow_d = nc.dram_tensor("ckrow_d", [1, D], F32, kind="Internal").ap()
    if dbg:
        scrd_d = nc.dram_tensor("scrd_d", [T, D], F32, kind="ExternalOutput").ap()
        spd_d = nc.dram_tensor("spd_d", [T, D], F16, kind="ExternalOutput").ap()
        sprd_d = nc.dram_tensor("sprd_d", [T, D], F16,
                                kind="ExternalOutput").ap()
        ebd_d = nc.dram_tensor("ebd_d", [T, 8 * ECOLS], BF16,
                               kind="ExternalOutput").ap()
        yfmd_d = nc.dram_tensor("yfmd_d", [T, D], BF16,
                                kind="ExternalOutput").ap()
        f1d_d = nc.dram_tensor("f1d_d", [128, 16384], BF16,
                               kind="ExternalOutput").ap()
        f2d_d = nc.dram_tensor("f2d_d", [128, 16384], BF16,
                               kind="ExternalOutput").ap()
        fhd_d = nc.dram_tensor("fhd_d", [128, 8192], BF16,
                               kind="ExternalOutput").ap()
        cbd_d = nc.dram_tensor("cbd_d", [128, 32], F32,
                               kind="ExternalOutput").ap()
        cbr_d = nc.dram_tensor("cbr_d", [1, 4096], BF16,
                               kind="ExternalOutput").ap()
        zbd_d = nc.dram_tensor("zbd_d", [1, 512], BF16,
                               kind="ExternalOutput").ap()
        qd_d = nc.dram_tensor("qd_d", [128, 2048], I32,
                              kind="ExternalOutput").ap()
        scd_d = nc.dram_tensor("scd_d", [128, 1024], F32,
                               kind="ExternalOutput").ap()
        arn0_d = nc.dram_tensor("arn0_d", [128, 512], F32,
                                kind="ExternalOutput").ap()
        arn1_d = nc.dram_tensor("arn1_d", [128, 512], F32,
                                kind="ExternalOutput").ap()
        arnc_d = nc.dram_tensor("arnc_d", [2048, 512], F32,
                                kind="ExternalOutput").ap()
    cb1_d = nc.dram_tensor("cb1_d", [1, 4096], F32, kind="Internal").ap()
    cvrow_d = nc.dram_tensor("cvrow_d", [1, D], F32, kind="Internal").ap()

    ARENA = (nc.sbuf_bytes_remaining - 64) & ~31
    arena = nc.alloc_sbuf_tensor("arena", [128, ARENA], mybir.dt.uint8)
    abase = nc.lookup_mloc(arena).addr
    sb_off = [abase]

    def sb(name, shape, dt):
        at = sb_off[0]
        per = int(np.prod(shape[1:])) * mybir.dt.size(dt)
        sb_off[0] = (at + per + 31) & ~31
        assert sb_off[0] <= abase + ARENA, (name, sb_off[0] - abase)
        ALLOC_LOG.append((name, at - abase, per, shape[0]))
        return nc.alloc_sbuf_tensor_at(name, shape, dt, offset=at)

    sb("scratchpad", [128, 6144], mybir.dt.uint8)
    gcol = sb("gcol", [128, KSL], F32)
    bgcol = sb("bgcol", [128, KSL], F32)
    g2col = sb("g2col", [128, KSL], F32)
    bg2col = sb("bg2col", [128, KSL], F32)
    g2bias = sb("g2bias", [128, KSL], F32)
    bgcol_b = sb("bgcol_b", [128, KSL], BF16)
    bg2col_b = sb("bg2col_b", [128, KSL], BF16)
    stats = sb("stats", [128, 128], F32)
    m3c = sb("m3c", [128, 3 * 16], F32)
    rowsA = sb("rowsA", [128, 3072], F32)
    rowsB = sb("rowsB", [128, 4096], BF16)
    rowsC = sb("rowsC", [128, 3 * 1024 + 128], BF16)
    zb = sb("zb", [128, 512], BF16)
    onef = sb("onef", [128, 1], F32)
    rowtmp = rowsA[0:1, 0:2048]
    bwrow = rowsA[0:1, 2048:3072]
    b1row = rowsB[0:1, :]
    crow_w = rowsC[0:1, 0:D]
    cb2row = rowsC[0:1, D:2 * D]
    crow_r = rowsC[0:1, 2 * D:3 * D]
    ones = rowsC[0:1, 3 * D:3 * D + 128]
    ones0 = ones

    base_big = sb_off[0]

    uA = sb("uA", [128, NA], F16)
    uB = sb("uB", [128, NB], F16)
    uBr = sb("uBr", [128, NB], F16)
    uKH = sb("uKH", [128, 128], F16)
    msk = sb("msk", [128, 128], BF16)
    idn = sb("idn", [128, 128], BF16)
    ckb = sb("ckb", [128, D], BF16)
    cvb = sb("cvb", [128, D], BF16)
    S0 = sb("S0", [128, 512], BF16)
    S1 = sb("S1", [128, 512], BF16)
    Sd = sb("Sd", [128, 512], BF16)
    S2 = sb("S2", [128, KSL * 128], BF16)
    wM = sb("wM", [128, KSL * D], BF16)
    wK = sb("wK", [128, KSL * D], BF16)
    wV = sb("wV", [128, KSL * D], BF16)
    wR = sb("wR", [128, KSL * D], BF16)
    wO = sb("wO", [128, KSL * D], BF16)
    xbuf = [sb(f"xbuf{i}", [128, D], F32) for i in range(2)]
    xn_tm = [sb("xn_tm0", [128, D], BF16)]
    xn_fm = [sb(f"xn_fm{i}", [128, D], BF16) for i in range(2)]
    spb = [sb(f"spb{i}", [128, D], F16) for i in range(2)]
    sprb = [sb(f"sprb{i}", [128, D], F16) for i in range(2)]
    scr = [sb(f"scr{i}", [128, D], F32) for i in range(2)]
    sqd = sb("sqd", [128, D], BF16)
    Eb = [sb(f"Eb{i}", [128, KSL * ECOLS], BF16) for i in range(2)]
    Ekh = [sb(f"Ekh{i}", [128, D], BF16) for i in range(2)]
    ktm = [sb(f"ktm{i}", [128, D], BF16) for i in range(2)]
    kfm = [sb(f"kfm{i}", [128, D], BF16) for i in range(2)]
    vtm = [sb(f"vtm{i}", [128, D], BF16) for i in range(2)]
    khtm = [sb(f"khtm{i}", [128, D], BF16) for i in range(2)]
    ktil = [sb(f"ktil{i}", [128, KSL * 128], BF16) for i in range(2)]
    pevac = [sb(f"pevac{i}", [128, 512], BF16) for i in range(2)]
    yfm = [sb(f"yfm{i}", [128, D], BF16) for i in range(2)]
    xmidb = [sb(f"xmidb{i}", [128, D], F32) for i in range(2)]
    xn2b_off = sb_off[0]
    xn2b = [sb(f"xn2b{i}", [128, D], BF16) for i in range(2)]
    xn2f = [sb(f"xn2f{i}", [128, D], BF16) for i in range(2)]
    cb1r = nc.alloc_sbuf_tensor_at("cb1r", [128, 4096], BF16,
                                   offset=xn2b_off)
    attn_end = sb_off[0]

    sb_off[0] = base_big
    fw1 = sb("fw1", [128, KSL * 2048], BF16)
    fw2 = sb("fw2", [128, 16 * D], BF16)
    qst = [sb(f"qst{i}", [128, 2048], I32) for i in range(2)]
    sbc = [sb(f"sbc{i}", [128, 2048], BF16) for i in range(2)]
    qcv = [sb(f"qcv{i}", [128, 2048], BF16) for i in range(2)]
    fxn2 = [sb(f"fxn2{i}", [128, KSL * 512], BF16) for i in range(2)]
    fh = sb("fh", [128, 16 * 512], BF16)
    fsq = sb("fsq", [128, D], BF16)
    fz2a = [sb(f"fz2a{i}", [128, D], F32) for i in range(2)]
    fxmid = [sb(f"fxmid{i}", [128, D], F32) for i in range(4)]
    ft = [sb(f"ft{i}", [128, D], F32) for i in range(4)]
    fo = [sb(f"fo{i}", [128, D], F32) for i in range(2)]
    ffn_end = sb_off[0]

    stg_off = (attn_end - KSL * D * 2 - 2 * D * 4 - 64) & ~31
    wstage = [nc.alloc_sbuf_tensor_at(f"wstage{i}", [128, D], F32,
                                      offset=stg_off + i * D * 4) for i in range(2)]
    mstage = nc.alloc_sbuf_tensor_at("mstage", [128, KSL * D], BF16,
                                     offset=stg_off + 2 * D * 4)

    assert max(attn_end, ffn_end) <= abase + ARENA, (attn_end - abase,
                                                     ffn_end - abase)

    pool = [nc.alloc_psum_tensor(f"pp{i}", [128, 512], F32) for i in range(5)]
    outp = nc.alloc_psum_tensor("outp", [128, 1024], F32).ap()
    sadd = nc.alloc_psum_tensor("sadd", [128, 512], F32).ap()

    p = P(nc, ctx)

    def mm_group(psl, lhsTs, rhss, wait, extra=None):
        n = len(lhsTs)
        for j in range(n):
            first = (j == 0)
            last = (j == n - 1) and extra is None
            tk = p.op("tensor",
                      lambda e, o=psl, a=lhsTs[j], b=rhss[j], f=first, s=last:
                      e.matmul(o, a, b, start=f, stop=s, skip_group_check=True),
                      wait=wait if j == 0 else (),
                      inc="sT" if last else None)
        if extra is not None:
            oa, ra = extra
            tk = p.op("tensor",
                      lambda e, o=psl, a=oa, b=ra:
                      e.matmul(o, a, b, start=False, stop=True,
                               skip_group_check=True),
                      inc="sT")
        return tk

    def eslice(buf, sl, c0, c1):
        return buf[:, ECOLS * sl + c0: ECOLS * sl + c1]

    tU = [p.dma("sync", uA[:, :], c_UA, q="qC1"), p.dma("sync", uB[:, :], c_UB, q="qC2"),
          p.dma("sync", uBr[:, :], c_UBr, q="qC3"), p.dma("sync", uKH[:, :], c_UKH, q="qC0"),
          p.dma("sync", msk[:, :], c_MSK, q="qC1"), p.dma("sync", idn[:, :], c_ID, q="qC2")]
    def col_stage(dst, vec, q_):
        trow = p.dma("sync", rowtmp[0:1, 0:1024], vec[None, :],
                     wait=p.wr("rowtmp"), q=q_)
        tm = None
        for s in range(KSL):
            tm = p.op("tensor", lambda e, s_=s: e.matmul(
                pool[1].ap()[:, s_:s_ + 1],
                rowtmp[0:1, 128 * s_:128 * (s_ + 1)],
                onef[0:1, 0:1], start=(s_ == 0), stop=(s_ == KSL - 1),
                skip_group_check=True),
                wait=([trow, tonef] + p.wr("pl1")) if s == 0 else (),
                inc="sT" if s == KSL - 1 else None)
        p.rd("rowtmp", tm)
        tev = p.op("scalar", lambda e, d_=dst: e.activation(
            out=d_[:, :], in_=pool[1].ap()[:, 0:KSL], func=AF.Identity),
            wait=[tm], inc="sS")
        p.use["pl1"] = [tev]
        return tev

    tonef = p.op("vector", lambda e: e.memset(onef[0:1, 0:1], 1.0), inc="sV")
    tg = col_stage(gcol, ln_g, "qC3")
    tb = col_stage(bgcol, ln_b, "qC0")
    tg2 = col_stage(g2col, g2, "qC1")
    tb2 = col_stage(bg2col, b2v, "qC2")
    if dbg:
        p.dma("sync", cbd_d[:, 24:32], gcol[:, :], wait=[tg], q="qNO")
        p.dma("sync", scd_d[:, 0:8], g2col[:, :], wait=[tg2], q="qNO")
        arenaf = arena[:, :].bitcast(F32)
        p.dma("sync", arn0_d[:, :], arenaf[:, 0:512], wait=[tg2, tb2],
              q="qNO")
    tbw = p.dma("sync", bwrow[:, :], bw[None, :], q="qC3")
    tone = p.op("vector", lambda e: e.memset(ones[:, :], 1.0), inc="sV")
    tzb = p.op("vector", lambda e: e.memset(zb[0:1, 0:512], 1.0), inc="sV")
    tS = p.op("vector", lambda e: e.memset(S0[:, :], 0.0), inc="sV")
    tS2z = p.op("vector", lambda e: e.memset(S2[:, :], 0.0), inc="sV")
    tS = [tS, tS2z]

    def recip_fold(dstcol, srccol, wtok, tmp0, tmp1):
        t1 = p.op("vector", lambda e: e.tensor_scalar_add(
            out=stats[:, tmp0:tmp0 + KSL], in0=srccol[:, :], scalar1=1e-20),
            wait=wtok, inc="sV")
        t2 = p.op("vector", lambda e: e.reciprocal(
            out=stats[:, tmp1:tmp1 + KSL], in_=stats[:, tmp0:tmp0 + KSL]),
            wait=[t1], inc="sV")
        t3 = p.op("vector", lambda e: e.tensor_mul(
            out=dstcol[:, :], in0=dstcol[:, :], in1=stats[:, tmp1:tmp1 + KSL]),
            wait=[t2], inc="sV")
        return t3

    tbg = recip_fold(bgcol, gcol, [tg, tb], 0, 8)
    tbg2 = recip_fold(bg2col, g2col, [tg2, tb2], 16, 24)
    tbg = p.op("vector", lambda e: e.tensor_copy(out=bgcol_b[:, :], in_=bgcol[:, :]),
               wait=[tbg], inc="sV")
    tbg2 = p.op("vector", lambda e: e.tensor_copy(out=bg2col_b[:, :],
                                                  in_=bg2col[:, :]),
                wait=[tbg2], inc="sV")
    tgb = p.op("vector", lambda e: e.tensor_scalar_mul(
        out=g2bias[:, :], in0=g2col[:, :], scalar1=-128.0), wait=[tg2], inc="sV")

    def load_fold(wdst, wsrc, coltok, colap, start_wait=()):
        tlast = None
        for sl in range(KSL):
            stg = wstage[sl % 2]
            tl = p.dma("sync", stg[:, :], wsrc[128 * sl:128 * (sl + 1), :],
                       wait=p.wr(f"wstg{sl % 2}") + list(start_wait), q=f"qW{sl % 4}")
            if colap is None:
                tc = p.op("vector", lambda e, s=sl, g_=stg: e.tensor_copy(
                    out=wdst[:, D * s:D * (s + 1)], in_=g_[:, :]),
                    wait=[tl], inc="sV")
            else:
                tc = p.op("vector", lambda e, s=sl, g_=stg: e.tensor_scalar_mul(
                    out=wdst[:, D * s:D * (s + 1)], in0=g_[:, :],
                    scalar1=colap[:, s:s + 1]),
                    wait=[tl, coltok], inc="sV")
            p.rd(f"wstg{sl % 2}", tc)
            tlast = tc
        return tlast

    tWw = load_fold(mstage, Ww, None, None)
    tWxg = load_fold(wK, Wx, tg, gcol)
    bt0 = pool[3].ap().bitcast(BF16)
    tTx = None
    for jc in range(KSL):
        tt = None
        for s in range(KSL):
            tt = p.op("tensor", lambda e, s_=s, jc_=jc: e.transpose(
                bt0[:, 128 * s_:128 * (s_ + 1)],
                wK[:, D * s_ + 128 * jc_: D * s_ + 128 * (jc_ + 1)], idn[:, :]),
                wait=([tWxg, tU[5]] + p.wr("pl3")) if s == 0 else (),
                inc="sT" if s == KSL - 1 else None)
        tTx = p.op("vector", lambda e, jc_=jc: e.tensor_copy(
            out=wO[:, D * jc_:D * (jc_ + 1)], in_=bt0[:, :]),
            wait=[tt], inc="sV")
        p.use["pl3"] = [tTx]
    tM = None
    for sl in range(KSL):
        for hf in range(2):
            ps = pool[(sl * 2 + hf) % 2]
            tmm = mm_group(ps[:, :],
                           [wO[:, D * k + 128 * sl: D * k + 128 * (sl + 1)]
                            for k in range(KSL)],
                           [mstage[:, D * k + 512 * hf: D * k + 512 * (hf + 1)]
                            for k in range(KSL)],
                           wait=[tWw, tTx, p.use.get(f"pl{(sl * 2 + hf) % 2}")])
            tM = p.op("vector", lambda e, s=sl, h=hf, ps_=ps: e.tensor_copy(
                out=wM[:, D * s + 512 * h: D * s + 512 * (h + 1)], in_=ps_[:, :]),
                wait=[tmm], inc="sV")
            p.use[f"pl{(sl * 2 + hf) % 2}"] = [tM]

    tWk = load_fold(wK, Wk, tg, gcol, start_wait=[tM])
    tWv = load_fold(wV, Wv, tg, gcol)
    tWr = load_fold(wR, Wr, tg, gcol)
    tWo = load_fold(wO, Wo, None, None, start_wait=[tM])

    def row_mm(wten, wtok, ncols, dst32, dst32tok):
        tlast = None
        for ns in range(ncols // 512):
            ps = pool[2 + (ns % 2)]
            tmm = mm_group(ps[0:1, :],
                           [bgcol_b[:, k:k + 1] for k in range(KSL)],
                           [wten[:, D * k + 512 * ns: D * k + 512 * (ns + 1)]
                            for k in range(KSL)],
                           wait=[tbg, wtok, p.use.get(f"pl{2 + ns % 2}")])
            tv = p.op("vector", lambda e, n=ns, ps_=ps: e.tensor_copy(
                out=dst32[0:1, 512 * n:512 * (n + 1)], in_=ps_[0:1, :]),
                wait=[tmm, dst32tok], inc="sV")
            p.use[f"pl{2 + ns % 2}"] = [tv]
            tlast = tv
        return tlast

    t_rw = row_mm(wM, tM, D, rowtmp, None)
    t_rw2 = p.op("vector", lambda e: e.tensor_add(
        out=rowtmp[0:1, 1024:2048], in0=rowtmp[0:1, 0:1024], in1=bwrow[0:1, :]),
        wait=[t_rw, tbw], inc="sV")
    t_cw = p.op("vector", lambda e: e.tensor_copy(
        out=crow_w[0:1, :], in_=rowtmp[0:1, 1024:2048]), wait=[t_rw2], inc="sV")
    t_rr = row_mm(wR, tWr, D, rowtmp, t_cw)
    t_cr = p.op("vector", lambda e: e.tensor_copy(
        out=crow_r[0:1, :], in_=rowtmp[0:1, 0:1024]), wait=[t_rr], inc="sV")
    t_rk = row_mm(wK, tWk, D, rowtmp, t_cr)
    t_dk = p.dma("sync", ckrow_d, rowtmp[0:1, 0:1024], wait=[t_rk], q="qR")
    t_rv = row_mm(wV, tWv, D, rowtmp[0:1, 1024:2048], t_cw)
    t_dv = p.dma("sync", cvrow_d, rowtmp[0:1, 1024:2048], wait=[t_rv], q="qR")
    t_bk = p.dma("gpsimd", ckb[:, :], ckrow_d.broadcast_to([128, D]), wait=[t_dk], q="qRB")
    t_bv = p.dma("gpsimd", cvb[:, :], cvrow_d.broadcast_to([128, D]), wait=[t_dv], q="qRB")

    PRO_END = [("sV", p.cnt["sV"]), ("sT", p.cnt["sT"]), ("sS", p.cnt["sS"])]
    for nm in ("yfm0", "yfm1", "xmidb0", "xmidb1", "xn2b0", "xn2b1",
               "xn2f0", "xn2f1", "ktil0", "ktil1", "pevac0", "pevac1",
               "khtm0", "khtm1", "vtm0", "vtm1", "kfm0", "kfm1",
               "ktm0", "ktm1", "Ekh0", "Ekh1", "Eb0", "Eb1"):
        p.use.setdefault(nm, []).extend(PRO_END)
    tok_S_box = [tS]

    def emit_attn_chunk(c):
        tok_S = tok_S_box[0]
        pr = c % 2

        def gate():
            toks = [("sT", p.cnt["sT"]), ("sV", p.cnt["sV"]),
                    ("sS", p.cnt["sS"]), ("sP", p.cnt["sP"])]
            for nm in list(p.use.keys()):
                p.use[nm] = p.use[nm] + toks
        sc = 32 + 48 * pr
        xb = xbuf[c % 2]
        tx = p.dma("sync", xb[:, :], x[L * c:L * (c + 1), :],
                   wait=p.wr(f"xbuf{c % 2}"), q="qX")
        tsq = p.op("scalar", lambda e, xb_=xb: e.activation(
            out=sqd[:, :], in_=xb_[:, :], func=AF.Square,
            accum_out=stats[:, sc:sc + 1]),
            wait=[tx] + p.wr("sqd") + p.wr(f"st{sc}"), inc="sS")
        p.use["sqd"] = [tsq]
        tsm = p.op("vector", lambda e, xb_=xb: e.reduce_sum(
            out=stats[:, sc + 1:sc + 2], in_=xb_[:, :], axis=AXX),
            wait=[tx] + p.wr(f"st{sc + 1}"), inc="sV")
        t1 = p.op("vector", lambda e: e.tensor_scalar_mul(
            out=stats[:, sc + 2:sc + 3], in0=stats[:, sc + 1:sc + 2],
            scalar1=1.0 / D), wait=[tsm], inc="sV")
        t2 = p.op("vector", lambda e: e.tensor_mul(
            out=stats[:, sc + 3:sc + 4], in0=stats[:, sc + 2:sc + 3],
            in1=stats[:, sc + 2:sc + 3]), wait=[t1], inc="sV")
        t3 = p.op("vector", lambda e: e.tensor_scalar(
            out=stats[:, sc + 4:sc + 5], in0=stats[:, sc:sc + 1],
            scalar1=1.0 / D, scalar2=None, op0=ALU.mult), wait=[tsq], inc="sV")
        t4 = p.op("vector", lambda e: e.tensor_sub(
            out=stats[:, sc + 5:sc + 6], in0=stats[:, sc + 4:sc + 5],
            in1=stats[:, sc + 3:sc + 4]), wait=[t2, t3], inc="sV")
        t5 = p.op("vector", lambda e: e.tensor_scalar(
            out=stats[:, sc + 6:sc + 7], in0=stats[:, sc + 5:sc + 6],
            scalar1=(1.0 + EPS), scalar2=EPS * EPS, op0=ALU.mult, op1=ALU.add),
            wait=[t4], inc="sV")
        t6 = p.op("scalar", lambda e: e.activation(
            out=stats[:, sc + 7:sc + 8], in_=stats[:, sc + 6:sc + 7], func=AF.Ln),
            wait=[t5], inc="sS")
        t7 = p.op("scalar", lambda e: e.activation(
            out=stats[:, sc + 8:sc + 9], in_=stats[:, sc + 7:sc + 8], func=AF.Exp,
            scale=-0.5), wait=[t6], inc="sS")
        t8 = p.op("vector", lambda e: e.tensor_mul(
            out=stats[:, sc + 9:sc + 10], in0=stats[:, sc + 2:sc + 3],
            in1=stats[:, sc + 8:sc + 9]), wait=[t7], inc="sV")
        t9 = p.op("vector", lambda e: e.tensor_scalar_mul(
            out=stats[:, sc + 10:sc + 11], in0=stats[:, sc + 9:sc + 10],
            scalar1=-1.0), wait=[t8], inc="sV")
        p.rd(f"st{sc}", t3)
        p.rd(f"st{sc + 1}", t1)
        txg = p.op("gpsimd", lambda e, xb_=xb: e.tensor_mul(
            out=scr[0][:, :], in0=xb_[:, :],
            in1=stats[:, sc + 8:sc + 9].broadcast_to([128, D])),
            wait=[t9] + p.wr("scr0"), inc="sP")
        p.use["scr0"] = [txg]
        txn = p.op("gpsimd", lambda e: e.tensor_add(
            out=xn_tm[0][:, :], in0=scr[0][:, :],
            in1=stats[:, sc + 10:sc + 11].broadcast_to([128, D])),
            wait=[txg] + p.wr("xn_tm0"), inc="sP")
        p.rd("scr0", txn)
        bt = pool[0].ap().bitcast(BF16)
        ttr = None
        for sl in range(KSL):
            ttr = p.op("tensor", lambda e, s=sl: e.transpose(
                bt[:, 128 * s:128 * (s + 1)], xn_tm[0][:, 128 * s:128 * (s + 1)],
                idn[:, :]),
                wait=([txn, tU[5]] + p.wr("pl0")) if sl == 0 else (),
                inc="sT" if sl == KSL - 1 else None)
        txnf = p.op("vector", lambda e: e.tensor_copy(
            out=xn_fm[pr][:, :], in_=bt[:, :]),
            wait=[ttr] + p.wr(f"xn_fm{pr}"), inc="sV")
        p.use["pl0"] = [txnf]
        p.rd("xn_tm0", txnf)
        if sub < 2:
            gate()
            return

        def softplus_chain(wten, wtok, crow, crtok, spdst, spname, neg):
            tps = []
            for hf in range(2):
                ps = pool[1 + hf]
                tmm = mm_group(ps[:, :],
                               [xn_fm[pr][:, 128 * k:128 * (k + 1)]
                                for k in range(KSL)],
                               [wten[:, D * k + 512 * hf:D * k + 512 * (hf + 1)]
                                for k in range(KSL)],
                               wait=[txnf, wtok, crtok, tone] + p.wr(f"pl{1 + hf}"),
                               extra=None if v604 == 20 else
                               (ones[:, :], crow[0:1, 512 * hf:512 * (hf + 1)]))
                a = scr[hf]
                ta = p.op("scalar", lambda e, ps_=ps, a_=a: e.activation(
                    out=a_[:, 0:512], in_=ps_[:, :], func=AF.Abs),
                    wait=[tmm] + p.wr(f"scr{hf}"), inc="sS")
                te = p.op("scalar", lambda e, a_=a: e.activation(
                    out=a_[:, 512:1024], in_=a_[:, 0:512], func=AF.Exp, scale=-1.0),
                    wait=[ta], inc="sS")
                tad = p.op("vector", lambda e, a_=a: e.tensor_scalar_add(
                    out=a_[:, 0:512], in0=a_[:, 512:1024], scalar1=1.0),
                    wait=[te], inc="sV")
                tl = p.op("scalar", lambda e, a_=a: e.activation(
                    out=a_[:, 512:1024], in_=a_[:, 0:512], func=AF.Ln),
                    wait=[tad], inc="sS")
                if dbg and not neg:
                    tds_ = p.dma("sync",
                                 scrd_d[L * c:L * (c + 1),
                                        512 * hf:512 * (hf + 1)],
                                 a[:, 512:1024], wait=[tl], q="qNO")
                    p.rd(f"scr{hf}", tds_)
                op0 = ALU.min if neg else ALU.max
                op1 = ALU.subtract if neg else ALU.add
                tp = p.op("vector", lambda e, ps_=ps, a_=a, h=hf, o0=op0, o1=op1:
                          e.scalar_tensor_tensor(
                    out=spdst[:, 512 * h:512 * (h + 1)], in0=ps_[:, :], scalar=0.0,
                    in1=a_[:, 512:1024], op0=o0, op1=o1),
                    wait=[tl] + (p.wr(spname) if hf == 0 else []), inc="sV")
                p.use[f"scr{hf}"] = [tp]
                p.use[f"pl{1 + hf}"] = [tp]
                tps.append(tp)
            return tps

        tsp = softplus_chain(wM, tM, crow_w, t_cw, spb[pr], f"spb{pr}", False)
        tspr = softplus_chain(wR, tWr, crow_r, t_cr, sprb[pr], f"sprb{pr}", True)
        if dbg:
            tdd = p.dma("sync", spd_d[L * c:L * (c + 1), :], spb[pr][:, :],
                        wait=[tsp[1]], q="qMO")
            p.rd(f"spb{pr}", tdd)
            tdd = p.dma("sync", sprd_d[L * c:L * (c + 1), :], sprb[pr][:, :],
                        wait=[tspr[1]], q="qMO")
            p.rd(f"sprb{pr}", tdd)

        def proj(wten, wtok, fold, foldtok, dst, dstn):
            tvs = []
            for hf in range(2):
                ps = pool[1 + hf]
                tmm = mm_group(ps[:, :],
                               [xn_fm[pr][:, 128 * k:128 * (k + 1)]
                                for k in range(KSL)],
                               [wten[:, D * k + 512 * hf: D * k + 512 * (hf + 1)]
                                for k in range(KSL)],
                               wait=[txnf, wtok] + p.wr(f"pl{1 + hf}"))
                tv = p.op("vector", lambda e, h=hf, ps_=ps: e.tensor_add(
                    out=dst[:, 512 * h:512 * (h + 1)], in0=ps_[:, :],
                    in1=fold[:, 512 * h:512 * (h + 1)]),
                    wait=[tmm, foldtok] + (p.wr(dstn) if hf == 0 else []), inc="sV")
                p.use[f"pl{1 + hf}"] = [tv]
                tvs.append(tv)
            return tvs[-1]

        tkp = proj(wK, tWk, ckb, t_bk, ktm[pr], f"ktm{pr}")
        tvp = proj(wV, tWv, cvb, t_bv, vtm[pr], f"vtm{pr}")

        bt2 = pool[0].ap().bitcast(BF16)
        ttr2 = None
        for sl in range(KSL):
            ttr2 = p.op("tensor", lambda e, s=sl: e.transpose(
                bt2[:, 128 * s:128 * (s + 1)], ktm[pr][:, 128 * s:128 * (s + 1)],
                idn[:, :]),
                wait=([tkp] + p.wr("pl0")) if sl == 0 else (),
                inc="sT" if sl == KSL - 1 else None)
        tkf = p.op("vector", lambda e: e.tensor_copy(
            out=kfm[pr][:, :], in_=bt2[:, :]),
            wait=[ttr2] + p.wr(f"kfm{pr}"), inc="sV")
        p.use["pl0"] = [tkf]
        p.rd(f"xn_fm{pr}", ttr2)
        p.rd(f"ktm{pr}", ttr2)
        if sub < 3:
            gate()
            return

        teAB = []
        for sl in range(KSL):
            psA = pool[1 + (sl % 2)]
            psB = pool[3]
            tA = mm_group(psA[:, 0:NA],
                          [spb[pr][:, 128 * sl:128 * (sl + 1)]], [uA[:, :]],
                          wait=[tsp[1], tU[0]] + p.wr(f"pl{1 + (sl % 2)}"))
            tB1 = p.op("tensor", lambda e, s=sl, ps_=psB: e.matmul(
                ps_[:, 0:NB], spb[pr][:, 128 * s:128 * (s + 1)], uB[:, :],
                start=True, stop=False, skip_group_check=True),
                wait=[tsp[1], tU[1], tU[2]] + p.wr("pl3"))
            tB2 = p.op("tensor", lambda e, s=sl, ps_=psB: e.matmul(
                ps_[:, 0:NB], sprb[pr][:, 128 * s:128 * (s + 1)], uBr[:, :],
                start=False, stop=True, skip_group_check=True),
                wait=[tspr[1]], inc="sT")
            teA = p.op("scalar", lambda e, s=sl, ps_=psA: e.activation(
                out=eslice(Eb[pr], s, 0, NA), in_=ps_[:, 0:NA], func=AF.Exp),
                wait=[tA] + (p.wr(f"Eb{pr}") if sl == 0 else []), inc="sS")
            teB = p.op("scalar", lambda e, s=sl, ps_=psB: e.activation(
                out=eslice(Eb[pr], s, NA, ECOLS), in_=ps_[:, 0:NB], func=AF.Exp),
                wait=[tB2], inc="sS")
            p.use[f"pl{1 + (sl % 2)}"] = [teA]
            p.use["pl3"] = [teB]
            teAB.append(teB if sl == KSL - 1 else teA)
        tEdone = teAB[-1]
        if dbg:
            tdd = p.dma("sync", ebd_d[L * c:L * (c + 1), :], Eb[pr][:, :],
                        wait=[tEdone], q="qNO")
            p.rd(f"Eb{pr}", tdd)

        tkh1 = p.op("tensor", lambda e: e.matmul(
            pool[0].ap()[:, :], uKH[:, :], spb[pr][:, 0:512],
            start=True, stop=True, skip_group_check=True),
            wait=[tsp[1], tU[3]] + p.wr("pl0"), inc="sT")
        tek1 = p.op("scalar", lambda e: e.activation(
            out=Ekh[pr][:, 0:512], in_=pool[0].ap()[:, :], func=AF.Exp),
            wait=[tkh1] + p.wr(f"Ekh{pr}"), inc="sS")
        p.use["pl0"] = [tek1]
        tkh2 = p.op("tensor", lambda e: e.matmul(
            pool[0].ap()[:, :], uKH[:, :], spb[pr][:, 512:1024],
            start=True, stop=True, skip_group_check=True),
            wait=p.wr("pl0"), inc="sT")
        tek2 = p.op("scalar", lambda e: e.activation(
            out=Ekh[pr][:, 512:1024], in_=pool[0].ap()[:, :], func=AF.Exp),
            wait=[tkh2], inc="sS")
        p.use["pl0"] = [tek2]
        p.rd(f"spb{pr}", tkh2)
        p.rd(f"spb{pr}", tEdone)
        p.rd(f"sprb{pr}", tEdone)

        tkh = p.op("gpsimd", lambda e: e.tensor_mul(
            out=khtm[pr][:, :], in0=ktm[pr][:, :], in1=Ekh[pr][:, :]),
            wait=[tek1, tek2, tkp] + p.wr(f"khtm{pr}"), inc="sP")
        p.rd(f"ktm{pr}", tkh)
        p.rd(f"Ekh{pr}", tkh)
        if sub < 4:
            gate()
            return

        Scur = S0 if pr == 0 else S1
        Snxt = S1 if pr == 0 else S0
        Ebr = Eb[pr][:, :].rearrange("q (s n) -> q s n", s=KSL)
        t_i = None
        for slh in range(KSL):
            t_i = p.op("tensor", lambda e, s_=slh: e.matmul(
                outp[:, 128 * s_:128 * (s_ + 1)],
                S2[:, 128 * s_:128 * (s_ + 1)],
                Ebr[:, s_, NA + 128:NA + 256],
                start=(s_ in (0, 4)), stop=False, skip_group_check=True),
                wait=([tok_S, tEdone] + p.wr("outp")) if slh == 0 else (),
                inc="sT" if slh == KSL - 1 else None)
        p.rd("S2buf", t_i)
        p.rd(f"Eb{pr}", t_i)
        tok_pv = t_i
        if sub < 4.1:
            gate()
            return

        for i in range(NSUB):
            kb = ktil[i % 2]
            pe = pevac[i % 2]
            sr = KT_SZ[i]
            tkt_i = p.op("gpsimd", lambda e, i_=i, kb_=kb: e.tensor_mul(
                out=kb_[:, :].rearrange("q (s n) -> q s n", s=KSL)[:, :, 0:KT_SZ[i_]],
                in0=kfm[pr][:, :].rearrange("q (s n) -> q s n", s=KSL)
                    [:, :, 0:KT_SZ[i_]],
                in1=Eb[pr][:, :].rearrange("q (s n) -> q s n", s=KSL)
                    [:, :, KT_OFF[i_]:KT_OFF[i_] + KT_SZ[i_]]),
                wait=[tkf, tEdone] + p.wr(f"ktil{i % 2}"), inc="sP")
            p.rd(f"kfm{pr}", tkt_i)
            p.rd(f"Eb{pr}", tkt_i)
            if sub < 4.2:
                if i == NSUB - 1:
                    gate()
                    return
                continue
            tP = None
            heads = list(range(0, NH, 2)) + list(range(1, NH, 2))
            for hi, h in enumerate(heads):
                pb, slh = (h % 2) * 64, h // 2
                tgt = pool[3] if h % 2 == 0 else pool[4]
                tP = p.op("tensor", lambda e, h_=h, pb_=pb, s_=slh, i_=i, kb_=kb,
                          sr_=sr, tg_=tgt: e.matmul(
                    tg_.ap()[0:sr_, 32 * (h_ // 2):32 * (h_ // 2) + 32],
                    kb_[pb_:pb_ + 64, 128 * s_:128 * s_ + sr_],
                    Ebr[pb_:pb_ + 64, s_, NA + 32 * i_:NA + 32 * (i_ + 1)],
                    start=True, stop=True, skip_group_check=True),
                    wait=([tkt_i] + p.wr("pl3") + p.wr("pl4")) if hi == 0 else (),
                    inc="sT" if hi == NH - 1 else None)
            p.rd(f"ktil{i % 2}", tP)
            p.rd(f"Eb{pr}", tP)
            if sub < 4.3:
                if i == NSUB - 1:
                    gate()
                    return
                continue
            tpe0 = p.op("vector", lambda e, i_=i, pe_=pe, sr_=sr: e.tensor_tensor(
                out=pe_[0:sr_, :].rearrange("q (h two n) -> q h two n",
                                            two=2, n=32)[:, :, 0, :],
                in0=pool[3].ap()[0:sr_, 0:256].rearrange("q (h n) -> q h n", n=32),
                in1=msk[0:sr_, 32 * i_:32 * (i_ + 1)].unsqueeze(1).broadcast_to(
                    [sr_, NH // 2, 32]), op=ALU.mult),
                wait=[tP, tU[4]] + p.wr(f"pevac{i % 2}"), inc="sV")
            tpe1 = p.op("vector", lambda e, i_=i, pe_=pe, sr_=sr: e.tensor_tensor(
                out=pe_[0:sr_, :].rearrange("q (h two n) -> q h two n",
                                            two=2, n=32)[:, :, 1, :],
                in0=pool[4].ap()[0:sr_, 0:256].rearrange("q (h n) -> q h n", n=32),
                in1=msk[0:sr_, 32 * i_:32 * (i_ + 1)].unsqueeze(1).broadcast_to(
                    [sr_, NH // 2, 32]), op=ALU.mult),
                wait=[tP], inc="sV")
            p.use["pl3"] = [tpe0]
            p.use["pl4"] = [tpe1]
            tpe = [tpe0, tpe1]
            if sub < 4.4:
                if i == NSUB - 1:
                    gate()
                    return
                continue
            tpv = None
            for h in range(NH):
                pb, slh = (h % 2) * 64, h // 2
                tpv = p.op("tensor", lambda e, h_=h, pb_=pb, s_=slh, i_=i, pe_=pe,
                           sr_=sr: e.matmul(
                    outp[pb_:pb_ + 64,
                         128 * s_ + 32 * i_:128 * s_ + 32 * (i_ + 1)],
                    vtm[pr][0:sr_, 64 * h_:64 * (h_ + 1)],
                    pe_[0:sr_, 32 * h_:32 * (h_ + 1)],
                    start=False, stop=(i_ == NSUB - 1 and h_ == NH - 1),
                    skip_group_check=True),
                    wait=([tpe, tvp, tok_pv] if h == 0 else ()),
                    inc="sT" if h == NH - 1 else None)
            tok_pv = tpv
            p.rd(f"pevac{i % 2}", tpv)
            p.rd(f"vtm{pr}", tpv)

        tyev = p.op("vector", lambda e: e.tensor_copy(
            out=yfm[pr][:, :], in_=outp[:, :]),
            wait=[tok_pv] + p.wr(f"yfm{pr}"), inc="sV")
        p.use["outp"] = [tyev]
        if dbg:
            tdd = p.dma("sync", yfmd_d[L * c:L * (c + 1), :], yfm[pr][:, :],
                        wait=[tyev], q="qNO")
            p.rd(f"yfm{pr}", tdd)
        if sub < 5:
            gate()
            return

        tSa = None
        for h in range(NH):
            tSa = p.op("tensor", lambda e, h_=h: e.matmul(
                sadd[(h_ % 2) * 64:(h_ % 2) * 64 + 64,
                     64 * (h_ // 2):64 * (h_ // 2) + 64],
                khtm[pr][:, 64 * h_:64 * (h_ + 1)],
                vtm[pr][:, 64 * h_:64 * (h_ + 1)],
                start=True, stop=True, skip_group_check=True),
                wait=([tkh, tvp] + p.wr("sadd")) if h == 0 else (),
                inc="sT" if h == NH - 1 else None)
        p.rd(f"khtm{pr}", tSa)
        p.rd(f"vtm{pr}", tSa)
        tds = None
        for sl in range(KSL):
            tds = p.op("gpsimd", lambda e, s=sl, S_=Scur: e.tensor_mul(
                out=Sd[:, 64 * s:64 * (s + 1)], in0=S_[:, 64 * s:64 * (s + 1)],
                in1=eslice(Eb[pr], s, DELTA_C,
                           DELTA_C + 1).broadcast_to([128, 64])),
                wait=([tEdone, tok_S] + p.wr("Sd")) if sl == 0 else (),
                inc="sP" if sl == KSL - 1 else None)
        p.rd(f"Sbuf{0 if pr == 0 else 1}", tds)
        p.rd(f"Eb{pr}", tds)
        tS2 = p.op("vector", lambda e, S_=Snxt: e.tensor_add(
            out=S_[:, :], in0=Sd[:, :], in1=sadd[:, :]),
            wait=[tSa, tds] + p.wr(f"Sbuf{1 if pr == 0 else 0}"), inc="sV")
        p.use["Sd"] = [tS2]
        p.use["sadd"] = [tS2]
        tbd0 = p.op("vector", lambda e, S_=Snxt: e.tensor_copy(
            out=S2[0:64, :].rearrange("q (s n) -> q s n", n=128)[:, :, 0:64],
            in_=S_[0:64, :].rearrange("q (s n) -> q s n", n=64)),
            wait=[tS2] + p.wr("S2buf"), inc="sV")
        tbd1 = p.op("vector", lambda e, S_=Snxt: e.tensor_copy(
            out=S2[64:128, :].rearrange("q (s n) -> q s n", n=128)[:, :, 64:128],
            in_=S_[64:128, :].rearrange("q (s n) -> q s n", n=64)),
            wait=[tS2], inc="sV")
        tok_S = [tbd0, tbd1]
        if sub < 6:
            tok_S_box[0] = tok_S
            gate()
            return

        if sub < 6:
            tok_S_box[0] = tok_S
            gate()
            return
        att = {}
        for hf in range(2):
            ps = pool[1 + hf]
            tmm = mm_group(ps[:, :],
                           [yfm[pr][:, 128 * k:128 * (k + 1)] for k in range(KSL)],
                           [wO[:, D * k + 512 * hf: D * k + 512 * (hf + 1)]
                            for k in range(KSL)],
                           wait=[tyev, tWo] + p.wr(f"pl{1 + hf}"))
            tev = p.op("scalar", lambda e, ps_=ps, h=hf: e.activation(
                out=scr[h][:, 0:512], in_=ps_[:, :], func=AF.Identity,
                accum_out=stats[:, sc + 13 + h:sc + 14 + h]),
                wait=[tmm] + p.wr(f"scr{hf}") + p.wr(f"st{sc + 13 + hf}"),
                inc="sS")
            p.use[f"pl{1 + hf}"] = [tev]
            p.use[f"scr{hf}"] = [tev]
            tsqr = p.op("scalar", lambda e, h=hf: e.activation(
                out=sqd[:, 512 * h:512 * (h + 1)], in_=scr[h][:, 0:512],
                func=AF.Square, accum_out=stats[:, sc + 11 + h:sc + 12 + h]),
                wait=[tev] + p.wr("sqd") + p.wr(f"st{sc + 11 + hf}"), inc="sS")
            p.use["sqd"] = p.use.get("sqd", []) + [tsqr]
            att[hf] = (tmm, tsqr, tev, ps)
            p.rd(f"yfm{pr}", tmm)
            if sub < 6.15 and hf == 1:
                tok_S_box[0] = tok_S
                gate()
                return
        if sub < 6.25:
            tok_S_box[0] = tok_S
            gate()
            return
        u1 = p.op("vector", lambda e: e.tensor_add(
            out=stats[:, sc + 15:sc + 16], in0=stats[:, sc + 11:sc + 12],
            in1=stats[:, sc + 12:sc + 13]), wait=[att[0][1], att[1][1]], inc="sV")
        u2 = p.op("vector", lambda e: e.tensor_add(
            out=stats[:, sc + 16:sc + 17], in0=stats[:, sc + 13:sc + 14],
            in1=stats[:, sc + 14:sc + 15]), wait=[att[0][2], att[1][2]], inc="sV")
        p.rd(f"st{sc + 11}", u1)
        p.rd(f"st{sc + 12}", u1)
        p.rd(f"st{sc + 13}", u2)
        p.rd(f"st{sc + 14}", u2)
        u3 = p.op("vector", lambda e: e.tensor_scalar_mul(
            out=stats[:, sc + 17:sc + 18], in0=stats[:, sc + 16:sc + 17],
            scalar1=1.0 / D), wait=[u2], inc="sV")
        u4 = p.op("vector", lambda e: e.tensor_mul(
            out=stats[:, sc + 18:sc + 19], in0=stats[:, sc + 17:sc + 18],
            in1=stats[:, sc + 17:sc + 18]), wait=[u3], inc="sV")
        u5 = p.op("vector", lambda e: e.tensor_scalar(
            out=stats[:, sc + 19:sc + 20], in0=stats[:, sc + 15:sc + 16],
            scalar1=1.0 / D, scalar2=EPS, op0=ALU.mult, op1=ALU.add),
            wait=[u1], inc="sV")
        u6 = p.op("vector", lambda e: e.tensor_sub(
            out=stats[:, sc + 20:sc + 21], in0=stats[:, sc + 19:sc + 20],
            in1=stats[:, sc + 18:sc + 19]), wait=[u4, u5], inc="sV")
        u7 = p.op("scalar", lambda e: e.activation(
            out=stats[:, sc + 21:sc + 22], in_=stats[:, sc + 20:sc + 21],
            func=AF.Ln), wait=[u6], inc="sS")
        u8 = p.op("scalar", lambda e: e.activation(
            out=stats[:, sc + 22:sc + 23], in_=stats[:, sc + 21:sc + 22],
            func=AF.Exp, scale=-0.5), wait=[u7], inc="sS")
        u9 = p.op("vector", lambda e: e.tensor_mul(
            out=stats[:, sc + 23:sc + 24], in0=stats[:, sc + 17:sc + 18],
            in1=stats[:, sc + 22:sc + 23]), wait=[u8], inc="sV")
        u10 = p.op("vector", lambda e: e.tensor_scalar_mul(
            out=stats[:, sc + 24:sc + 25], in0=stats[:, sc + 23:sc + 24],
            scalar1=-1.0), wait=[u9], inc="sV")
        if sub < 6.35:
            tok_S_box[0] = tok_S
            gate()
            return
        txm = None
        for hf in range(2):
            ta1 = p.op("gpsimd", lambda e, h=hf: e.tensor_mul(
                out=scr[h][:, 512:1024], in0=scr[h][:, 0:512],
                in1=stats[:, sc + 22:sc + 23].broadcast_to([128, 512])),
                wait=[u10], inc="sP")
            ta2 = p.op("gpsimd", lambda e, h=hf: e.tensor_add(
                out=scr[h][:, 512:1024], in0=scr[h][:, 512:1024],
                in1=stats[:, sc + 24:sc + 25].broadcast_to([128, 512])),
                wait=[ta1], inc="sP")
            txm = p.op("vector", lambda e, xb_=xb, h=hf: e.tensor_add(
                out=xmidb[pr][:, 512 * h:512 * (h + 1)],
                in0=scr[h][:, 512:1024],
                in1=xb_[:, 512 * h:512 * (h + 1)]),
                wait=[ta2] + (p.wr(f"xmidb{pr}") if hf == 0 else []), inc="sV")
            p.rd(f"scr{hf}", txm)
            p.use[f"scr{hf}"] = p.use.get(f"scr{hf}", []) + [ta2]
            p.rd(f"xbuf{c % 2}", txm)
        tdx = p.dma("sync", xmid_d[L * c:L * (c + 1), :], xmidb[pr][:, :],
                    wait=[txm], q="qMO")
        p.rd(f"xmidb{pr}", tdx)
        if sub < 7:
            tok_S_box[0] = tok_S
            gate()
            return
        tsq3 = p.op("scalar", lambda e: e.activation(
            out=sqd[:, :], in_=xmidb[pr][:, :], func=AF.Square,
            accum_out=stats[:, sc + 25:sc + 26]),
            wait=[txm] + p.wr("sqd"), inc="sS")
        p.use["sqd"] = [tsq3]
        tsm3 = p.op("vector", lambda e: e.reduce_sum(
            out=stats[:, sc + 26:sc + 27], in_=xmidb[pr][:, :], axis=AXX),
            wait=[txm], inc="sV")
        v1 = p.op("vector", lambda e, c_=c: e.tensor_scalar_mul(
            out=m3c[:, 3 * c_:3 * c_ + 1], in0=stats[:, sc + 26:sc + 27],
            scalar1=1.0 / D), wait=[tsm3], inc="sV")
        v2 = p.op("vector", lambda e, c_=c: e.tensor_mul(
            out=stats[:, sc + 27:sc + 28], in0=m3c[:, 3 * c_:3 * c_ + 1],
            in1=m3c[:, 3 * c_:3 * c_ + 1]), wait=[v1], inc="sV")
        v3 = p.op("vector", lambda e: e.tensor_scalar_mul(
            out=stats[:, sc + 28:sc + 29], in0=stats[:, sc + 25:sc + 26],
            scalar1=1.0 / D), wait=[tsq3], inc="sV")
        v4 = p.op("vector", lambda e: e.tensor_sub(
            out=stats[:, sc + 29:sc + 30], in0=stats[:, sc + 28:sc + 29],
            in1=stats[:, sc + 27:sc + 28]), wait=[v2, v3], inc="sV")
        v5 = p.op("vector", lambda e: e.tensor_scalar_add(
            out=stats[:, sc + 30:sc + 31], in0=stats[:, sc + 29:sc + 30],
            scalar1=EPS), wait=[v4], inc="sV")
        v6 = p.op("scalar", lambda e: e.activation(
            out=stats[:, sc + 31:sc + 32], in_=stats[:, sc + 30:sc + 31],
            func=AF.Ln), wait=[v5], inc="sS")
        v7 = p.op("scalar", lambda e, c_=c: e.activation(
            out=m3c[:, 3 * c_ + 1:3 * c_ + 2], in_=stats[:, sc + 31:sc + 32],
            func=AF.Exp, scale=-0.5), wait=[v6], inc="sS")
        v8 = p.op("vector", lambda e, c_=c: e.tensor_mul(
            out=stats[:, sc + 32:sc + 33], in0=m3c[:, 3 * c_:3 * c_ + 1],
            in1=m3c[:, 3 * c_ + 1:3 * c_ + 2]), wait=[v7], inc="sV")
        v9 = p.op("vector", lambda e, c_=c: e.tensor_scalar_mul(
            out=m3c[:, 3 * c_ + 2:3 * c_ + 3], in0=stats[:, sc + 32:sc + 33],
            scalar1=-1.0), wait=[v8], inc="sV")
        va = p.op("vector", lambda e: e.tensor_scalar(
            out=stats[:, sc + 33:sc + 34], in0=stats[:, sc + 29:sc + 30],
            scalar1=(1.0 + EPS), scalar2=EPS * EPS, op0=ALU.mult, op1=ALU.add),
            wait=[v4], inc="sV")
        vb = p.op("scalar", lambda e: e.activation(
            out=stats[:, sc + 34:sc + 35], in_=stats[:, sc + 33:sc + 34],
            func=AF.Ln), wait=[va], inc="sS")
        vc = p.op("scalar", lambda e: e.activation(
            out=stats[:, sc + 35:sc + 36], in_=stats[:, sc + 34:sc + 35],
            func=AF.Exp, scale=-0.5), wait=[vb], inc="sS")
        vd = p.op("vector", lambda e, c_=c: e.tensor_mul(
            out=stats[:, sc + 36:sc + 37], in0=m3c[:, 3 * c_:3 * c_ + 1],
            in1=stats[:, sc + 35:sc + 36]), wait=[vc, v1], inc="sV")
        ve_ = p.op("vector", lambda e: e.tensor_scalar_mul(
            out=stats[:, sc + 37:sc + 38], in0=stats[:, sc + 36:sc + 37],
            scalar1=-1.0), wait=[vd], inc="sV")
        tx2g = p.op("gpsimd", lambda e: e.tensor_mul(
            out=scr[1][:, :], in0=xmidb[pr][:, :],
            in1=stats[:, sc + 35:sc + 36].broadcast_to([128, D])),
            wait=[ve_] + p.wr("scr1"), inc="sP")
        p.use["scr1"] = [tx2g]
        txn2 = p.op("gpsimd", lambda e: e.tensor_add(
            out=xn2b[pr][:, :], in0=scr[1][:, :],
            in1=stats[:, sc + 37:sc + 38].broadcast_to([128, D])),
            wait=[tx2g] + p.wr(f"xn2b{pr}"), inc="sP")
        p.rd("scr1", txn2)
        p.rd(f"xmidb{pr}", txn2)
        if sub < 8:
            tok_S_box[0] = tok_S
            gate()
            return
        bt3 = pool[0].ap().bitcast(BF16)
        ttr3 = None
        for sl in range(KSL):
            ttr3 = p.op("tensor", lambda e, s=sl: e.transpose(
                bt3[:, 128 * s:128 * (s + 1)], xn2b[pr][:, 128 * s:128 * (s + 1)],
                idn[:, :]),
                wait=([txn2] + p.wr("pl0")) if sl == 0 else (),
                inc="sT" if sl == KSL - 1 else None)
        p.rd(f"xn2b{pr}", ttr3)
        txn2f = p.op("vector", lambda e: e.tensor_copy(
            out=xn2f[pr][:, :], in_=bt3[:, :]),
            wait=[ttr3] + p.wr(f"xn2f{pr}"), inc="sV")
        p.use["pl0"] = [txn2f]
        tdn = p.dma("sync", xn2_d[L * c:L * (c + 1), :], xn2f[pr][:, :],
                    wait=[txn2f], q="qNO")
        p.rd(f"xn2f{pr}", tdn)
        if dbg and c == 0:
            p.dma("sync", scd_d[:, 8:16], g2col[:, :], wait=[tdn], q="qNO")
        if dbg:
            p.dma("sync", arnc_d[128 * c:128 * (c + 1), :],
                  arena[:, :].bitcast(F32)[:, 0:512],
                  wait=[tdn], q="qNO")
        if dbg and c == NCH - 1:
            p.dma("sync", scd_d[:, 16:24], g2col[:, :], wait=[tdn], q="qNO")
            p.dma("sync", arn1_d[:, :], arena[:, :].bitcast(F32)[:, 0:512],
                  wait=[tdn], q="qNO")
        tok_S_box[0] = tok_S

    if stage >= 2:
        for c in range(NCH):
            emit_attn_chunk(c)

    attn_done = [("sT", p.cnt["sT"]), ("sV", p.cnt["sV"]), ("sS", p.cnt["sS"]),
                 ("sP", p.cnt["sP"]), ("qMO", p.cnt.get("qMO", 0)),
                 ("qNO", p.cnt.get("qNO", 0)), ("qX", p.cnt.get("qX", 0))]
    attn_done = [t for t in attn_done if t[1] > 0]

    f32v = fw1[:, :].bitcast(F32)
    ts1 = p.dma("sync", f32v[0:8, 0:4096], s1, wait=attn_done, q="qC0")
    tc1 = p.op("vector", lambda e: e.tensor_copy(
        out=fw2[0:8, 0:4096], in_=f32v[0:8, 0:4096]),
        wait=[ts1] + attn_done, inc="sV")
    td1 = p.dma("sync", s1b_d, fw2[0:8, 0:4096], wait=[tc1], q="qR")
    ts2 = p.dma("sync", f32v[0:32, 4096:5120], s2, wait=attn_done, q="qC1")
    tc2 = p.op("vector", lambda e: e.tensor_copy(
        out=fw2[0:32, 4096:5120], in_=f32v[0:32, 4096:5120]),
        wait=[ts2] + attn_done, inc="sV")
    td2 = p.dma("sync", s2b_d, fw2[0:32, 4096:5120], wait=[tc2], q="qR")
    p.use["fw1st"] = [tc1, tc2]
    p.use["fw2st"] = [td1, td2]

    tb1r = p.dma("gpsimd", b1row[0:1, :], b1[None, :], wait=attn_done, q="qRB")
    tb2r = p.dma("sync", rowtmp[0:1, 0:1024], b2f[None, :], wait=attn_done, q="qC3")
    tcb2 = p.op("vector", lambda e: e.tensor_scalar_mul(
        out=cb2row[0:1, :], in0=rowtmp[0:1, 0:1024], scalar1=0.5),
        wait=[tb2r] + attn_done, inc="sV")
    p.use["rowtmp"] = [tcb2]

    def dequant_w1(colrange, extra_wait):
        tws = []
        for sl in range(KSL):
            b_ = sl % 2
            tq = p.dma("sync", qst[b_][:, 0:2048],
                       q1[128 * sl:128 * (sl + 1), colrange[0]:colrange[1]],
                       wait=p.wr(f"qst{b_}") + list(extra_wait), q=f"qQ{b_}")
            tsb = p.dma("sync", sbc[b_][:, 0:2048],
                        s1b_d[sl:sl + 1, colrange[0]:colrange[1]].broadcast_to(
                            [128, 2048]),
                        wait=[td1] + p.wr(f"sbc{b_}") + list(extra_wait), q=f"qSB{b_}")
            tw = None
            for half in range(2):
                sc_ = scr[half]
                tcs = p.op("gpsimd", lambda e, s=sl, bb=b_, h=half, sc2=sc_:
                           e.tensor_mul(
                    out=sc2[:, 0:1024],
                    in0=sbc[bb][:, 1024 * h:1024 * (h + 1)],
                    in1=g2col[:, s:s + 1].broadcast_to([128, 1024])),
                    wait=[tsb, tgb] + p.wr(f"scr{half}"), inc="sP")
                p.use[f"scr{half}"] = [tcs]
                tw = p.op("vector", lambda e, s=sl, bb=b_, h=half, sc2=sc_:
                          e.scalar_tensor_tensor(
                    out=fw1[:, 2048 * s + 1024 * h:2048 * s + 1024 * (h + 1)],
                    in0=qst[bb][:, 1024 * h:1024 * (h + 1)],
                    scalar=-128.0, op0=ALU.add,
                    in1=sc2[:, 0:1024], op1=ALU.mult),
                    wait=[tcs, tq] + (p.wr("fw1st") if half == 0 else []) +
                         ([] if sl > 1 else list(extra_wait)), inc="sV")
                p.rd(f"qst{b_}", tw)
                p.rd(f"sbc{b_}", tw)
                p.rd(f"scr{half}", tw)
                if dbg and colrange[0] == 0 and sl == KSL - 1 and half == 1:
                    td_ = p.dma("sync", qd_d[:, :], qst[b_][:, 0:2048],
                                wait=[tq], q="qNO")
                    p.rd(f"qst{b_}", td_)
            tws.append(tw)
        return tws

    def dequant_w2(rowrange, extra_wait):
        tws = []
        for sl in range(16):
            b_ = sl % 2
            gsl = rowrange[0] // 128 + sl
            tq = p.dma("sync", qst[b_][:, 0:1024],
                       q2[rowrange[0] + 128 * sl: rowrange[0] + 128 * (sl + 1), :],
                       wait=p.wr(f"qst{b_}") + list(extra_wait), q=f"qQ{b_}")
            tsb = p.dma("sync", sbc[b_][:, 0:1024],
                        s2b_d[gsl:gsl + 1, :].broadcast_to([128, 1024]),
                        wait=[td2] + p.wr(f"sbc{b_}") + list(extra_wait), q=f"qSB{b_}")
            sc_ = scr[b_]
            tts = p.op("vector", lambda e, bb=b_, sc2=sc_: e.tensor_scalar(
                out=sc2[:, 0:1024], in0=qst[bb][:, 0:1024],
                scalar1=-128.0, scalar2=0.5, op0=ALU.add, op1=ALU.mult),
                wait=[tq] + p.wr(f"scr{b_}") + list(extra_wait), inc="sV")
            p.rd(f"qst{b_}", tts)
            p.use[f"scr{b_}"] = [tts]
            tw = p.op("vector", lambda e, s=sl, bb=b_, sc2=sc_: e.tensor_mul(
                out=fw2[:, D * s:D * (s + 1)], in0=sc2[:, 0:1024],
                in1=sbc[bb][:, 0:1024]),
                wait=[tts, tsb] + p.wr("fw2st") +
                     ([] if sl > 1 else list(extra_wait)), inc="sV")
            p.rd(f"sbc{b_}", tw)
            p.rd(f"scr{b_}", tw)
            tws.append(tw)
        return tws

    def beta_row(dst_off, w1toks, extra_wait):
        tvs = []
        for ns in range(4):
            ps = pool[2 + (ns % 2)]
            tmm = mm_group(ps[0:1, :],
                           [bg2col_b[:, k:k + 1] for k in range(KSL)],
                           [fw1[:, 2048 * k + 512 * ns: 2048 * k + 512 * (ns + 1)]
                            for k in range(KSL)],
                           wait=[tbg2, w1toks[-1]] + p.wr(f"pl{2 + ns % 2}") +
                                list(extra_wait))
            tv = p.op("vector", lambda e, n=ns, ps_=ps, o=dst_off:
                      e.scalar_tensor_tensor(
                out=rowtmp[0:1, 512 * n:512 * (n + 1)], in0=ps_[0:1, :], scalar=1.0,
                in1=b1row[0:1, o + 512 * n:o + 512 * (n + 1)],
                op0=ALU.mult, op1=ALU.add),
                wait=[tmm, tb1r] + p.wr("rowtmp") + list(extra_wait), inc="sV")
            p.use[f"pl{2 + ns % 2}"] = [tv]
            tvs.append(tv)
        td = p.dma("sync", cb1_d[0:1, dst_off:dst_off + 2048],
                   rowtmp[0:1, 0:2048], wait=tvs, q="qR")
        p.rd("rowtmp", td)
        tcvt = p.dma("gpsimd", cb1r[0:1, dst_off:dst_off + 2048],
                     cb1_d[0:1, dst_off:dst_off + 2048], wait=[td], q="qRB")
        return tcvt

    if stage < 3:
        p.op("gpsimd", lambda e: e.engine_nop(),
             wait=attn_done + [("qW0", p.cnt.get("qW0", 0)),
                               ("qC0", p.cnt.get("qC0", 0))])
        with nc.Block() as block:
            p.finish(block)
        return nc
    tw1a = dequant_w1((0, 2048), attn_done)
    tw2a = dequant_w2((0, 2048), attn_done)
    tcb1a = beta_row(0, tw1a, [])
    if dbg:
        for j_ in range(4):
            td_ = p.dma("sync", f1d_d[:, 4096 * j_:4096 * (j_ + 1)],
                        fw1[:, 4096 * j_:4096 * (j_ + 1)],
                        wait=list(tw1a), q="qNO")
            p.rd("fw1st", td_)
        td_ = p.dma("sync", cbr_d[0:1, :], cb1r[0:1, :], wait=[tcb1a],
                    q="qNO")
        p.rd("cb1rb", td_)
        td_ = p.dma("sync", cbd_d[:, 0:8], g2col[:, :], wait=[tcb1a],
                    q="qNO")
        td_ = p.dma("sync", cbd_d[:, 8:16], g2bias[:, :], wait=[td_],
                    q="qNO")
        td_ = p.dma("sync", cbd_d[:, 16:24], gcol[:, :], wait=[td_],
                    q="qNO")
        td_ = p.dma("sync", zbd_d[0:1, :], zb[0:1, 0:512], wait=[tzb],
                    q="qNO")

    def ffn_pass(passb, w1toks, w2toks, cbtok):
        def emit_group(g0, gs):
            gi = (g0 // 4) % 2
            GT = 128 * gs
            txns = []
            for j in range(gs):
                c = g0 + j
                tx = p.dma("sync",
                           fxn2[gi][:, :].rearrange("q (s t) -> q s t", s=KSL)
                               [:, :, 128 * j:128 * (j + 1)],
                           xn2_d[L * c:L * (c + 1), :].rearrange(
                               "q (s t) -> q s t", s=KSL),
                           wait=(p.wr(f"fxn2{gi}") if j == 0 else []) +
                                [w1toks[-1], w2toks[-1]], q="qNI")
                txns.append(tx)
            tsl_last = None
            for nch in range(16):
                ps = pool[nch % 4]
                tmm = mm_group(ps[:, 0:GT],
                               [fw1[:, 2048 * k + 128 * nch:
                                    2048 * k + 128 * (nch + 1)]
                                for k in range(KSL)],
                               [fxn2[gi][:, 512 * k:512 * k + GT]
                                for k in range(KSL)],
                               wait=txns + [cbtok, w1toks[-1], tzb] +
                                    p.wr(f"pl{nch % 4}") +
                                    (p.wr("fh") if nch == 0 else []),
                               extra=(cb1r[0:1, 2048 * passb + 128 * nch:
                                           2048 * passb + 128 * (nch + 1)],
                                      zb[0:1, 0:GT]))
                tsl_last = p.op("scalar", lambda e, n=nch, ps_=ps, G_=GT:
                                e.activation(
                    out=fh[:, 512 * n:512 * n + G_], in_=ps_[:, 0:G_],
                    func=AF.Silu),
                    wait=[tmm], inc="sS")
                p.use[f"pl{nch % 4}"] = [tsl_last]
            p.rd(f"fxn2{gi}", tsl_last)

            for j in range(gs):
                c = g0 + j
                b_ = c % 2
                q_ = c % 4
                tz = []
                for hf in range(2):
                    extra = (ones[:, :], cb2row[0:1, 512 * hf:512 * (hf + 1)]) \
                        if passb else None
                    tmm = mm_group(outp[:, 512 * hf:512 * (hf + 1)],
                                   [fh[:, 512 * k + 128 * j:512 * k + 128 * (j + 1)]
                                    for k in range(16)],
                                   [fw2[:, D * k + 512 * hf: D * k + 512 * (hf + 1)]
                                    for k in range(16)],
                                   wait=[tsl_last, tcb2, w2toks[-1], tone] +
                                        p.wr(f"fop{hf}"),
                                   extra=extra)
                    tz.append(tmm)
                p.rd("fh", tz[-1])
                if not passb:
                    tev = p.op("vector", lambda e, bb=b_: e.tensor_copy(
                        out=ft[bb][:, :], in_=outp[:, :]),
                        wait=tz + p.wr(f"ft{b_}"), inc="sV")
                    p.use["fop0"] = [tev]
                    p.use["fop1"] = [tev]
                    tdz = p.dma("sync", z2a_d[L * c:L * (c + 1), :], ft[b_][:, :],
                                wait=[tev], q="qZO")
                    p.rd(f"ft{b_}", tdz)
                else:
                    tza = p.dma("sync", fz2a[b_][:, :], z2a_d[L * c:L * (c + 1), :],
                                wait=p.wr(f"fz2a{b_}"), q="qZI")
                    txd = p.dma("sync", fxmid[q_][:, :],
                                xmid_d[L * c:L * (c + 1), :],
                                wait=p.wr(f"fxmid{q_}"), q="qMI")
                    tt1 = p.op("vector", lambda e, bb=b_, q2_=q_: e.tensor_add(
                        out=ft[q2_][:, :], in0=outp[:, :], in1=fz2a[bb][:, :]),
                        wait=tz + [tza] + p.wr(f"ft{q_}"), inc="sV")
                    p.use["fop0"] = [tt1]
                    p.use["fop1"] = [tt1]
                    p.rd(f"fz2a{b_}", tt1)
                    tp1 = p.op("gpsimd", lambda e, q2_=q_, c_=c: e.tensor_mul(
                        out=fo[c_ % 2][:, :], in0=fxmid[q2_][:, :],
                        in1=m3c[:, 3 * c_ + 1:3 * c_ + 2]
                            .broadcast_to([128, D])),
                        wait=[txd] + p.wr(f"fo{c % 2}"), inc="sP")
                    p.rd(f"fxmid{q_}", tp1)
                    tp2 = p.op("gpsimd", lambda e, c_=c: e.tensor_add(
                        out=fo[c_ % 2][:, :], in0=fo[c_ % 2][:, :],
                        in1=m3c[:, 3 * c_ + 2:3 * c_ + 3]
                            .broadcast_to([128, D])),
                        wait=[tp1], inc="sP")
                    tt2 = p.op("vector", lambda e, q2_=q_, c_=c: e.tensor_add(
                        out=ft[q2_][:, :], in0=ft[q2_][:, :],
                        in1=fo[c_ % 2][:, :]),
                        wait=[tt1, tp2], inc="sV")
                    p.rd(f"fo{c % 2}", tt2)
                    sc4 = 32 + 48 * (c % 2)
                    tq4 = p.op("scalar", lambda e, q2_=q_, s4=sc4: e.activation(
                        out=fsq[:, :], in_=ft[q2_][:, :], func=AF.Square,
                        accum_out=stats[:, s4:s4 + 1]),
                        wait=[tt2] + p.wr("fsq") + p.wr(f"fst{sc4}"), inc="sS")
                    p.use["fsq"] = [tq4]
                    tr4 = p.op("vector", lambda e, q2_=q_, s4=sc4: e.reduce_sum(
                        out=stats[:, s4 + 1:s4 + 2], in_=ft[q2_][:, :], axis=AXX),
                        wait=[tt2] + p.wr(f"fst{sc4}b"), inc="sV")
                    w1_ = p.op("vector", lambda e, s4=sc4: e.tensor_scalar_mul(
                        out=stats[:, s4 + 2:s4 + 3], in0=stats[:, s4 + 1:s4 + 2],
                        scalar1=1.0 / D), wait=[tr4], inc="sV")
                    w2_ = p.op("vector", lambda e, s4=sc4: e.tensor_mul(
                        out=stats[:, s4 + 3:s4 + 4], in0=stats[:, s4 + 2:s4 + 3],
                        in1=stats[:, s4 + 2:s4 + 3]), wait=[w1_], inc="sV")
                    w3_ = p.op("vector", lambda e, s4=sc4: e.tensor_scalar(
                        out=stats[:, s4 + 4:s4 + 5], in0=stats[:, s4:s4 + 1],
                        scalar1=1.0 / D, scalar2=EPS, op0=ALU.mult, op1=ALU.add),
                        wait=[tq4], inc="sV")
                    w4_ = p.op("vector", lambda e, s4=sc4: e.tensor_sub(
                        out=stats[:, s4 + 5:s4 + 6], in0=stats[:, s4 + 4:s4 + 5],
                        in1=stats[:, s4 + 3:s4 + 4]), wait=[w2_, w3_], inc="sV")
                    p.rd(f"fst{sc4}", w3_)
                    p.rd(f"fst{sc4}b", w1_)
                    w5_ = p.op("vector", lambda e, q2_=q_, s4=sc4: e.tensor_copy(
                        out=stats[:, 96 + 2 * q2_:97 + 2 * q2_],
                        in_=stats[:, s4 + 5:s4 + 6]),
                        wait=[w4_] + p.wr(f"fq{q_}"), inc="sV")
                    w6_ = p.op("vector", lambda e, q2_=q_, s4=sc4: e.tensor_copy(
                        out=stats[:, 97 + 2 * q2_:98 + 2 * q2_],
                        in_=stats[:, s4 + 2:s4 + 3]),
                        wait=[w1_, w5_], inc="sV")
                    p.use[f"fvq{q_}"] = [w6_]

        def emit_epi(c):
                q_ = c % 4
                vtok = p.wr(f"fvq{q_}")
                e1 = p.op("scalar", lambda e, q2_=q_: e.activation(
                    out=stats[:, 104 + 4 * q2_:105 + 4 * q2_],
                    in_=stats[:, 96 + 2 * q2_:97 + 2 * q2_], func=AF.Ln),
                    wait=vtok, inc="sS")
                e2 = p.op("scalar", lambda e, q2_=q_: e.activation(
                    out=stats[:, 105 + 4 * q2_:106 + 4 * q2_],
                    in_=stats[:, 104 + 4 * q2_:105 + 4 * q2_], func=AF.Exp,
                    scale=-0.5), wait=[e1], inc="sS")
                e3 = p.op("vector", lambda e, q2_=q_: e.tensor_mul(
                    out=stats[:, 106 + 4 * q2_:107 + 4 * q2_],
                    in0=stats[:, 97 + 2 * q2_:98 + 2 * q2_],
                    in1=stats[:, 105 + 4 * q2_:106 + 4 * q2_]), wait=[e2], inc="sV")
                e4 = p.op("vector", lambda e, q2_=q_: e.tensor_scalar_mul(
                    out=stats[:, 107 + 4 * q2_:108 + 4 * q2_],
                    in0=stats[:, 106 + 4 * q2_:107 + 4 * q2_], scalar1=-1.0),
                    wait=[e3], inc="sV")
                p.rd(f"fq{q_}", e3)
                to1 = p.op("gpsimd", lambda e, q2_=q_, c_=c: e.tensor_mul(
                    out=fo[c_ % 2][:, :], in0=ft[q2_][:, :],
                    in1=stats[:, 105 + 4 * q2_:106 + 4 * q2_]
                        .broadcast_to([128, D])),
                    wait=[e4] + p.wr(f"fo{c % 2}"), inc="sP")
                p.rd(f"ft{q_}", to1)
                to2 = p.op("gpsimd", lambda e, q2_=q_, c_=c: e.tensor_add(
                    out=fo[c_ % 2][:, :], in0=fo[c_ % 2][:, :],
                    in1=stats[:, 107 + 4 * q2_:108 + 4 * q2_]
                        .broadcast_to([128, D])),
                    wait=[to1], inc="sP")
                ob = p.op("vector", lambda e, q2_=q_, c_=c: e.tensor_add(
                    out=fo[c_ % 2][:, :], in0=fo[c_ % 2][:, :],
                    in1=fxmid[q2_][:, :]),
                    wait=[to2], inc="sV")
                p.rd(f"fxmid{q_}", ob)
                tdo = p.dma("sync", out[L * c:L * (c + 1), :], fo[c % 2][:, :],
                            q="sO", wait=[ob])
                p.rd(f"fo{c % 2}", tdo)

        for g0 in range(0, NCH, 4):
            gs = min(4, NCH - g0)
            emit_group(g0, gs)
            if passb:
                for c in range(g0, g0 + gs):
                    emit_epi(c)

    if stage < 4:
        p.op("gpsimd", lambda e: e.engine_nop(),
             wait=[("sT", p.cnt["sT"]), ("sV", p.cnt["sV"]), ("sS", p.cnt["sS"]),
                   ("qC0", p.cnt.get("qC0", 0))])
        with nc.Block() as block:
            p.finish(block)
        return nc
    ffn_pass(False, tw1a, tw2a, tcb1a)

    passA = [("sT", p.cnt["sT"]), ("sS", p.cnt["sS"]), ("sV", p.cnt["sV"]),
             ("qZO", p.cnt["qZO"]), ("qNI", p.cnt["qNI"])]
    tw1b = dequant_w1((2048, 4096), passA)
    tw2b = dequant_w2((2048, 4096), passA)
    tcb1b = beta_row(2048, tw1b, [])

    ffn_pass(True, tw1b, tw2b, tcb1b)

    p.op("gpsimd", lambda e: e.engine_nop(), wait=[("sO", p.cnt["sO"])])

    with nc.Block() as block:
        p.finish(block)
    return nc


def kernel(L0_attn_x, L0_attn_ln_g, L0_attn_ln_b, L0_attn_Wx, L0_attn_Ww,
           L0_attn_bw, L0_attn_Wk, L0_attn_Wv, L0_attn_Wr, L0_attn_Wo,
           L0_ffn_ln_gamma, L0_ffn_ln_beta, L0_ffn_fc1_q, L0_ffn_fc1_scales,
           L0_ffn_fc1_bias, L0_ffn_fc2_q, L0_ffn_fc2_scales, L0_ffn_fc2_bias):
    consts = build_constants()
    nc = build(T=2048)
    base = dict(
        ln_g=np.ascontiguousarray(L0_attn_ln_g),
        ln_b=np.ascontiguousarray(L0_attn_ln_b),
        bw=np.ascontiguousarray(L0_attn_bw),
        Wx=np.ascontiguousarray(L0_attn_Wx), Ww=np.ascontiguousarray(L0_attn_Ww),
        Wk=np.ascontiguousarray(L0_attn_Wk), Wv=np.ascontiguousarray(L0_attn_Wv),
        Wr=np.ascontiguousarray(L0_attn_Wr), Wo=np.ascontiguousarray(L0_attn_Wo),
        g2=np.ascontiguousarray(L0_ffn_ln_gamma),
        b2v=np.ascontiguousarray(L0_ffn_ln_beta),
        q1=np.ascontiguousarray(L0_ffn_fc1_q),
        s1=np.ascontiguousarray(L0_ffn_fc1_scales),
        b1=np.ascontiguousarray(L0_ffn_fc1_bias),
        q2=np.ascontiguousarray(L0_ffn_fc2_q),
        s2=np.ascontiguousarray(L0_ffn_fc2_scales),
        b2f=np.ascontiguousarray(L0_ffn_fc2_bias),
        **consts,
    )
    in_maps = [dict(base, x=np.ascontiguousarray(L0_attn_x[b])) for b in range(8)]
    res = run_bass_kernel_spmd(nc, in_maps, core_ids=list(range(8)))
    return np.stack([r["out"] for r in res.results])


# version 11
# speedup vs baseline: 1.0402x; 1.0402x over previous
import numpy as np
import ml_dtypes
import concourse.bass as bass
import concourse.mybir as mybir
from concourse.bass_utils import run_bass_kernel_spmd

F32 = mybir.dt.float32
F16 = mybir.dt.float16
BF16 = mybir.dt.bfloat16
I32 = mybir.dt.int32
AF = mybir.ActivationFunctionType
ALU = mybir.AluOpType
AXX = mybir.AxisListType.X

D = 1024
NH = 16
L = 128
W = 32
NSUB = 4
EPS = 1e-5
KSL = 8

KT_OFF = [0, 32, 96, 192]
KT_SZ = [32, 64, 96, 128]
NA = 321
DELTA_C = 320
NB = 256
ECOLS = NA + NB


def build_constants():
    ind = np.triu(np.ones((L, L), np.float32))
    UA = np.zeros((L, NA), np.float32)
    for i in range(NSUB):
        ri = i * W - 1
        for s in range(W * (i + 1)):
            UA[:, KT_OFF[i] + s] = ind[:, s] - (ind[:, ri] if ri >= 0 else 0.0)
    UA[:, DELTA_C] = -1.0
    UB = np.zeros((L, NB), np.float32)
    for i in range(NSUB):
        ri = i * W - 1
        for a in range(W):
            t = i * W + a
            UB[:, 32 * i + a] = (ind[:, ri] if ri >= 0 else 0.0) - ind[:, t]
    for t in range(L):
        UB[:, 128 + t] = -ind[:, t]
    UBr = np.zeros((L, NB), np.float32)
    for i in range(NSUB):
        for a in range(W):
            UBr[i * W + a, 32 * i + a] = 1.0
    for t in range(L):
        UBr[t, 128 + t] = 1.0
    UKH = np.zeros((L, L), np.float32)
    for s in range(L):
        UKH[:, s] = ind[:, s] - ind[:, 127]
    MSK = np.zeros((L, NSUB * W), np.float32)
    for i in range(NSUB):
        MSK[: i * W, 32 * i: 32 * (i + 1)] = 1.0
        blk = (np.arange(W)[:, None] <= np.arange(W)[None, :]).astype(np.float32)
        MSK[i * W: (i + 1) * W, 32 * i: 32 * (i + 1)] = blk
    return dict(
        c_UA=UA.astype(np.float16), c_UB=UB.astype(np.float16),
        c_UBr=UBr.astype(np.float16), c_UKH=UKH.astype(np.float16),
        c_MSK=MSK.astype(ml_dtypes.bfloat16),
        c_ID=np.eye(128, dtype=np.float32).astype(ml_dtypes.bfloat16),
    )


class P:
    ENGS = ("sync", "tensor", "vector", "scalar", "gpsimd")

    def __init__(self, nc, ctx):
        self.nc = nc
        self.ctx = ctx
        self.streams = {e: [] for e in self.ENGS}
        self.h = {}
        self.cnt = {}
        self.marks = {}
        self.use = {}
        for name in ("sT", "sV", "sS", "sP"):
            self.h[name] = ctx.enter_context(nc.semaphore(name))
            self.cnt[name] = 0

    def mksem(self, name):
        if name not in self.h:
            self.h[name] = self.ctx.enter_context(self.nc.semaphore(name))
            self.cnt[name] = 0

    def op(self, eng, fn, wait=(), inc=None, inc_n=1):
        waits = []
        for tok in wait:
            if tok is None:
                continue
            if isinstance(tok, list):
                for t2 in tok:
                    if t2 is not None:
                        waits.append(t2)
                continue
            waits.append(tok)
        dedup = []
        for sem, val in waits:
            key = (eng, sem)
            if self.marks.get(key, 0) < val:
                dedup.append((sem, val))
                self.marks[key] = val
        tok_out = None
        if inc is not None:
            self.cnt[inc] += inc_n
            tok_out = (inc, self.cnt[inc])

        def thunk(e, waits=tuple(dedup), fn=fn, inc=inc, inc_n=inc_n):
            for sem, val in waits:
                e.wait_ge(self.h[sem], val)
            ins = fn(e)
            if inc is not None:
                ins.then_inc(self.h[inc], inc_n)
        self.streams[eng].append(thunk)
        return tok_out

    def dma(self, eng, out, in_, wait=(), q=None, slow=False):
        q = q or ("qS" if eng == "sync" else "qG")
        self.mksem(q)
        k = self.cnt[q] // 16 + 1
        gate = [(q, 16 * (k - 1))] if k >= 2 else []
        return self.op(eng, lambda e, o=out, i=in_, s=slow: e.dma_start(
            out=o, in_=i, allow_slow_non_contiguous=s),
            wait=list(wait) + gate, inc=q, inc_n=16)

    def rd(self, name, tok):
        self.use.setdefault(name, []).append(tok)
        return tok

    def wr(self, name):
        toks = self.use.pop(name, [])
        return toks

    def finish(self, block):
        def run(eng, thunks):
            for t in thunks:
                t(eng)
        for e in self.ENGS:
            getattr(block, e)(lambda eng, th=tuple(self.streams[e]): run(eng, th))


ALLOC_LOG = []


def build(T=2048, dbg=False, stage=4, sub=9, v604=0):
    ALLOC_LOG.clear()
    NCH = T // L
    nc = bass.Bass("TRN2", target_bir_lowering=False,
                   dynamic_dma_scratch_size=2048)
    from contextlib import ExitStack
    ctx = ExitStack()

    def din(name, shape, dt=F32):
        return nc.dram_tensor(name, shape, dt, kind="ExternalInput").ap()

    x = din("x", [T, D])
    ln_g, ln_b, bw = din("ln_g", [D]), din("ln_b", [D]), din("bw", [D])
    Wx, Ww = din("Wx", [D, D]), din("Ww", [D, D])
    Wk, Wv, Wr, Wo = (din(n, [D, D]) for n in ("Wk", "Wv", "Wr", "Wo"))
    g2, b2v = din("g2", [D]), din("b2v", [D])
    q1, s1, b1 = din("q1", [D, 4096], I32), din("s1", [8, 4096]), din("b1", [4096])
    q2, s2, b2f = din("q2", [4096, D], I32), din("s2", [32, D]), din("b2f", [D])
    c_UA, c_UB = din("c_UA", [L, NA], F16), din("c_UB", [L, NB], F16)
    c_UBr, c_UKH = din("c_UBr", [L, NB], F16), din("c_UKH", [L, L], F16)
    c_MSK, c_ID = din("c_MSK", [L, 128], BF16), din("c_ID", [128, 128], BF16)

    out = nc.dram_tensor("out", [T, D], F32, kind="ExternalOutput").ap()
    ikind = "ExternalOutput" if dbg else "Internal"
    xmid_d = nc.dram_tensor("xmid_d", [T, D], F32, kind=ikind).ap()
    xn2_d = nc.dram_tensor("xn2_d", [T, D], BF16, kind=ikind).ap()
    z2a_d = nc.dram_tensor("z2a_d", [T, D], F32, kind=ikind).ap()
    s1b_d = nc.dram_tensor("s1b_d", [8, 4096], BF16, kind=ikind).ap()
    s2b_d = nc.dram_tensor("s2b_d", [32, D], BF16, kind=ikind).ap()
    ckrow_d = nc.dram_tensor("ckrow_d", [1, D], F32, kind="Internal").ap()
    if dbg:
        scrd_d = nc.dram_tensor("scrd_d", [T, D], F32, kind="ExternalOutput").ap()
        spd_d = nc.dram_tensor("spd_d", [T, D], F16, kind="ExternalOutput").ap()
        sprd_d = nc.dram_tensor("sprd_d", [T, D], F16,
                                kind="ExternalOutput").ap()
        ebd_d = nc.dram_tensor("ebd_d", [T, 8 * ECOLS], BF16,
                               kind="ExternalOutput").ap()
        yfmd_d = nc.dram_tensor("yfmd_d", [T, D], BF16,
                                kind="ExternalOutput").ap()
        f1d_d = nc.dram_tensor("f1d_d", [128, 16384], BF16,
                               kind="ExternalOutput").ap()
        f2d_d = nc.dram_tensor("f2d_d", [128, 16384], BF16,
                               kind="ExternalOutput").ap()
        fhd_d = nc.dram_tensor("fhd_d", [128, 8192], BF16,
                               kind="ExternalOutput").ap()
        cbd_d = nc.dram_tensor("cbd_d", [128, 32], F32,
                               kind="ExternalOutput").ap()
        cbr_d = nc.dram_tensor("cbr_d", [1, 4096], BF16,
                               kind="ExternalOutput").ap()
        zbd_d = nc.dram_tensor("zbd_d", [1, 512], BF16,
                               kind="ExternalOutput").ap()
        qd_d = nc.dram_tensor("qd_d", [128, 2048], I32,
                              kind="ExternalOutput").ap()
        scd_d = nc.dram_tensor("scd_d", [128, 1024], F32,
                               kind="ExternalOutput").ap()
        arn0_d = nc.dram_tensor("arn0_d", [128, 512], F32,
                                kind="ExternalOutput").ap()
        arn1_d = nc.dram_tensor("arn1_d", [128, 512], F32,
                                kind="ExternalOutput").ap()
        arnc_d = nc.dram_tensor("arnc_d", [2048, 512], F32,
                                kind="ExternalOutput").ap()
    cb1_d = nc.dram_tensor("cb1_d", [1, 4096], F32, kind="Internal").ap()
    cvrow_d = nc.dram_tensor("cvrow_d", [1, D], F32, kind="Internal").ap()

    ARENA = (nc.sbuf_bytes_remaining - 64) & ~31
    arena = nc.alloc_sbuf_tensor("arena", [128, ARENA], mybir.dt.uint8)
    abase = nc.lookup_mloc(arena).addr
    sb_off = [abase]

    def sb(name, shape, dt):
        at = sb_off[0]
        per = int(np.prod(shape[1:])) * mybir.dt.size(dt)
        sb_off[0] = (at + per + 31) & ~31
        assert sb_off[0] <= abase + ARENA, (name, sb_off[0] - abase)
        ALLOC_LOG.append((name, at - abase, per, shape[0]))
        return nc.alloc_sbuf_tensor_at(name, shape, dt, offset=at)

    sb("scratchpad", [128, 6144], mybir.dt.uint8)
    gcol = sb("gcol", [128, KSL], F32)
    bgcol = sb("bgcol", [128, KSL], F32)
    g2col = sb("g2col", [128, KSL], F32)
    bg2col = sb("bg2col", [128, KSL], F32)
    g2bias = sb("g2bias", [128, KSL], F32)
    bgcol_b = sb("bgcol_b", [128, KSL], BF16)
    bg2col_b = sb("bg2col_b", [128, KSL], BF16)
    stats = sb("stats", [128, 128], F32)
    m3c = sb("m3c", [128, 3 * 16], F32)
    rowsA = sb("rowsA", [128, 3072], F32)
    rowsB = sb("rowsB", [128, 4096], BF16)
    rowsC = sb("rowsC", [128, 3 * 1024 + 128], BF16)
    zb = sb("zb", [128, 512], BF16)
    onef = sb("onef", [128, 1], F32)
    rowtmp = rowsA[0:1, 0:2048]
    bwrow = rowsA[0:1, 2048:3072]
    b1row = rowsB[0:1, :]
    crow_w = rowsC[0:1, 0:D]
    cb2row = rowsC[0:1, D:2 * D]
    crow_r = rowsC[0:1, 2 * D:3 * D]
    ones = rowsC[0:1, 3 * D:3 * D + 128]
    ones0 = ones

    base_big = sb_off[0]

    uA = sb("uA", [128, NA], F16)
    uB = sb("uB", [128, NB], F16)
    uBr = sb("uBr", [128, NB], F16)
    uKH = sb("uKH", [128, 128], F16)
    msk = sb("msk", [128, 128], BF16)
    idn = sb("idn", [128, 128], BF16)
    ckb = sb("ckb", [128, D], BF16)
    cvb = sb("cvb", [128, D], BF16)
    S0 = sb("S0", [128, 512], BF16)
    S1 = sb("S1", [128, 512], BF16)
    Sd = sb("Sd", [128, 512], BF16)
    S2 = sb("S2", [128, KSL * 128], BF16)
    wM = sb("wM", [128, KSL * D], BF16)
    wK = sb("wK", [128, KSL * D], BF16)
    wV = sb("wV", [128, KSL * D], BF16)
    wR = sb("wR", [128, KSL * D], BF16)
    wO = sb("wO", [128, KSL * D], BF16)
    xbuf = [sb(f"xbuf{i}", [128, D], F32) for i in range(2)]
    xn_tm = [sb("xn_tm0", [128, D], BF16)]
    xn_fm = [sb(f"xn_fm{i}", [128, D], BF16) for i in range(2)]
    spb = [sb(f"spb{i}", [128, D], F16) for i in range(2)]
    sprb = [sb(f"sprb{i}", [128, D], F16) for i in range(2)]
    scr = [sb(f"scr{i}", [128, D], F32) for i in range(2)]
    sqd = sb("sqd", [128, D], BF16)
    Eb = [sb(f"Eb{i}", [128, KSL * ECOLS], BF16) for i in range(2)]
    Ekh = [sb(f"Ekh{i}", [128, D], BF16) for i in range(2)]
    ktm = [sb(f"ktm{i}", [128, D], BF16) for i in range(2)]
    kfm = [sb(f"kfm{i}", [128, D], BF16) for i in range(2)]
    vtm = [sb(f"vtm{i}", [128, D], BF16) for i in range(2)]
    khtm = [sb(f"khtm{i}", [128, D], BF16) for i in range(2)]
    ktil = [sb(f"ktil{i}", [128, KSL * 128], BF16) for i in range(2)]
    pevac = [sb(f"pevac{i}", [128, 512], BF16) for i in range(2)]
    yfm = [sb(f"yfm{i}", [128, D], BF16) for i in range(2)]
    xmidb = [sb(f"xmidb{i}", [128, D], F32) for i in range(2)]
    xn2b_off = sb_off[0]
    xn2b = [sb(f"xn2b{i}", [128, D], BF16) for i in range(2)]
    xn2f = [sb(f"xn2f{i}", [128, D], BF16) for i in range(2)]
    cb1r = nc.alloc_sbuf_tensor_at("cb1r", [128, 4096], BF16,
                                   offset=xn2b_off)
    attn_end = sb_off[0]

    sb_off[0] = base_big
    fw1 = sb("fw1", [128, KSL * 2048], BF16)
    fw2 = sb("fw2", [128, 16 * D], BF16)
    qst = [sb(f"qst{i}", [128, 2048], I32) for i in range(2)]
    sbc = [sb(f"sbc{i}", [128, 2048], BF16) for i in range(2)]
    qcv = [sb(f"qcv{i}", [128, 2048], BF16) for i in range(2)]
    fxn2 = [sb(f"fxn2{i}", [128, KSL * 512], BF16) for i in range(2)]
    fh = sb("fh", [128, 16 * 512], BF16)
    fsq = sb("fsq", [128, D], BF16)
    fz2a = [sb(f"fz2a{i}", [128, D], F32) for i in range(2)]
    fxmid = [sb(f"fxmid{i}", [128, D], F32) for i in range(4)]
    ft = [sb(f"ft{i}", [128, D], F32) for i in range(4)]
    fo = [sb(f"fo{i}", [128, D], F32) for i in range(2)]
    ffn_end = sb_off[0]

    stg_off = (attn_end - KSL * D * 2 - 2 * D * 4 - 64) & ~31
    wstage = [nc.alloc_sbuf_tensor_at(f"wstage{i}", [128, D], F32,
                                      offset=stg_off + i * D * 4) for i in range(2)]
    mstage = nc.alloc_sbuf_tensor_at("mstage", [128, KSL * D], BF16,
                                     offset=stg_off + 2 * D * 4)

    assert max(attn_end, ffn_end) <= abase + ARENA, (attn_end - abase,
                                                     ffn_end - abase)

    pool = [nc.alloc_psum_tensor(f"pp{i}", [128, 512], F32) for i in range(5)]
    outp = nc.alloc_psum_tensor("outp", [128, 1024], F32).ap()
    sadd = nc.alloc_psum_tensor("sadd", [128, 512], F32).ap()

    p = P(nc, ctx)

    def mm_group(psl, lhsTs, rhss, wait, extra=None):
        n = len(lhsTs)
        for j in range(n):
            first = (j == 0)
            last = (j == n - 1) and extra is None
            tk = p.op("tensor",
                      lambda e, o=psl, a=lhsTs[j], b=rhss[j], f=first, s=last:
                      e.matmul(o, a, b, start=f, stop=s, skip_group_check=True),
                      wait=wait if j == 0 else (),
                      inc="sT" if last else None)
        if extra is not None:
            oa, ra = extra
            tk = p.op("tensor",
                      lambda e, o=psl, a=oa, b=ra:
                      e.matmul(o, a, b, start=False, stop=True,
                               skip_group_check=True),
                      inc="sT")
        return tk

    def eslice(buf, sl, c0, c1):
        return buf[:, ECOLS * sl + c0: ECOLS * sl + c1]

    tU = [p.dma("sync", uA[:, :], c_UA, q="qC1"), p.dma("sync", uB[:, :], c_UB, q="qC2"),
          p.dma("sync", uBr[:, :], c_UBr, q="qC3"), p.dma("sync", uKH[:, :], c_UKH, q="qC0"),
          p.dma("sync", msk[:, :], c_MSK, q="qC1"), p.dma("sync", idn[:, :], c_ID, q="qC2")]
    def col_stage(dst, vec, q_):
        trow = p.dma("sync", rowtmp[0:1, 0:1024], vec[None, :],
                     wait=p.wr("rowtmp"), q=q_)
        tm = None
        for s in range(KSL):
            tm = p.op("tensor", lambda e, s_=s: e.matmul(
                pool[1].ap()[:, s_:s_ + 1],
                rowtmp[0:1, 128 * s_:128 * (s_ + 1)],
                onef[0:1, 0:1], start=(s_ == 0), stop=(s_ == KSL - 1),
                skip_group_check=True),
                wait=([trow, tonef] + p.wr("pl1")) if s == 0 else (),
                inc="sT" if s == KSL - 1 else None)
        p.rd("rowtmp", tm)
        tev = p.op("scalar", lambda e, d_=dst: e.activation(
            out=d_[:, :], in_=pool[1].ap()[:, 0:KSL], func=AF.Identity),
            wait=[tm], inc="sS")
        p.use["pl1"] = [tev]
        return tev

    tonef = p.op("vector", lambda e: e.memset(onef[0:1, 0:1], 1.0), inc="sV")
    tg = col_stage(gcol, ln_g, "qC3")
    tb = col_stage(bgcol, ln_b, "qC0")
    tg2 = col_stage(g2col, g2, "qC1")
    tb2 = col_stage(bg2col, b2v, "qC2")
    if dbg:
        p.dma("sync", cbd_d[:, 24:32], gcol[:, :], wait=[tg], q="qNO")
        p.dma("sync", scd_d[:, 0:8], g2col[:, :], wait=[tg2], q="qNO")
        arenaf = arena[:, :].bitcast(F32)
        p.dma("sync", arn0_d[:, :], arenaf[:, 0:512], wait=[tg2, tb2],
              q="qNO")
    tbw = p.dma("sync", bwrow[:, :], bw[None, :], q="qC3")
    tone = p.op("vector", lambda e: e.memset(ones[:, :], 1.0), inc="sV")
    tzb = p.op("vector", lambda e: e.memset(zb[0:1, 0:512], 1.0), inc="sV")
    tS = p.op("vector", lambda e: e.memset(S0[:, :], 0.0), inc="sV")
    tS2z = p.op("vector", lambda e: e.memset(S2[:, :], 0.0), inc="sV")
    tS = [tS, tS2z]

    def recip_fold(dstcol, srccol, wtok, tmp0, tmp1):
        t1 = p.op("vector", lambda e: e.tensor_scalar_add(
            out=stats[:, tmp0:tmp0 + KSL], in0=srccol[:, :], scalar1=1e-20),
            wait=wtok, inc="sV")
        t2 = p.op("vector", lambda e: e.reciprocal(
            out=stats[:, tmp1:tmp1 + KSL], in_=stats[:, tmp0:tmp0 + KSL]),
            wait=[t1], inc="sV")
        t3 = p.op("vector", lambda e: e.tensor_mul(
            out=dstcol[:, :], in0=dstcol[:, :], in1=stats[:, tmp1:tmp1 + KSL]),
            wait=[t2], inc="sV")
        return t3

    tbg = recip_fold(bgcol, gcol, [tg, tb], 0, 8)
    tbg2 = recip_fold(bg2col, g2col, [tg2, tb2], 16, 24)
    tbg = p.op("vector", lambda e: e.tensor_copy(out=bgcol_b[:, :], in_=bgcol[:, :]),
               wait=[tbg], inc="sV")
    tbg2 = p.op("vector", lambda e: e.tensor_copy(out=bg2col_b[:, :],
                                                  in_=bg2col[:, :]),
                wait=[tbg2], inc="sV")
    tgb = p.op("vector", lambda e: e.tensor_scalar_mul(
        out=g2bias[:, :], in0=g2col[:, :], scalar1=-128.0), wait=[tg2], inc="sV")

    def load_fold(wdst, wsrc, coltok, colap, start_wait=()):
        tlast = None
        for sl in range(KSL):
            stg = wstage[sl % 2]
            tl = p.dma("sync", stg[:, :], wsrc[128 * sl:128 * (sl + 1), :],
                       wait=p.wr(f"wstg{sl % 2}") + list(start_wait), q=f"qW{sl % 4}")
            if colap is None:
                tc = p.op("vector", lambda e, s=sl, g_=stg: e.tensor_copy(
                    out=wdst[:, D * s:D * (s + 1)], in_=g_[:, :]),
                    wait=[tl], inc="sV")
            else:
                tc = p.op("vector", lambda e, s=sl, g_=stg: e.tensor_scalar_mul(
                    out=wdst[:, D * s:D * (s + 1)], in0=g_[:, :],
                    scalar1=colap[:, s:s + 1]),
                    wait=[tl, coltok], inc="sV")
            p.rd(f"wstg{sl % 2}", tc)
            tlast = tc
        return tlast

    tWw = load_fold(mstage, Ww, None, None)
    tWxg = load_fold(wK, Wx, tg, gcol)
    bt0 = pool[3].ap().bitcast(BF16)
    tTx = None
    for jc in range(KSL):
        tt = None
        for s in range(KSL):
            tt = p.op("tensor", lambda e, s_=s, jc_=jc: e.transpose(
                bt0[:, 128 * s_:128 * (s_ + 1)],
                wK[:, D * s_ + 128 * jc_: D * s_ + 128 * (jc_ + 1)], idn[:, :]),
                wait=([tWxg, tU[5]] + p.wr("pl3")) if s == 0 else (),
                inc="sT" if s == KSL - 1 else None)
        tTx = p.op("vector", lambda e, jc_=jc: e.tensor_copy(
            out=wO[:, D * jc_:D * (jc_ + 1)], in_=bt0[:, :]),
            wait=[tt], inc="sV")
        p.use["pl3"] = [tTx]
    tM = None
    for sl in range(KSL):
        for hf in range(2):
            ps = pool[(sl * 2 + hf) % 2]
            tmm = mm_group(ps[:, :],
                           [wO[:, D * k + 128 * sl: D * k + 128 * (sl + 1)]
                            for k in range(KSL)],
                           [mstage[:, D * k + 512 * hf: D * k + 512 * (hf + 1)]
                            for k in range(KSL)],
                           wait=[tWw, tTx, p.use.get(f"pl{(sl * 2 + hf) % 2}")])
            tM = p.op("vector", lambda e, s=sl, h=hf, ps_=ps: e.tensor_copy(
                out=wM[:, D * s + 512 * h: D * s + 512 * (h + 1)], in_=ps_[:, :]),
                wait=[tmm], inc="sV")
            p.use[f"pl{(sl * 2 + hf) % 2}"] = [tM]

    tWk = load_fold(wK, Wk, tg, gcol, start_wait=[tM])
    tWv = load_fold(wV, Wv, tg, gcol)
    tWr = load_fold(wR, Wr, tg, gcol)
    tWo = load_fold(wO, Wo, None, None, start_wait=[tM])

    def row_mm(wten, wtok, ncols, dst32, dst32tok):
        tlast = None
        for ns in range(ncols // 512):
            ps = pool[2 + (ns % 2)]
            tmm = mm_group(ps[0:1, :],
                           [bgcol_b[:, k:k + 1] for k in range(KSL)],
                           [wten[:, D * k + 512 * ns: D * k + 512 * (ns + 1)]
                            for k in range(KSL)],
                           wait=[tbg, wtok, p.use.get(f"pl{2 + ns % 2}")])
            tv = p.op("vector", lambda e, n=ns, ps_=ps: e.tensor_copy(
                out=dst32[0:1, 512 * n:512 * (n + 1)], in_=ps_[0:1, :]),
                wait=[tmm, dst32tok], inc="sV")
            p.use[f"pl{2 + ns % 2}"] = [tv]
            tlast = tv
        return tlast

    t_rw = row_mm(wM, tM, D, rowtmp, None)
    t_rw2 = p.op("vector", lambda e: e.tensor_add(
        out=rowtmp[0:1, 1024:2048], in0=rowtmp[0:1, 0:1024], in1=bwrow[0:1, :]),
        wait=[t_rw, tbw], inc="sV")
    t_cw = p.op("vector", lambda e: e.tensor_copy(
        out=crow_w[0:1, :], in_=rowtmp[0:1, 1024:2048]), wait=[t_rw2], inc="sV")
    t_rr = row_mm(wR, tWr, D, rowtmp, t_cw)
    t_cr = p.op("vector", lambda e: e.tensor_copy(
        out=crow_r[0:1, :], in_=rowtmp[0:1, 0:1024]), wait=[t_rr], inc="sV")
    t_rk = row_mm(wK, tWk, D, rowtmp, t_cr)
    t_dk = p.dma("sync", ckrow_d, rowtmp[0:1, 0:1024], wait=[t_rk], q="qR")
    t_rv = row_mm(wV, tWv, D, rowtmp[0:1, 1024:2048], t_cw)
    t_dv = p.dma("sync", cvrow_d, rowtmp[0:1, 1024:2048], wait=[t_rv], q="qR")
    t_bk = p.dma("gpsimd", ckb[:, :], ckrow_d.broadcast_to([128, D]), wait=[t_dk], q="qRB")
    t_bv = p.dma("gpsimd", cvb[:, :], cvrow_d.broadcast_to([128, D]), wait=[t_dv], q="qRB")

    PRO_END = [("sV", p.cnt["sV"]), ("sT", p.cnt["sT"]), ("sS", p.cnt["sS"])]
    for nm in ("yfm0", "yfm1", "xmidb0", "xmidb1", "xn2b0", "xn2b1",
               "xn2f0", "xn2f1", "ktil0", "ktil1", "pevac0", "pevac1",
               "khtm0", "khtm1", "vtm0", "vtm1", "kfm0", "kfm1",
               "ktm0", "ktm1", "Ekh0", "Ekh1", "Eb0", "Eb1"):
        p.use.setdefault(nm, []).extend(PRO_END)
    tok_S_box = [tS]

    def emit_attn_chunk(c):
        tok_S = tok_S_box[0]
        pr = c % 2

        def gate():
            toks = [("sT", p.cnt["sT"]), ("sV", p.cnt["sV"]),
                    ("sS", p.cnt["sS"]), ("sP", p.cnt["sP"])]
            for nm in list(p.use.keys()):
                p.use[nm] = p.use[nm] + toks
        sc = 32 + 48 * pr
        xb = xbuf[c % 2]
        tx = p.dma("sync", xb[:, :], x[L * c:L * (c + 1), :],
                   wait=p.wr(f"xbuf{c % 2}"), q="qX")
        tsq = p.op("scalar", lambda e, xb_=xb: e.activation(
            out=sqd[:, :], in_=xb_[:, :], func=AF.Square,
            accum_out=stats[:, sc:sc + 1]),
            wait=[tx] + p.wr("sqd") + p.wr(f"st{sc}"), inc="sS")
        p.use["sqd"] = [tsq]
        tsm = p.op("vector", lambda e, xb_=xb: e.reduce_sum(
            out=stats[:, sc + 1:sc + 2], in_=xb_[:, :], axis=AXX),
            wait=[tx] + p.wr(f"st{sc + 1}"), inc="sV")
        t1 = p.op("vector", lambda e: e.tensor_scalar_mul(
            out=stats[:, sc + 2:sc + 3], in0=stats[:, sc + 1:sc + 2],
            scalar1=1.0 / D), wait=[tsm], inc="sV")
        t2 = p.op("vector", lambda e: e.tensor_mul(
            out=stats[:, sc + 3:sc + 4], in0=stats[:, sc + 2:sc + 3],
            in1=stats[:, sc + 2:sc + 3]), wait=[t1], inc="sV")
        t3 = p.op("vector", lambda e: e.tensor_scalar(
            out=stats[:, sc + 4:sc + 5], in0=stats[:, sc:sc + 1],
            scalar1=1.0 / D, scalar2=None, op0=ALU.mult), wait=[tsq], inc="sV")
        t4 = p.op("vector", lambda e: e.tensor_sub(
            out=stats[:, sc + 5:sc + 6], in0=stats[:, sc + 4:sc + 5],
            in1=stats[:, sc + 3:sc + 4]), wait=[t2, t3], inc="sV")
        t5 = p.op("vector", lambda e: e.tensor_scalar(
            out=stats[:, sc + 6:sc + 7], in0=stats[:, sc + 5:sc + 6],
            scalar1=(1.0 + EPS), scalar2=EPS * EPS, op0=ALU.mult, op1=ALU.add),
            wait=[t4], inc="sV")
        t6 = p.op("scalar", lambda e: e.activation(
            out=stats[:, sc + 7:sc + 8], in_=stats[:, sc + 6:sc + 7], func=AF.Ln),
            wait=[t5], inc="sS")
        t7 = p.op("scalar", lambda e: e.activation(
            out=stats[:, sc + 8:sc + 9], in_=stats[:, sc + 7:sc + 8], func=AF.Exp,
            scale=-0.5), wait=[t6], inc="sS")
        t8 = p.op("vector", lambda e: e.tensor_mul(
            out=stats[:, sc + 9:sc + 10], in0=stats[:, sc + 2:sc + 3],
            in1=stats[:, sc + 8:sc + 9]), wait=[t7], inc="sV")
        t9 = p.op("vector", lambda e: e.tensor_scalar_mul(
            out=stats[:, sc + 10:sc + 11], in0=stats[:, sc + 9:sc + 10],
            scalar1=-1.0), wait=[t8], inc="sV")
        p.rd(f"st{sc}", t3)
        p.rd(f"st{sc + 1}", t1)
        txg = p.op("gpsimd", lambda e, xb_=xb: e.tensor_mul(
            out=scr[0][:, :], in0=xb_[:, :],
            in1=stats[:, sc + 8:sc + 9].broadcast_to([128, D])),
            wait=[t9] + p.wr("scr0"), inc="sP")
        p.use["scr0"] = [txg]
        txn = p.op("gpsimd", lambda e: e.tensor_add(
            out=xn_tm[0][:, :], in0=scr[0][:, :],
            in1=stats[:, sc + 10:sc + 11].broadcast_to([128, D])),
            wait=[txg] + p.wr("xn_tm0"), inc="sP")
        p.rd("scr0", txn)
        bt = pool[0].ap().bitcast(BF16)
        ttr = None
        for sl in range(KSL):
            ttr = p.op("tensor", lambda e, s=sl: e.transpose(
                bt[:, 128 * s:128 * (s + 1)], xn_tm[0][:, 128 * s:128 * (s + 1)],
                idn[:, :]),
                wait=([txn, tU[5]] + p.wr("pl0")) if sl == 0 else (),
                inc="sT" if sl == KSL - 1 else None)
        txnf = p.op("vector", lambda e: e.tensor_copy(
            out=xn_fm[pr][:, :], in_=bt[:, :]),
            wait=[ttr] + p.wr(f"xn_fm{pr}"), inc="sV")
        p.use["pl0"] = [txnf]
        p.rd("xn_tm0", txnf)
        if sub < 2:
            gate()
            return

        def softplus_chain(wten, wtok, crow, crtok, spdst, spname, neg):
            tps = []
            for hf in range(2):
                ps = pool[1 + hf]
                tmm = mm_group(ps[:, :],
                               [xn_fm[pr][:, 128 * k:128 * (k + 1)]
                                for k in range(KSL)],
                               [wten[:, D * k + 512 * hf:D * k + 512 * (hf + 1)]
                                for k in range(KSL)],
                               wait=[txnf, wtok, crtok, tone] + p.wr(f"pl{1 + hf}"),
                               extra=None if v604 == 20 else
                               (ones[:, :], crow[0:1, 512 * hf:512 * (hf + 1)]))
                a = scr[hf]
                ta = p.op("scalar", lambda e, ps_=ps, a_=a: e.activation(
                    out=a_[:, 0:512], in_=ps_[:, :], func=AF.Abs),
                    wait=[tmm] + p.wr(f"scr{hf}"), inc="sS")
                te = p.op("scalar", lambda e, a_=a: e.activation(
                    out=a_[:, 512:1024], in_=a_[:, 0:512], func=AF.Exp, scale=-1.0),
                    wait=[ta], inc="sS")
                tad = p.op("vector", lambda e, a_=a: e.tensor_scalar_add(
                    out=a_[:, 0:512], in0=a_[:, 512:1024], scalar1=1.0),
                    wait=[te], inc="sV")
                tl = p.op("scalar", lambda e, a_=a: e.activation(
                    out=a_[:, 512:1024], in_=a_[:, 0:512], func=AF.Ln),
                    wait=[tad], inc="sS")
                if dbg and not neg:
                    tds_ = p.dma("sync",
                                 scrd_d[L * c:L * (c + 1),
                                        512 * hf:512 * (hf + 1)],
                                 a[:, 512:1024], wait=[tl], q="qNO")
                    p.rd(f"scr{hf}", tds_)
                op0 = ALU.min if neg else ALU.max
                op1 = ALU.subtract if neg else ALU.add
                tp = p.op("vector", lambda e, ps_=ps, a_=a, h=hf, o0=op0, o1=op1:
                          e.scalar_tensor_tensor(
                    out=spdst[:, 512 * h:512 * (h + 1)], in0=ps_[:, :], scalar=0.0,
                    in1=a_[:, 512:1024], op0=o0, op1=o1),
                    wait=[tl] + (p.wr(spname) if hf == 0 else []), inc="sV")
                p.use[f"scr{hf}"] = [tp]
                p.use[f"pl{1 + hf}"] = [tp]
                tps.append(tp)
            return tps

        tsp = softplus_chain(wM, tM, crow_w, t_cw, spb[pr], f"spb{pr}", False)
        tspr = softplus_chain(wR, tWr, crow_r, t_cr, sprb[pr], f"sprb{pr}", True)
        if dbg:
            tdd = p.dma("sync", spd_d[L * c:L * (c + 1), :], spb[pr][:, :],
                        wait=[tsp[1]], q="qMO")
            p.rd(f"spb{pr}", tdd)
            tdd = p.dma("sync", sprd_d[L * c:L * (c + 1), :], sprb[pr][:, :],
                        wait=[tspr[1]], q="qMO")
            p.rd(f"sprb{pr}", tdd)

        def proj(wten, wtok, fold, foldtok, dst, dstn):
            tvs = []
            for hf in range(2):
                ps = pool[1 + hf]
                tmm = mm_group(ps[:, :],
                               [xn_fm[pr][:, 128 * k:128 * (k + 1)]
                                for k in range(KSL)],
                               [wten[:, D * k + 512 * hf: D * k + 512 * (hf + 1)]
                                for k in range(KSL)],
                               wait=[txnf, wtok] + p.wr(f"pl{1 + hf}"))
                tv = p.op("vector", lambda e, h=hf, ps_=ps: e.tensor_add(
                    out=dst[:, 512 * h:512 * (h + 1)], in0=ps_[:, :],
                    in1=fold[:, 512 * h:512 * (h + 1)]),
                    wait=[tmm, foldtok] + (p.wr(dstn) if hf == 0 else []), inc="sV")
                p.use[f"pl{1 + hf}"] = [tv]
                tvs.append(tv)
            return tvs[-1]

        tkp = proj(wK, tWk, ckb, t_bk, ktm[pr], f"ktm{pr}")
        tvp = proj(wV, tWv, cvb, t_bv, vtm[pr], f"vtm{pr}")

        bt2 = pool[0].ap().bitcast(BF16)
        ttr2 = None
        for sl in range(KSL):
            ttr2 = p.op("tensor", lambda e, s=sl: e.transpose(
                bt2[:, 128 * s:128 * (s + 1)], ktm[pr][:, 128 * s:128 * (s + 1)],
                idn[:, :]),
                wait=([tkp] + p.wr("pl0")) if sl == 0 else (),
                inc="sT" if sl == KSL - 1 else None)
        tkf = p.op("vector", lambda e: e.tensor_copy(
            out=kfm[pr][:, :], in_=bt2[:, :]),
            wait=[ttr2] + p.wr(f"kfm{pr}"), inc="sV")
        p.use["pl0"] = [tkf]
        p.rd(f"xn_fm{pr}", ttr2)
        p.rd(f"ktm{pr}", ttr2)
        if sub < 3:
            gate()
            return

        teAB = []
        for sl in range(KSL):
            psA = pool[1 + (sl % 2)]
            psB = pool[3]
            tA = mm_group(psA[:, 0:NA],
                          [spb[pr][:, 128 * sl:128 * (sl + 1)]], [uA[:, :]],
                          wait=[tsp[1], tU[0]] + p.wr(f"pl{1 + (sl % 2)}"))
            tB1 = p.op("tensor", lambda e, s=sl, ps_=psB: e.matmul(
                ps_[:, 0:NB], spb[pr][:, 128 * s:128 * (s + 1)], uB[:, :],
                start=True, stop=False, skip_group_check=True),
                wait=[tsp[1], tU[1], tU[2]] + p.wr("pl3"))
            tB2 = p.op("tensor", lambda e, s=sl, ps_=psB: e.matmul(
                ps_[:, 0:NB], sprb[pr][:, 128 * s:128 * (s + 1)], uBr[:, :],
                start=False, stop=True, skip_group_check=True),
                wait=[tspr[1]], inc="sT")
            teA = p.op("scalar", lambda e, s=sl, ps_=psA: e.activation(
                out=eslice(Eb[pr], s, 0, NA), in_=ps_[:, 0:NA], func=AF.Exp),
                wait=[tA] + (p.wr(f"Eb{pr}") if sl == 0 else []), inc="sS")
            teB = p.op("scalar", lambda e, s=sl, ps_=psB: e.activation(
                out=eslice(Eb[pr], s, NA, ECOLS), in_=ps_[:, 0:NB], func=AF.Exp),
                wait=[tB2], inc="sS")
            p.use[f"pl{1 + (sl % 2)}"] = [teA]
            p.use["pl3"] = [teB]
            teAB.append(teB if sl == KSL - 1 else teA)
        tEdone = teAB[-1]
        if dbg:
            tdd = p.dma("sync", ebd_d[L * c:L * (c + 1), :], Eb[pr][:, :],
                        wait=[tEdone], q="qNO")
            p.rd(f"Eb{pr}", tdd)

        tkh1 = p.op("tensor", lambda e: e.matmul(
            pool[0].ap()[:, :], uKH[:, :], spb[pr][:, 0:512],
            start=True, stop=True, skip_group_check=True),
            wait=[tsp[1], tU[3]] + p.wr("pl0"), inc="sT")
        tek1 = p.op("scalar", lambda e: e.activation(
            out=Ekh[pr][:, 0:512], in_=pool[0].ap()[:, :], func=AF.Exp),
            wait=[tkh1] + p.wr(f"Ekh{pr}"), inc="sS")
        p.use["pl0"] = [tek1]
        tkh2 = p.op("tensor", lambda e: e.matmul(
            pool[0].ap()[:, :], uKH[:, :], spb[pr][:, 512:1024],
            start=True, stop=True, skip_group_check=True),
            wait=p.wr("pl0"), inc="sT")
        tek2 = p.op("scalar", lambda e: e.activation(
            out=Ekh[pr][:, 512:1024], in_=pool[0].ap()[:, :], func=AF.Exp),
            wait=[tkh2], inc="sS")
        p.use["pl0"] = [tek2]
        p.rd(f"spb{pr}", tkh2)
        p.rd(f"spb{pr}", tEdone)
        p.rd(f"sprb{pr}", tEdone)

        tkh = p.op("gpsimd", lambda e: e.tensor_mul(
            out=khtm[pr][:, :], in0=ktm[pr][:, :], in1=Ekh[pr][:, :]),
            wait=[tek1, tek2, tkp] + p.wr(f"khtm{pr}"), inc="sP")
        p.rd(f"ktm{pr}", tkh)
        p.rd(f"Ekh{pr}", tkh)
        if sub < 4:
            gate()
            return

        Scur = S0 if pr == 0 else S1
        Snxt = S1 if pr == 0 else S0
        Ebr = Eb[pr][:, :].rearrange("q (s n) -> q s n", s=KSL)
        t_i = None
        for slh in range(KSL):
            t_i = p.op("tensor", lambda e, s_=slh: e.matmul(
                outp[:, 128 * s_:128 * (s_ + 1)],
                S2[:, 128 * s_:128 * (s_ + 1)],
                Ebr[:, s_, NA + 128:NA + 256],
                start=(s_ in (0, 4)), stop=False, skip_group_check=True),
                wait=([tok_S, tEdone] + p.wr("outp")) if slh == 0 else (),
                inc="sT" if slh == KSL - 1 else None)
        p.rd("S2buf", t_i)
        p.rd(f"Eb{pr}", t_i)
        tok_pv = t_i
        if sub < 4.1:
            gate()
            return

        for i in range(NSUB):
            kb = ktil[i % 2]
            pe = pevac[i % 2]
            sr = KT_SZ[i]
            tkt_i = p.op("gpsimd", lambda e, i_=i, kb_=kb: e.tensor_mul(
                out=kb_[:, :].rearrange("q (s n) -> q s n", s=KSL)[:, :, 0:KT_SZ[i_]],
                in0=kfm[pr][:, :].rearrange("q (s n) -> q s n", s=KSL)
                    [:, :, 0:KT_SZ[i_]],
                in1=Eb[pr][:, :].rearrange("q (s n) -> q s n", s=KSL)
                    [:, :, KT_OFF[i_]:KT_OFF[i_] + KT_SZ[i_]]),
                wait=[tkf, tEdone] + p.wr(f"ktil{i % 2}"), inc="sP")
            p.rd(f"kfm{pr}", tkt_i)
            p.rd(f"Eb{pr}", tkt_i)
            if sub < 4.2:
                if i == NSUB - 1:
                    gate()
                    return
                continue
            tP = None
            heads = list(range(0, NH, 2)) + list(range(1, NH, 2))
            for hi, h in enumerate(heads):
                pb, slh = (h % 2) * 64, h // 2
                tgt = pool[3] if h % 2 == 0 else pool[4]
                tP = p.op("tensor", lambda e, h_=h, pb_=pb, s_=slh, i_=i, kb_=kb,
                          sr_=sr, tg_=tgt: e.matmul(
                    tg_.ap()[0:sr_, 32 * (h_ // 2):32 * (h_ // 2) + 32],
                    kb_[pb_:pb_ + 64, 128 * s_:128 * s_ + sr_],
                    Ebr[pb_:pb_ + 64, s_, NA + 32 * i_:NA + 32 * (i_ + 1)],
                    start=True, stop=True, skip_group_check=True),
                    wait=([tkt_i] + p.wr("pl3") + p.wr("pl4")) if hi == 0 else (),
                    inc="sT" if hi == NH - 1 else None)
            p.rd(f"ktil{i % 2}", tP)
            p.rd(f"Eb{pr}", tP)
            if sub < 4.3:
                if i == NSUB - 1:
                    gate()
                    return
                continue
            tpe0 = p.op("vector", lambda e, i_=i, pe_=pe, sr_=sr: e.tensor_tensor(
                out=pe_[0:sr_, :].rearrange("q (h two n) -> q h two n",
                                            two=2, n=32)[:, :, 0, :],
                in0=pool[3].ap()[0:sr_, 0:256].rearrange("q (h n) -> q h n", n=32),
                in1=msk[0:sr_, 32 * i_:32 * (i_ + 1)].unsqueeze(1).broadcast_to(
                    [sr_, NH // 2, 32]), op=ALU.mult),
                wait=[tP, tU[4]] + p.wr(f"pevac{i % 2}"), inc="sV")
            tpe1 = p.op("vector", lambda e, i_=i, pe_=pe, sr_=sr: e.tensor_tensor(
                out=pe_[0:sr_, :].rearrange("q (h two n) -> q h two n",
                                            two=2, n=32)[:, :, 1, :],
                in0=pool[4].ap()[0:sr_, 0:256].rearrange("q (h n) -> q h n", n=32),
                in1=msk[0:sr_, 32 * i_:32 * (i_ + 1)].unsqueeze(1).broadcast_to(
                    [sr_, NH // 2, 32]), op=ALU.mult),
                wait=[tP], inc="sV")
            p.use["pl3"] = [tpe0]
            p.use["pl4"] = [tpe1]
            tpe = [tpe0, tpe1]
            if sub < 4.4:
                if i == NSUB - 1:
                    gate()
                    return
                continue
            tpv = None
            for h in range(NH):
                pb, slh = (h % 2) * 64, h // 2
                tpv = p.op("tensor", lambda e, h_=h, pb_=pb, s_=slh, i_=i, pe_=pe,
                           sr_=sr: e.matmul(
                    outp[pb_:pb_ + 64,
                         128 * s_ + 32 * i_:128 * s_ + 32 * (i_ + 1)],
                    vtm[pr][0:sr_, 64 * h_:64 * (h_ + 1)],
                    pe_[0:sr_, 32 * h_:32 * (h_ + 1)],
                    start=False, stop=(i_ == NSUB - 1 and h_ == NH - 1),
                    skip_group_check=True),
                    wait=([tpe, tvp, tok_pv] if h == 0 else ()),
                    inc="sT" if h == NH - 1 else None)
            tok_pv = tpv
            p.rd(f"pevac{i % 2}", tpv)
            p.rd(f"vtm{pr}", tpv)

        tyev = p.op("vector", lambda e: e.tensor_copy(
            out=yfm[pr][:, :], in_=outp[:, :]),
            wait=[tok_pv] + p.wr(f"yfm{pr}"), inc="sV")
        p.use["outp"] = [tyev]
        if dbg:
            tdd = p.dma("sync", yfmd_d[L * c:L * (c + 1), :], yfm[pr][:, :],
                        wait=[tyev], q="qNO")
            p.rd(f"yfm{pr}", tdd)
        if sub < 5:
            gate()
            return

        tSa = None
        for h in range(NH):
            tSa = p.op("tensor", lambda e, h_=h: e.matmul(
                sadd[(h_ % 2) * 64:(h_ % 2) * 64 + 64,
                     64 * (h_ // 2):64 * (h_ // 2) + 64],
                khtm[pr][:, 64 * h_:64 * (h_ + 1)],
                vtm[pr][:, 64 * h_:64 * (h_ + 1)],
                start=True, stop=True, skip_group_check=True),
                wait=([tkh, tvp] + p.wr("sadd")) if h == 0 else (),
                inc="sT" if h == NH - 1 else None)
        p.rd(f"khtm{pr}", tSa)
        p.rd(f"vtm{pr}", tSa)
        tds = None
        for sl in range(KSL):
            tds = p.op("gpsimd", lambda e, s=sl, S_=Scur: e.tensor_mul(
                out=Sd[:, 64 * s:64 * (s + 1)], in0=S_[:, 64 * s:64 * (s + 1)],
                in1=eslice(Eb[pr], s, DELTA_C,
                           DELTA_C + 1).broadcast_to([128, 64])),
                wait=([tEdone, tok_S] + p.wr("Sd")) if sl == 0 else (),
                inc="sP" if sl == KSL - 1 else None)
        p.rd(f"Sbuf{0 if pr == 0 else 1}", tds)
        p.rd(f"Eb{pr}", tds)
        tS2 = p.op("vector", lambda e, S_=Snxt: e.tensor_add(
            out=S_[:, :], in0=Sd[:, :], in1=sadd[:, :]),
            wait=[tSa, tds] + p.wr(f"Sbuf{1 if pr == 0 else 0}"), inc="sV")
        p.use["Sd"] = [tS2]
        p.use["sadd"] = [tS2]
        tbd0 = p.op("vector", lambda e, S_=Snxt: e.tensor_copy(
            out=S2[0:64, :].rearrange("q (s n) -> q s n", n=128)[:, :, 0:64],
            in_=S_[0:64, :].rearrange("q (s n) -> q s n", n=64)),
            wait=[tS2] + p.wr("S2buf"), inc="sV")
        tbd1 = p.op("vector", lambda e, S_=Snxt: e.tensor_copy(
            out=S2[64:128, :].rearrange("q (s n) -> q s n", n=128)[:, :, 64:128],
            in_=S_[64:128, :].rearrange("q (s n) -> q s n", n=64)),
            wait=[tS2], inc="sV")
        tok_S = [tbd0, tbd1]
        if sub < 6:
            tok_S_box[0] = tok_S
            gate()
            return

        if sub < 6:
            tok_S_box[0] = tok_S
            gate()
            return
        att = {}
        for hf in range(2):
            ps = pool[1 + hf]
            tmm = mm_group(ps[:, :],
                           [yfm[pr][:, 128 * k:128 * (k + 1)] for k in range(KSL)],
                           [wO[:, D * k + 512 * hf: D * k + 512 * (hf + 1)]
                            for k in range(KSL)],
                           wait=[tyev, tWo] + p.wr(f"pl{1 + hf}"))
            tev = p.op("scalar", lambda e, ps_=ps, h=hf: e.activation(
                out=scr[h][:, 0:512], in_=ps_[:, :], func=AF.Identity,
                accum_out=stats[:, sc + 13 + h:sc + 14 + h]),
                wait=[tmm] + p.wr(f"scr{hf}") + p.wr(f"st{sc + 13 + hf}"),
                inc="sS")
            p.use[f"pl{1 + hf}"] = [tev]
            p.use[f"scr{hf}"] = [tev]
            tsqr = p.op("scalar", lambda e, h=hf: e.activation(
                out=sqd[:, 512 * h:512 * (h + 1)], in_=scr[h][:, 0:512],
                func=AF.Square, accum_out=stats[:, sc + 11 + h:sc + 12 + h]),
                wait=[tev] + p.wr("sqd") + p.wr(f"st{sc + 11 + hf}"), inc="sS")
            p.use["sqd"] = p.use.get("sqd", []) + [tsqr]
            att[hf] = (tmm, tsqr, tev, ps)
            p.rd(f"yfm{pr}", tmm)
            if sub < 6.15 and hf == 1:
                tok_S_box[0] = tok_S
                gate()
                return
        if sub < 6.25:
            tok_S_box[0] = tok_S
            gate()
            return
        u1 = p.op("vector", lambda e: e.tensor_add(
            out=stats[:, sc + 15:sc + 16], in0=stats[:, sc + 11:sc + 12],
            in1=stats[:, sc + 12:sc + 13]), wait=[att[0][1], att[1][1]], inc="sV")
        u2 = p.op("vector", lambda e: e.tensor_add(
            out=stats[:, sc + 16:sc + 17], in0=stats[:, sc + 13:sc + 14],
            in1=stats[:, sc + 14:sc + 15]), wait=[att[0][2], att[1][2]], inc="sV")
        p.rd(f"st{sc + 11}", u1)
        p.rd(f"st{sc + 12}", u1)
        p.rd(f"st{sc + 13}", u2)
        p.rd(f"st{sc + 14}", u2)
        u3 = p.op("vector", lambda e: e.tensor_scalar_mul(
            out=stats[:, sc + 17:sc + 18], in0=stats[:, sc + 16:sc + 17],
            scalar1=1.0 / D), wait=[u2], inc="sV")
        u4 = p.op("vector", lambda e: e.tensor_mul(
            out=stats[:, sc + 18:sc + 19], in0=stats[:, sc + 17:sc + 18],
            in1=stats[:, sc + 17:sc + 18]), wait=[u3], inc="sV")
        u5 = p.op("vector", lambda e: e.tensor_scalar(
            out=stats[:, sc + 19:sc + 20], in0=stats[:, sc + 15:sc + 16],
            scalar1=1.0 / D, scalar2=EPS, op0=ALU.mult, op1=ALU.add),
            wait=[u1], inc="sV")
        u6 = p.op("vector", lambda e: e.tensor_sub(
            out=stats[:, sc + 20:sc + 21], in0=stats[:, sc + 19:sc + 20],
            in1=stats[:, sc + 18:sc + 19]), wait=[u4, u5], inc="sV")
        u7 = p.op("scalar", lambda e: e.activation(
            out=stats[:, sc + 21:sc + 22], in_=stats[:, sc + 20:sc + 21],
            func=AF.Ln), wait=[u6], inc="sS")
        u8 = p.op("scalar", lambda e: e.activation(
            out=stats[:, sc + 22:sc + 23], in_=stats[:, sc + 21:sc + 22],
            func=AF.Exp, scale=-0.5), wait=[u7], inc="sS")
        u9 = p.op("vector", lambda e: e.tensor_mul(
            out=stats[:, sc + 23:sc + 24], in0=stats[:, sc + 17:sc + 18],
            in1=stats[:, sc + 22:sc + 23]), wait=[u8], inc="sV")
        u10 = p.op("vector", lambda e: e.tensor_scalar_mul(
            out=stats[:, sc + 24:sc + 25], in0=stats[:, sc + 23:sc + 24],
            scalar1=-1.0), wait=[u9], inc="sV")
        if sub < 6.35:
            tok_S_box[0] = tok_S
            gate()
            return
        txm = None
        for hf in range(2):
            ta1 = p.op("gpsimd", lambda e, h=hf: e.tensor_mul(
                out=scr[h][:, 512:1024], in0=scr[h][:, 0:512],
                in1=stats[:, sc + 22:sc + 23].broadcast_to([128, 512])),
                wait=[u10], inc="sP")
            ta2 = p.op("gpsimd", lambda e, h=hf: e.tensor_add(
                out=scr[h][:, 512:1024], in0=scr[h][:, 512:1024],
                in1=stats[:, sc + 24:sc + 25].broadcast_to([128, 512])),
                wait=[ta1], inc="sP")
            txm = p.op("vector", lambda e, xb_=xb, h=hf: e.tensor_add(
                out=xmidb[pr][:, 512 * h:512 * (h + 1)],
                in0=scr[h][:, 512:1024],
                in1=xb_[:, 512 * h:512 * (h + 1)]),
                wait=[ta2] + (p.wr(f"xmidb{pr}") if hf == 0 else []), inc="sV")
            p.rd(f"scr{hf}", txm)
            p.use[f"scr{hf}"] = p.use.get(f"scr{hf}", []) + [ta2]
            p.rd(f"xbuf{c % 2}", txm)
        tdx = p.dma("sync", xmid_d[L * c:L * (c + 1), :], xmidb[pr][:, :],
                    wait=[txm], q="qMO")
        p.rd(f"xmidb{pr}", tdx)
        if sub < 7:
            tok_S_box[0] = tok_S
            gate()
            return
        tsq3 = p.op("scalar", lambda e: e.activation(
            out=sqd[:, :], in_=xmidb[pr][:, :], func=AF.Square,
            accum_out=stats[:, sc + 25:sc + 26]),
            wait=[txm] + p.wr("sqd"), inc="sS")
        p.use["sqd"] = [tsq3]
        tsm3 = p.op("vector", lambda e: e.reduce_sum(
            out=stats[:, sc + 26:sc + 27], in_=xmidb[pr][:, :], axis=AXX),
            wait=[txm], inc="sV")
        v1 = p.op("vector", lambda e, c_=c: e.tensor_scalar_mul(
            out=m3c[:, 3 * c_:3 * c_ + 1], in0=stats[:, sc + 26:sc + 27],
            scalar1=1.0 / D), wait=[tsm3], inc="sV")
        v2 = p.op("vector", lambda e, c_=c: e.tensor_mul(
            out=stats[:, sc + 27:sc + 28], in0=m3c[:, 3 * c_:3 * c_ + 1],
            in1=m3c[:, 3 * c_:3 * c_ + 1]), wait=[v1], inc="sV")
        v3 = p.op("vector", lambda e: e.tensor_scalar_mul(
            out=stats[:, sc + 28:sc + 29], in0=stats[:, sc + 25:sc + 26],
            scalar1=1.0 / D), wait=[tsq3], inc="sV")
        v4 = p.op("vector", lambda e: e.tensor_sub(
            out=stats[:, sc + 29:sc + 30], in0=stats[:, sc + 28:sc + 29],
            in1=stats[:, sc + 27:sc + 28]), wait=[v2, v3], inc="sV")
        v5 = p.op("vector", lambda e: e.tensor_scalar_add(
            out=stats[:, sc + 30:sc + 31], in0=stats[:, sc + 29:sc + 30],
            scalar1=EPS), wait=[v4], inc="sV")
        v6 = p.op("scalar", lambda e: e.activation(
            out=stats[:, sc + 31:sc + 32], in_=stats[:, sc + 30:sc + 31],
            func=AF.Ln), wait=[v5], inc="sS")
        v7 = p.op("scalar", lambda e, c_=c: e.activation(
            out=m3c[:, 3 * c_ + 1:3 * c_ + 2], in_=stats[:, sc + 31:sc + 32],
            func=AF.Exp, scale=-0.5), wait=[v6], inc="sS")
        v8 = p.op("vector", lambda e, c_=c: e.tensor_mul(
            out=stats[:, sc + 32:sc + 33], in0=m3c[:, 3 * c_:3 * c_ + 1],
            in1=m3c[:, 3 * c_ + 1:3 * c_ + 2]), wait=[v7], inc="sV")
        v9 = p.op("vector", lambda e, c_=c: e.tensor_scalar_mul(
            out=m3c[:, 3 * c_ + 2:3 * c_ + 3], in0=stats[:, sc + 32:sc + 33],
            scalar1=-1.0), wait=[v8], inc="sV")
        va = p.op("vector", lambda e: e.tensor_scalar(
            out=stats[:, sc + 33:sc + 34], in0=stats[:, sc + 29:sc + 30],
            scalar1=(1.0 + EPS), scalar2=EPS * EPS, op0=ALU.mult, op1=ALU.add),
            wait=[v4], inc="sV")
        vb = p.op("scalar", lambda e: e.activation(
            out=stats[:, sc + 34:sc + 35], in_=stats[:, sc + 33:sc + 34],
            func=AF.Ln), wait=[va], inc="sS")
        vc = p.op("scalar", lambda e: e.activation(
            out=stats[:, sc + 35:sc + 36], in_=stats[:, sc + 34:sc + 35],
            func=AF.Exp, scale=-0.5), wait=[vb], inc="sS")
        vd = p.op("vector", lambda e, c_=c: e.tensor_mul(
            out=stats[:, sc + 36:sc + 37], in0=m3c[:, 3 * c_:3 * c_ + 1],
            in1=stats[:, sc + 35:sc + 36]), wait=[vc, v1], inc="sV")
        ve_ = p.op("vector", lambda e: e.tensor_scalar_mul(
            out=stats[:, sc + 37:sc + 38], in0=stats[:, sc + 36:sc + 37],
            scalar1=-1.0), wait=[vd], inc="sV")
        tx2g = p.op("gpsimd", lambda e: e.tensor_mul(
            out=scr[1][:, :], in0=xmidb[pr][:, :],
            in1=stats[:, sc + 35:sc + 36].broadcast_to([128, D])),
            wait=[ve_] + p.wr("scr1"), inc="sP")
        p.use["scr1"] = [tx2g]
        txn2 = p.op("gpsimd", lambda e: e.tensor_add(
            out=xn2b[pr][:, :], in0=scr[1][:, :],
            in1=stats[:, sc + 37:sc + 38].broadcast_to([128, D])),
            wait=[tx2g] + p.wr(f"xn2b{pr}"), inc="sP")
        p.rd("scr1", txn2)
        p.rd(f"xmidb{pr}", txn2)
        if sub < 8:
            tok_S_box[0] = tok_S
            gate()
            return
        bt3 = pool[0].ap().bitcast(BF16)
        ttr3 = None
        for sl in range(KSL):
            ttr3 = p.op("tensor", lambda e, s=sl: e.transpose(
                bt3[:, 128 * s:128 * (s + 1)], xn2b[pr][:, 128 * s:128 * (s + 1)],
                idn[:, :]),
                wait=([txn2] + p.wr("pl0")) if sl == 0 else (),
                inc="sT" if sl == KSL - 1 else None)
        p.rd(f"xn2b{pr}", ttr3)
        txn2f = p.op("vector", lambda e: e.tensor_copy(
            out=xn2f[pr][:, :], in_=bt3[:, :]),
            wait=[ttr3] + p.wr(f"xn2f{pr}"), inc="sV")
        p.use["pl0"] = [txn2f]
        tdn = p.dma("sync", xn2_d[L * c:L * (c + 1), :], xn2f[pr][:, :],
                    wait=[txn2f], q="qNO")
        p.rd(f"xn2f{pr}", tdn)
        if dbg and c == 0:
            p.dma("sync", scd_d[:, 8:16], g2col[:, :], wait=[tdn], q="qNO")
        if dbg:
            p.dma("sync", arnc_d[128 * c:128 * (c + 1), :],
                  arena[:, :].bitcast(F32)[:, 0:512],
                  wait=[tdn], q="qNO")
        if dbg and c == NCH - 1:
            p.dma("sync", scd_d[:, 16:24], g2col[:, :], wait=[tdn], q="qNO")
            p.dma("sync", arn1_d[:, :], arena[:, :].bitcast(F32)[:, 0:512],
                  wait=[tdn], q="qNO")
        tok_S_box[0] = tok_S

    if stage >= 2:
        for c in range(NCH):
            emit_attn_chunk(c)

    attn_done = [("sT", p.cnt["sT"]), ("sV", p.cnt["sV"]), ("sS", p.cnt["sS"]),
                 ("sP", p.cnt["sP"]), ("qMO", p.cnt.get("qMO", 0)),
                 ("qNO", p.cnt.get("qNO", 0)), ("qX", p.cnt.get("qX", 0))]
    attn_done = [t for t in attn_done if t[1] > 0]

    f32v = fw1[:, :].bitcast(F32)
    ts1 = p.dma("sync", f32v[0:8, 0:4096], s1, wait=attn_done, q="qC0")
    tc1 = p.op("vector", lambda e: e.tensor_copy(
        out=fw2[0:8, 0:4096], in_=f32v[0:8, 0:4096]),
        wait=[ts1] + attn_done, inc="sV")
    td1 = p.dma("sync", s1b_d, fw2[0:8, 0:4096], wait=[tc1], q="qR")
    ts2 = p.dma("sync", f32v[0:32, 4096:5120], s2, wait=attn_done, q="qC1")
    tc2 = p.op("vector", lambda e: e.tensor_copy(
        out=fw2[0:32, 4096:5120], in_=f32v[0:32, 4096:5120]),
        wait=[ts2] + attn_done, inc="sV")
    td2 = p.dma("sync", s2b_d, fw2[0:32, 4096:5120], wait=[tc2], q="qR")
    p.use["fw1st"] = [tc1, tc2]
    p.use["fw2st"] = [td1, td2]

    tb1r = p.dma("gpsimd", b1row[0:1, :], b1[None, :], wait=attn_done, q="qRB")
    tb2r = p.dma("sync", rowtmp[0:1, 0:1024], b2f[None, :], wait=attn_done, q="qC3")
    tcb2 = p.op("vector", lambda e: e.tensor_scalar_mul(
        out=cb2row[0:1, :], in0=rowtmp[0:1, 0:1024], scalar1=0.5),
        wait=[tb2r] + attn_done, inc="sV")
    p.use["rowtmp"] = [tcb2]

    def dequant_w1(colrange, extra_wait):
        tws = []
        for sl in range(KSL):
            b_ = sl % 2
            tq = p.dma("sync", qst[b_][:, 0:2048],
                       q1[128 * sl:128 * (sl + 1), colrange[0]:colrange[1]],
                       wait=p.wr(f"qst{b_}") + list(extra_wait), q=f"qQ{b_}")
            tsb = p.dma("sync", sbc[b_][:, 0:2048],
                        s1b_d[sl:sl + 1, colrange[0]:colrange[1]].broadcast_to(
                            [128, 2048]),
                        wait=[td1] + p.wr(f"sbc{b_}") + list(extra_wait), q=f"qSB{b_}")
            tw = None
            for half in range(2):
                sc_ = scr[half]
                tcs = p.op("gpsimd", lambda e, s=sl, bb=b_, h=half, sc2=sc_:
                           e.tensor_mul(
                    out=sc2[:, 0:1024],
                    in0=sbc[bb][:, 1024 * h:1024 * (h + 1)],
                    in1=g2col[:, s:s + 1].broadcast_to([128, 1024])),
                    wait=[tsb, tgb] + p.wr(f"scr{half}"), inc="sP")
                p.use[f"scr{half}"] = [tcs]
                tw = p.op("vector", lambda e, s=sl, bb=b_, h=half, sc2=sc_:
                          e.scalar_tensor_tensor(
                    out=fw1[:, 2048 * s + 1024 * h:2048 * s + 1024 * (h + 1)],
                    in0=qst[bb][:, 1024 * h:1024 * (h + 1)],
                    scalar=-128.0, op0=ALU.add,
                    in1=sc2[:, 0:1024], op1=ALU.mult),
                    wait=[tcs, tq] + (p.wr("fw1st") if half == 0 else []) +
                         ([] if sl > 1 else list(extra_wait)), inc="sV")
                p.rd(f"qst{b_}", tw)
                p.rd(f"sbc{b_}", tw)
                p.rd(f"scr{half}", tw)
                if dbg and colrange[0] == 0 and sl == KSL - 1 and half == 1:
                    td_ = p.dma("sync", qd_d[:, :], qst[b_][:, 0:2048],
                                wait=[tq], q="qNO")
                    p.rd(f"qst{b_}", td_)
            tws.append(tw)
        return tws

    def dequant_w2(rowrange, extra_wait):
        tws = []
        for sl in range(16):
            b_ = sl % 2
            gsl = rowrange[0] // 128 + sl
            tq = p.dma("sync", qst[b_][:, 0:1024],
                       q2[rowrange[0] + 128 * sl: rowrange[0] + 128 * (sl + 1), :],
                       wait=p.wr(f"qst{b_}") + list(extra_wait), q=f"qQ{b_}")
            tsb = p.dma("sync", sbc[b_][:, 0:1024],
                        s2b_d[gsl:gsl + 1, :].broadcast_to([128, 1024]),
                        wait=[td2] + p.wr(f"sbc{b_}") + list(extra_wait), q=f"qSB{b_}")
            sc_ = scr[b_]
            tts = p.op("vector", lambda e, bb=b_, sc2=sc_: e.tensor_scalar(
                out=sc2[:, 0:1024], in0=qst[bb][:, 0:1024],
                scalar1=-128.0, scalar2=0.5, op0=ALU.add, op1=ALU.mult),
                wait=[tq] + p.wr(f"scr{b_}") + list(extra_wait), inc="sV")
            p.rd(f"qst{b_}", tts)
            p.use[f"scr{b_}"] = [tts]
            tw = p.op("vector", lambda e, s=sl, bb=b_, sc2=sc_: e.tensor_mul(
                out=fw2[:, D * s:D * (s + 1)], in0=sc2[:, 0:1024],
                in1=sbc[bb][:, 0:1024]),
                wait=[tts, tsb] + p.wr("fw2st") +
                     ([] if sl > 1 else list(extra_wait)), inc="sV")
            p.rd(f"sbc{b_}", tw)
            p.rd(f"scr{b_}", tw)
            tws.append(tw)
        return tws

    def beta_row(dst_off, w1toks, extra_wait):
        tvs = []
        for ns in range(4):
            ps = pool[2 + (ns % 2)]
            tmm = mm_group(ps[0:1, :],
                           [bg2col_b[:, k:k + 1] for k in range(KSL)],
                           [fw1[:, 2048 * k + 512 * ns: 2048 * k + 512 * (ns + 1)]
                            for k in range(KSL)],
                           wait=[tbg2, w1toks[-1]] + p.wr(f"pl{2 + ns % 2}") +
                                list(extra_wait))
            tv = p.op("vector", lambda e, n=ns, ps_=ps, o=dst_off:
                      e.scalar_tensor_tensor(
                out=rowtmp[0:1, 512 * n:512 * (n + 1)], in0=ps_[0:1, :], scalar=1.0,
                in1=b1row[0:1, o + 512 * n:o + 512 * (n + 1)],
                op0=ALU.mult, op1=ALU.add),
                wait=[tmm, tb1r] + p.wr("rowtmp") + list(extra_wait), inc="sV")
            p.use[f"pl{2 + ns % 2}"] = [tv]
            tvs.append(tv)
        td = p.dma("sync", cb1_d[0:1, dst_off:dst_off + 2048],
                   rowtmp[0:1, 0:2048], wait=tvs, q="qR")
        p.rd("rowtmp", td)
        tcvt = p.dma("gpsimd", cb1r[0:1, dst_off:dst_off + 2048],
                     cb1_d[0:1, dst_off:dst_off + 2048], wait=[td], q="qRB")
        return tcvt

    if stage < 3:
        p.op("gpsimd", lambda e: e.engine_nop(),
             wait=attn_done + [("qW0", p.cnt.get("qW0", 0)),
                               ("qC0", p.cnt.get("qC0", 0))])
        with nc.Block() as block:
            p.finish(block)
        return nc
    tw1a = dequant_w1((0, 2048), attn_done)
    tw2a = dequant_w2((0, 2048), attn_done)
    tcb1a = beta_row(0, tw1a, [])
    if dbg:
        for j_ in range(4):
            td_ = p.dma("sync", f1d_d[:, 4096 * j_:4096 * (j_ + 1)],
                        fw1[:, 4096 * j_:4096 * (j_ + 1)],
                        wait=list(tw1a), q="qNO")
            p.rd("fw1st", td_)
        td_ = p.dma("sync", cbr_d[0:1, :], cb1r[0:1, :], wait=[tcb1a],
                    q="qNO")
        p.rd("cb1rb", td_)
        td_ = p.dma("sync", cbd_d[:, 0:8], g2col[:, :], wait=[tcb1a],
                    q="qNO")
        td_ = p.dma("sync", cbd_d[:, 8:16], g2bias[:, :], wait=[td_],
                    q="qNO")
        td_ = p.dma("sync", cbd_d[:, 16:24], gcol[:, :], wait=[td_],
                    q="qNO")
        td_ = p.dma("sync", zbd_d[0:1, :], zb[0:1, 0:512], wait=[tzb],
                    q="qNO")

    def ffn_pass(passb, w1toks, w2toks, cbtok):
        def emit_group(g0, gs):
            gi = (g0 // 4) % 2
            GT = 128 * gs
            txns = []
            for j in range(gs):
                c = g0 + j
                tx = p.dma("sync",
                           fxn2[gi][:, :].rearrange("q (s t) -> q s t", s=KSL)
                               [:, :, 128 * j:128 * (j + 1)],
                           xn2_d[L * c:L * (c + 1), :].rearrange(
                               "q (s t) -> q s t", s=KSL),
                           wait=(p.wr(f"fxn2{gi}") if j == 0 else []) +
                                [w1toks[-1], w2toks[-1]], q="qNI")
                txns.append(tx)
            tsl_last = None
            for nch in range(16):
                ps = pool[nch % 4]
                tmm = mm_group(ps[:, 0:GT],
                               [fw1[:, 2048 * k + 128 * nch:
                                    2048 * k + 128 * (nch + 1)]
                                for k in range(KSL)],
                               [fxn2[gi][:, 512 * k:512 * k + GT]
                                for k in range(KSL)],
                               wait=txns + [cbtok, w1toks[-1], tzb] +
                                    p.wr(f"pl{nch % 4}") +
                                    (p.wr("fh") if nch == 0 else []),
                               extra=(cb1r[0:1, 2048 * passb + 128 * nch:
                                           2048 * passb + 128 * (nch + 1)],
                                      zb[0:1, 0:GT]))
                tsl_last = p.op("scalar", lambda e, n=nch, ps_=ps, G_=GT:
                                e.activation(
                    out=fh[:, 512 * n:512 * n + G_], in_=ps_[:, 0:G_],
                    func=AF.Silu),
                    wait=[tmm], inc="sS")
                p.use[f"pl{nch % 4}"] = [tsl_last]
            p.rd(f"fxn2{gi}", tsl_last)

            for j in range(gs):
                c = g0 + j
                b_ = c % 2
                q_ = c % 4
                tz = []
                for hf in range(2):
                    extra = (ones[:, :], cb2row[0:1, 512 * hf:512 * (hf + 1)]) \
                        if passb else None
                    tmm = mm_group(outp[:, 512 * hf:512 * (hf + 1)],
                                   [fh[:, 512 * k + 128 * j:512 * k + 128 * (j + 1)]
                                    for k in range(16)],
                                   [fw2[:, D * k + 512 * hf: D * k + 512 * (hf + 1)]
                                    for k in range(16)],
                                   wait=[tsl_last, tcb2, w2toks[-1], tone] +
                                        p.wr(f"fop{hf}"),
                                   extra=extra)
                    tz.append(tmm)
                p.rd("fh", tz[-1])
                if not passb:
                    tev = p.op("vector", lambda e, bb=b_: e.tensor_copy(
                        out=ft[bb][:, :], in_=outp[:, :]),
                        wait=tz + p.wr(f"ft{b_}"), inc="sV")
                    p.use["fop0"] = [tev]
                    p.use["fop1"] = [tev]
                    tdz = p.dma("sync", z2a_d[L * c:L * (c + 1), :], ft[b_][:, :],
                                wait=[tev], q="qZO")
                    p.rd(f"ft{b_}", tdz)
                else:
                    tza = p.dma("sync", fz2a[b_][:, :], z2a_d[L * c:L * (c + 1), :],
                                wait=p.wr(f"fz2a{b_}"), q="qZI")
                    txd = p.dma("sync", fxmid[q_][:, :],
                                xmid_d[L * c:L * (c + 1), :],
                                wait=p.wr(f"fxmid{q_}"), q="qMI")
                    tt1 = p.op("vector", lambda e, bb=b_, q2_=q_: e.tensor_add(
                        out=ft[q2_][:, :], in0=outp[:, :], in1=fz2a[bb][:, :]),
                        wait=tz + [tza] + p.wr(f"ft{q_}"), inc="sV")
                    p.use["fop0"] = [tt1]
                    p.use["fop1"] = [tt1]
                    p.rd(f"fz2a{b_}", tt1)
                    tp1 = p.op("gpsimd", lambda e, q2_=q_, c_=c: e.tensor_mul(
                        out=fo[c_ % 2][:, :], in0=fxmid[q2_][:, :],
                        in1=m3c[:, 3 * c_ + 1:3 * c_ + 2]
                            .broadcast_to([128, D])),
                        wait=[txd] + p.wr(f"fo{c % 2}"), inc="sP")
                    p.rd(f"fxmid{q_}", tp1)
                    tp2 = p.op("gpsimd", lambda e, c_=c: e.tensor_add(
                        out=fo[c_ % 2][:, :], in0=fo[c_ % 2][:, :],
                        in1=m3c[:, 3 * c_ + 2:3 * c_ + 3]
                            .broadcast_to([128, D])),
                        wait=[tp1], inc="sP")
                    tt2 = p.op("vector", lambda e, q2_=q_, c_=c: e.tensor_add(
                        out=ft[q2_][:, :], in0=ft[q2_][:, :],
                        in1=fo[c_ % 2][:, :]),
                        wait=[tt1, tp2], inc="sV")
                    p.rd(f"fo{c % 2}", tt2)
                    sc4 = 32 + 48 * (c % 2)
                    tq4 = p.op("scalar", lambda e, q2_=q_, s4=sc4: e.activation(
                        out=fsq[:, :], in_=ft[q2_][:, :], func=AF.Square,
                        accum_out=stats[:, s4:s4 + 1]),
                        wait=[tt2] + p.wr("fsq") + p.wr(f"fst{sc4}"), inc="sS")
                    p.use["fsq"] = [tq4]
                    tr4 = p.op("vector", lambda e, q2_=q_, s4=sc4: e.reduce_sum(
                        out=stats[:, s4 + 1:s4 + 2], in_=ft[q2_][:, :], axis=AXX),
                        wait=[tt2] + p.wr(f"fst{sc4}b"), inc="sV")
                    w1_ = p.op("vector", lambda e, s4=sc4: e.tensor_scalar_mul(
                        out=stats[:, s4 + 2:s4 + 3], in0=stats[:, s4 + 1:s4 + 2],
                        scalar1=1.0 / D), wait=[tr4], inc="sV")
                    w2_ = p.op("vector", lambda e, s4=sc4: e.tensor_mul(
                        out=stats[:, s4 + 3:s4 + 4], in0=stats[:, s4 + 2:s4 + 3],
                        in1=stats[:, s4 + 2:s4 + 3]), wait=[w1_], inc="sV")
                    w3_ = p.op("vector", lambda e, s4=sc4: e.tensor_scalar(
                        out=stats[:, s4 + 4:s4 + 5], in0=stats[:, s4:s4 + 1],
                        scalar1=1.0 / D, scalar2=EPS, op0=ALU.mult, op1=ALU.add),
                        wait=[tq4], inc="sV")
                    w4_ = p.op("vector", lambda e, s4=sc4: e.tensor_sub(
                        out=stats[:, s4 + 5:s4 + 6], in0=stats[:, s4 + 4:s4 + 5],
                        in1=stats[:, s4 + 3:s4 + 4]), wait=[w2_, w3_], inc="sV")
                    p.rd(f"fst{sc4}", w3_)
                    p.rd(f"fst{sc4}b", w1_)
                    w5_ = p.op("vector", lambda e, q2_=q_, s4=sc4: e.tensor_copy(
                        out=stats[:, 96 + 2 * q2_:97 + 2 * q2_],
                        in_=stats[:, s4 + 5:s4 + 6]),
                        wait=[w4_] + p.wr(f"fq{q_}"), inc="sV")
                    w6_ = p.op("vector", lambda e, q2_=q_, s4=sc4: e.tensor_copy(
                        out=stats[:, 97 + 2 * q2_:98 + 2 * q2_],
                        in_=stats[:, s4 + 2:s4 + 3]),
                        wait=[w1_, w5_], inc="sV")
                    p.use[f"fvq{q_}"] = [w6_]

        def emit_epi(c):
                q_ = c % 4
                vtok = p.wr(f"fvq{q_}")
                e1 = p.op("scalar", lambda e, q2_=q_: e.activation(
                    out=stats[:, 104 + 4 * q2_:105 + 4 * q2_],
                    in_=stats[:, 96 + 2 * q2_:97 + 2 * q2_], func=AF.Ln),
                    wait=vtok, inc="sS")
                e2 = p.op("scalar", lambda e, q2_=q_: e.activation(
                    out=stats[:, 105 + 4 * q2_:106 + 4 * q2_],
                    in_=stats[:, 104 + 4 * q2_:105 + 4 * q2_], func=AF.Exp,
                    scale=-0.5), wait=[e1], inc="sS")
                e3 = p.op("vector", lambda e, q2_=q_: e.tensor_mul(
                    out=stats[:, 106 + 4 * q2_:107 + 4 * q2_],
                    in0=stats[:, 97 + 2 * q2_:98 + 2 * q2_],
                    in1=stats[:, 105 + 4 * q2_:106 + 4 * q2_]), wait=[e2], inc="sV")
                e4 = p.op("vector", lambda e, q2_=q_: e.tensor_scalar_mul(
                    out=stats[:, 107 + 4 * q2_:108 + 4 * q2_],
                    in0=stats[:, 106 + 4 * q2_:107 + 4 * q2_], scalar1=-1.0),
                    wait=[e3], inc="sV")
                p.rd(f"fq{q_}", e3)
                to1 = p.op("gpsimd", lambda e, q2_=q_, c_=c: e.tensor_mul(
                    out=fo[c_ % 2][:, :], in0=ft[q2_][:, :],
                    in1=stats[:, 105 + 4 * q2_:106 + 4 * q2_]
                        .broadcast_to([128, D])),
                    wait=[e4] + p.wr(f"fo{c % 2}"), inc="sP")
                p.rd(f"ft{q_}", to1)
                to2 = p.op("gpsimd", lambda e, q2_=q_, c_=c: e.tensor_add(
                    out=fo[c_ % 2][:, :], in0=fo[c_ % 2][:, :],
                    in1=stats[:, 107 + 4 * q2_:108 + 4 * q2_]
                        .broadcast_to([128, D])),
                    wait=[to1], inc="sP")
                ob = p.op("vector", lambda e, q2_=q_, c_=c: e.tensor_add(
                    out=fo[c_ % 2][:, :], in0=fo[c_ % 2][:, :],
                    in1=fxmid[q2_][:, :]),
                    wait=[to2], inc="sV")
                p.rd(f"fxmid{q_}", ob)
                tdo = p.dma("sync", out[L * c:L * (c + 1), :], fo[c % 2][:, :],
                            q="sO" if c % 2 == 0 else "sO2", wait=[ob])
                p.rd(f"fo{c % 2}", tdo)

        for g0 in range(0, NCH, 4):
            gs = min(4, NCH - g0)
            emit_group(g0, gs)
            if passb:
                for c in range(g0, g0 + gs):
                    emit_epi(c)

    if stage < 4:
        p.op("gpsimd", lambda e: e.engine_nop(),
             wait=[("sT", p.cnt["sT"]), ("sV", p.cnt["sV"]), ("sS", p.cnt["sS"]),
                   ("qC0", p.cnt.get("qC0", 0))])
        with nc.Block() as block:
            p.finish(block)
        return nc
    ffn_pass(False, tw1a, tw2a, tcb1a)

    passA = [("sT", p.cnt["sT"]), ("sS", p.cnt["sS"]), ("sV", p.cnt["sV"]),
             ("qZO", p.cnt["qZO"]), ("qNI", p.cnt["qNI"])]
    tw1b = dequant_w1((2048, 4096), passA)
    tw2b = dequant_w2((2048, 4096), passA)
    tcb1b = beta_row(2048, tw1b, [])

    ffn_pass(True, tw1b, tw2b, tcb1b)

    p.op("gpsimd", lambda e: e.engine_nop(),
         wait=[(q_, p.cnt[q_]) for q_ in ("sO", "sO2") if q_ in p.cnt])

    with nc.Block() as block:
        p.finish(block)
    return nc


def kernel(L0_attn_x, L0_attn_ln_g, L0_attn_ln_b, L0_attn_Wx, L0_attn_Ww,
           L0_attn_bw, L0_attn_Wk, L0_attn_Wv, L0_attn_Wr, L0_attn_Wo,
           L0_ffn_ln_gamma, L0_ffn_ln_beta, L0_ffn_fc1_q, L0_ffn_fc1_scales,
           L0_ffn_fc1_bias, L0_ffn_fc2_q, L0_ffn_fc2_scales, L0_ffn_fc2_bias):
    consts = build_constants()
    nc = build(T=2048)
    base = dict(
        ln_g=np.ascontiguousarray(L0_attn_ln_g),
        ln_b=np.ascontiguousarray(L0_attn_ln_b),
        bw=np.ascontiguousarray(L0_attn_bw),
        Wx=np.ascontiguousarray(L0_attn_Wx), Ww=np.ascontiguousarray(L0_attn_Ww),
        Wk=np.ascontiguousarray(L0_attn_Wk), Wv=np.ascontiguousarray(L0_attn_Wv),
        Wr=np.ascontiguousarray(L0_attn_Wr), Wo=np.ascontiguousarray(L0_attn_Wo),
        g2=np.ascontiguousarray(L0_ffn_ln_gamma),
        b2v=np.ascontiguousarray(L0_ffn_ln_beta),
        q1=np.ascontiguousarray(L0_ffn_fc1_q),
        s1=np.ascontiguousarray(L0_ffn_fc1_scales),
        b1=np.ascontiguousarray(L0_ffn_fc1_bias),
        q2=np.ascontiguousarray(L0_ffn_fc2_q),
        s2=np.ascontiguousarray(L0_ffn_fc2_scales),
        b2f=np.ascontiguousarray(L0_ffn_fc2_bias),
        **consts,
    )
    in_maps = [dict(base, x=np.ascontiguousarray(L0_attn_x[b])) for b in range(8)]
    res = run_bass_kernel_spmd(nc, in_maps, core_ids=list(range(8)))
    return np.stack([r["out"] for r in res.results])


# version 12
# speedup vs baseline: 1.0543x; 1.0136x over previous
import numpy as np
import ml_dtypes
import concourse.bass as bass
import concourse.mybir as mybir
from concourse.bass_utils import run_bass_kernel_spmd

F32 = mybir.dt.float32
F16 = mybir.dt.float16
BF16 = mybir.dt.bfloat16
I32 = mybir.dt.int32
AF = mybir.ActivationFunctionType
ALU = mybir.AluOpType
AXX = mybir.AxisListType.X

D = 1024
NH = 16
L = 128
W = 32
NSUB = 4
EPS = 1e-5
KSL = 8

KT_OFF = [0, 32, 96, 192]
KT_SZ = [32, 64, 96, 128]
NA = 321
DELTA_C = 320
NB = 256
ECOLS = NA + NB


def build_constants():
    ind = np.triu(np.ones((L, L), np.float32))
    UA = np.zeros((L, NA), np.float32)
    for i in range(NSUB):
        ri = i * W - 1
        for s in range(W * (i + 1)):
            UA[:, KT_OFF[i] + s] = ind[:, s] - (ind[:, ri] if ri >= 0 else 0.0)
    UA[:, DELTA_C] = -1.0
    UB = np.zeros((L, NB), np.float32)
    for i in range(NSUB):
        ri = i * W - 1
        for a in range(W):
            t = i * W + a
            UB[:, 32 * i + a] = (ind[:, ri] if ri >= 0 else 0.0) - ind[:, t]
    for t in range(L):
        UB[:, 128 + t] = -ind[:, t]
    UBr = np.zeros((L, NB), np.float32)
    for i in range(NSUB):
        for a in range(W):
            UBr[i * W + a, 32 * i + a] = 1.0
    for t in range(L):
        UBr[t, 128 + t] = 1.0
    UKH = np.zeros((L, L), np.float32)
    for s in range(L):
        UKH[:, s] = ind[:, s] - ind[:, 127]
    MSK = np.zeros((L, NSUB * W), np.float32)
    for i in range(NSUB):
        MSK[: i * W, 32 * i: 32 * (i + 1)] = 1.0
        blk = (np.arange(W)[:, None] <= np.arange(W)[None, :]).astype(np.float32)
        MSK[i * W: (i + 1) * W, 32 * i: 32 * (i + 1)] = blk
    return dict(
        c_UA=UA.astype(np.float16), c_UB=UB.astype(np.float16),
        c_UBr=UBr.astype(np.float16), c_UKH=UKH.astype(np.float16),
        c_MSK=MSK.astype(ml_dtypes.bfloat16),
        c_ID=np.eye(128, dtype=np.float32).astype(ml_dtypes.bfloat16),
    )


class P:
    ENGS = ("sync", "tensor", "vector", "scalar", "gpsimd")

    def __init__(self, nc, ctx):
        self.nc = nc
        self.ctx = ctx
        self.streams = {e: [] for e in self.ENGS}
        self.h = {}
        self.cnt = {}
        self.marks = {}
        self.use = {}
        for name in ("sT", "sV", "sS", "sP"):
            self.h[name] = ctx.enter_context(nc.semaphore(name))
            self.cnt[name] = 0

    def mksem(self, name):
        if name not in self.h:
            self.h[name] = self.ctx.enter_context(self.nc.semaphore(name))
            self.cnt[name] = 0

    def op(self, eng, fn, wait=(), inc=None, inc_n=1):
        waits = []
        for tok in wait:
            if tok is None:
                continue
            if isinstance(tok, list):
                for t2 in tok:
                    if t2 is not None:
                        waits.append(t2)
                continue
            waits.append(tok)
        dedup = []
        for sem, val in waits:
            key = (eng, sem)
            if self.marks.get(key, 0) < val:
                dedup.append((sem, val))
                self.marks[key] = val
        tok_out = None
        if inc is not None:
            self.cnt[inc] += inc_n
            tok_out = (inc, self.cnt[inc])

        def thunk(e, waits=tuple(dedup), fn=fn, inc=inc, inc_n=inc_n):
            for sem, val in waits:
                e.wait_ge(self.h[sem], val)
            ins = fn(e)
            if inc is not None:
                ins.then_inc(self.h[inc], inc_n)
        self.streams[eng].append(thunk)
        return tok_out

    def dma(self, eng, out, in_, wait=(), q=None, slow=False):
        q = q or ("qS" if eng == "sync" else "qG")
        self.mksem(q)
        k = self.cnt[q] // 16 + 1
        gate = [(q, 16 * (k - 1))] if k >= 2 else []
        return self.op(eng, lambda e, o=out, i=in_, s=slow: e.dma_start(
            out=o, in_=i, allow_slow_non_contiguous=s),
            wait=list(wait) + gate, inc=q, inc_n=16)

    def rd(self, name, tok):
        self.use.setdefault(name, []).append(tok)
        return tok

    def wr(self, name):
        toks = self.use.pop(name, [])
        return toks

    def finish(self, block):
        def run(eng, thunks):
            for t in thunks:
                t(eng)
        for e in self.ENGS:
            getattr(block, e)(lambda eng, th=tuple(self.streams[e]): run(eng, th))


ALLOC_LOG = []


def build(T=2048, dbg=False, stage=4, sub=9, v604=0):
    ALLOC_LOG.clear()
    NCH = T // L
    nc = bass.Bass("TRN2", target_bir_lowering=False,
                   dynamic_dma_scratch_size=2048)
    from contextlib import ExitStack
    ctx = ExitStack()

    def din(name, shape, dt=F32):
        return nc.dram_tensor(name, shape, dt, kind="ExternalInput").ap()

    x = din("x", [T, D])
    ln_g, ln_b, bw = din("ln_g", [D]), din("ln_b", [D]), din("bw", [D])
    Wx, Ww = din("Wx", [D, D]), din("Ww", [D, D])
    Wk, Wv, Wr, Wo = (din(n, [D, D]) for n in ("Wk", "Wv", "Wr", "Wo"))
    g2, b2v = din("g2", [D]), din("b2v", [D])
    q1, s1, b1 = din("q1", [D, 4096], I32), din("s1", [8, 4096]), din("b1", [4096])
    q2, s2, b2f = din("q2", [4096, D], I32), din("s2", [32, D]), din("b2f", [D])
    c_UA, c_UB = din("c_UA", [L, NA], F16), din("c_UB", [L, NB], F16)
    c_UBr, c_UKH = din("c_UBr", [L, NB], F16), din("c_UKH", [L, L], F16)
    c_MSK, c_ID = din("c_MSK", [L, 128], BF16), din("c_ID", [128, 128], BF16)

    out = nc.dram_tensor("out", [T, D], F32, kind="ExternalOutput").ap()
    ikind = "ExternalOutput" if dbg else "Internal"
    xmid_d = nc.dram_tensor("xmid_d", [T, D], F32, kind=ikind).ap()
    xn2_d = nc.dram_tensor("xn2_d", [T, D], BF16, kind=ikind).ap()
    z2a_d = nc.dram_tensor("z2a_d", [T, D], F32, kind=ikind).ap()
    s1b_d = nc.dram_tensor("s1b_d", [8, 4096], BF16, kind=ikind).ap()
    s2b_d = nc.dram_tensor("s2b_d", [32, D], BF16, kind=ikind).ap()
    ckrow_d = nc.dram_tensor("ckrow_d", [1, D], F32, kind="Internal").ap()
    if dbg:
        scrd_d = nc.dram_tensor("scrd_d", [T, D], F32, kind="ExternalOutput").ap()
        spd_d = nc.dram_tensor("spd_d", [T, D], F16, kind="ExternalOutput").ap()
        sprd_d = nc.dram_tensor("sprd_d", [T, D], F16,
                                kind="ExternalOutput").ap()
        ebd_d = nc.dram_tensor("ebd_d", [T, 8 * ECOLS], BF16,
                               kind="ExternalOutput").ap()
        yfmd_d = nc.dram_tensor("yfmd_d", [T, D], BF16,
                                kind="ExternalOutput").ap()
        f1d_d = nc.dram_tensor("f1d_d", [128, 16384], BF16,
                               kind="ExternalOutput").ap()
        f2d_d = nc.dram_tensor("f2d_d", [128, 16384], BF16,
                               kind="ExternalOutput").ap()
        fhd_d = nc.dram_tensor("fhd_d", [128, 8192], BF16,
                               kind="ExternalOutput").ap()
        cbd_d = nc.dram_tensor("cbd_d", [128, 32], F32,
                               kind="ExternalOutput").ap()
        cbr_d = nc.dram_tensor("cbr_d", [1, 4096], BF16,
                               kind="ExternalOutput").ap()
        zbd_d = nc.dram_tensor("zbd_d", [1, 512], BF16,
                               kind="ExternalOutput").ap()
        qd_d = nc.dram_tensor("qd_d", [128, 2048], I32,
                              kind="ExternalOutput").ap()
        scd_d = nc.dram_tensor("scd_d", [128, 1024], F32,
                               kind="ExternalOutput").ap()
        arn0_d = nc.dram_tensor("arn0_d", [128, 512], F32,
                                kind="ExternalOutput").ap()
        arn1_d = nc.dram_tensor("arn1_d", [128, 512], F32,
                                kind="ExternalOutput").ap()
        arnc_d = nc.dram_tensor("arnc_d", [2048, 512], F32,
                                kind="ExternalOutput").ap()
    cb1_d = nc.dram_tensor("cb1_d", [1, 4096], F32, kind="Internal").ap()
    cvrow_d = nc.dram_tensor("cvrow_d", [1, D], F32, kind="Internal").ap()

    ARENA = (nc.sbuf_bytes_remaining - 64) & ~31
    arena = nc.alloc_sbuf_tensor("arena", [128, ARENA], mybir.dt.uint8)
    abase = nc.lookup_mloc(arena).addr
    sb_off = [abase]

    def sb(name, shape, dt):
        at = sb_off[0]
        per = int(np.prod(shape[1:])) * mybir.dt.size(dt)
        sb_off[0] = (at + per + 31) & ~31
        assert sb_off[0] <= abase + ARENA, (name, sb_off[0] - abase)
        ALLOC_LOG.append((name, at - abase, per, shape[0]))
        return nc.alloc_sbuf_tensor_at(name, shape, dt, offset=at)

    sb("scratchpad", [128, 6144], mybir.dt.uint8)
    gcol = sb("gcol", [128, KSL], F32)
    bgcol = sb("bgcol", [128, KSL], F32)
    g2col = sb("g2col", [128, KSL], F32)
    bg2col = sb("bg2col", [128, KSL], F32)
    g2bias = sb("g2bias", [128, KSL], F32)
    bgcol_b = sb("bgcol_b", [128, KSL], BF16)
    bg2col_b = sb("bg2col_b", [128, KSL], BF16)
    stats = sb("stats", [128, 128], F32)
    m3c = sb("m3c", [128, 3 * 16], F32)
    rowsA = sb("rowsA", [128, 3072], F32)
    rowsB = sb("rowsB", [128, 4096], BF16)
    rowsC = sb("rowsC", [128, 3 * 1024 + 128], BF16)
    zb = sb("zb", [128, 512], BF16)
    onef = sb("onef", [128, 1], F32)
    rowtmp = rowsA[0:1, 0:2048]
    bwrow = rowsA[0:1, 2048:3072]
    b1row = rowsB[0:1, :]
    crow_w = rowsC[0:1, 0:D]
    cb2row = rowsC[0:1, D:2 * D]
    crow_r = rowsC[0:1, 2 * D:3 * D]
    ones = rowsC[0:1, 3 * D:3 * D + 128]
    ones0 = ones

    base_big = sb_off[0]

    uA = sb("uA", [128, NA], F16)
    uB = sb("uB", [128, NB], F16)
    uBr = sb("uBr", [128, NB], F16)
    uKH = sb("uKH", [128, 128], F16)
    msk = sb("msk", [128, 128], BF16)
    idn = sb("idn", [128, 128], BF16)
    ckb = sb("ckb", [128, D], BF16)
    cvb = sb("cvb", [128, D], BF16)
    S0 = sb("S0", [128, 512], BF16)
    S1 = sb("S1", [128, 512], BF16)
    Sd = sb("Sd", [128, 512], BF16)
    S2 = sb("S2", [128, KSL * 128], BF16)
    wM = sb("wM", [128, KSL * D], BF16)
    wK = sb("wK", [128, KSL * D], BF16)
    wV = sb("wV", [128, KSL * D], BF16)
    wR = sb("wR", [128, KSL * D], BF16)
    wO = sb("wO", [128, KSL * D], BF16)
    xbuf = [sb(f"xbuf{i}", [128, D], F32) for i in range(2)]
    xn_tm = [sb("xn_tm0", [128, D], BF16)]
    xn_fm = [sb(f"xn_fm{i}", [128, D], BF16) for i in range(2)]
    spb = [sb(f"spb{i}", [128, D], F16) for i in range(2)]
    sprb = [sb(f"sprb{i}", [128, D], F16) for i in range(2)]
    scr = [sb(f"scr{i}", [128, D], F32) for i in range(2)]
    sqd = sb("sqd", [128, D], BF16)
    Eb = [sb(f"Eb{i}", [128, KSL * ECOLS], BF16) for i in range(2)]
    Ekh = [sb(f"Ekh{i}", [128, D], BF16) for i in range(2)]
    ktm = [sb(f"ktm{i}", [128, D], BF16) for i in range(2)]
    kfm = [sb(f"kfm{i}", [128, D], BF16) for i in range(2)]
    vtm = [sb(f"vtm{i}", [128, D], BF16) for i in range(2)]
    khtm = [sb(f"khtm{i}", [128, D], BF16) for i in range(2)]
    ktil = [sb(f"ktil{i}", [128, KSL * 128], BF16) for i in range(2)]
    pevac = [sb(f"pevac{i}", [128, 512], BF16) for i in range(2)]
    yfm = [sb(f"yfm{i}", [128, D], BF16) for i in range(2)]
    xmidb = [sb(f"xmidb{i}", [128, D], F32) for i in range(2)]
    xn2b_off = sb_off[0]
    xn2b = [sb(f"xn2b{i}", [128, D], BF16) for i in range(2)]
    xn2f = [sb(f"xn2f{i}", [128, D], BF16) for i in range(2)]
    cb1r = nc.alloc_sbuf_tensor_at("cb1r", [128, 4096], BF16,
                                   offset=xn2b_off)
    attn_end = sb_off[0]

    sb_off[0] = base_big
    fw1 = sb("fw1", [128, KSL * 2048], BF16)
    fw2 = sb("fw2", [128, 16 * D], BF16)
    qst = [sb(f"qst{i}", [128, 2048], I32) for i in range(2)]
    sbc = [sb(f"sbc{i}", [128, 2048], BF16) for i in range(2)]
    qcv = [sb(f"qcv{i}", [128, 2048], BF16) for i in range(2)]
    fxn2 = [sb(f"fxn2{i}", [128, KSL * 512], BF16) for i in range(2)]
    fh = sb("fh", [128, 16 * 512], BF16)
    fsq = sb("fsq", [128, D], BF16)
    fz2a = [sb(f"fz2a{i}", [128, D], F32) for i in range(2)]
    fxmid = [sb(f"fxmid{i}", [128, D], F32) for i in range(4)]
    ft = [sb(f"ft{i}", [128, D], F32) for i in range(4)]
    fo = [sb(f"fo{i}", [128, D], F32) for i in range(2)]
    ffn_end = sb_off[0]

    stg_off = (attn_end - KSL * D * 2 - 2 * D * 4 - 64) & ~31
    wstage = [nc.alloc_sbuf_tensor_at(f"wstage{i}", [128, D], F32,
                                      offset=stg_off + i * D * 4) for i in range(2)]
    mstage = nc.alloc_sbuf_tensor_at("mstage", [128, KSL * D], BF16,
                                     offset=stg_off + 2 * D * 4)

    assert max(attn_end, ffn_end) <= abase + ARENA, (attn_end - abase,
                                                     ffn_end - abase)

    pool = [nc.alloc_psum_tensor(f"pp{i}", [128, 512], F32) for i in range(5)]
    outp = nc.alloc_psum_tensor("outp", [128, 1024], F32).ap()
    sadd = nc.alloc_psum_tensor("sadd", [128, 512], F32).ap()

    p = P(nc, ctx)

    def mm_group(psl, lhsTs, rhss, wait, extra=None):
        n = len(lhsTs)
        for j in range(n):
            first = (j == 0)
            last = (j == n - 1) and extra is None
            tk = p.op("tensor",
                      lambda e, o=psl, a=lhsTs[j], b=rhss[j], f=first, s=last:
                      e.matmul(o, a, b, start=f, stop=s, skip_group_check=True),
                      wait=wait if j == 0 else (),
                      inc="sT" if last else None)
        if extra is not None:
            oa, ra = extra
            tk = p.op("tensor",
                      lambda e, o=psl, a=oa, b=ra:
                      e.matmul(o, a, b, start=False, stop=True,
                               skip_group_check=True),
                      inc="sT")
        return tk

    def eslice(buf, sl, c0, c1):
        return buf[:, ECOLS * sl + c0: ECOLS * sl + c1]

    tU = [p.dma("sync", uA[:, :], c_UA, q="qC1"), p.dma("sync", uB[:, :], c_UB, q="qC2"),
          p.dma("sync", uBr[:, :], c_UBr, q="qC3"), p.dma("sync", uKH[:, :], c_UKH, q="qC0"),
          p.dma("sync", msk[:, :], c_MSK, q="qC1"), p.dma("sync", idn[:, :], c_ID, q="qC2")]
    def col_stage(dst, vec, q_):
        trow = p.dma("sync", rowtmp[0:1, 0:1024], vec[None, :],
                     wait=p.wr("rowtmp"), q=q_)
        tm = None
        for s in range(KSL):
            tm = p.op("tensor", lambda e, s_=s: e.matmul(
                pool[1].ap()[:, s_:s_ + 1],
                rowtmp[0:1, 128 * s_:128 * (s_ + 1)],
                onef[0:1, 0:1], start=(s_ == 0), stop=(s_ == KSL - 1),
                skip_group_check=True),
                wait=([trow, tonef] + p.wr("pl1")) if s == 0 else (),
                inc="sT" if s == KSL - 1 else None)
        p.rd("rowtmp", tm)
        tev = p.op("scalar", lambda e, d_=dst: e.activation(
            out=d_[:, :], in_=pool[1].ap()[:, 0:KSL], func=AF.Identity),
            wait=[tm], inc="sS")
        p.use["pl1"] = [tev]
        return tev

    tonef = p.op("vector", lambda e: e.memset(onef[0:1, 0:1], 1.0), inc="sV")
    tg = col_stage(gcol, ln_g, "qC3")
    tb = col_stage(bgcol, ln_b, "qC0")
    tg2 = col_stage(g2col, g2, "qC1")
    tb2 = col_stage(bg2col, b2v, "qC2")
    if dbg:
        p.dma("sync", cbd_d[:, 24:32], gcol[:, :], wait=[tg], q="qNO")
        p.dma("sync", scd_d[:, 0:8], g2col[:, :], wait=[tg2], q="qNO")
        arenaf = arena[:, :].bitcast(F32)
        p.dma("sync", arn0_d[:, :], arenaf[:, 0:512], wait=[tg2, tb2],
              q="qNO")
    tbw = p.dma("sync", bwrow[:, :], bw[None, :], q="qC3")
    tone = p.op("vector", lambda e: e.memset(ones[:, :], 1.0), inc="sV")
    tzb = p.op("vector", lambda e: e.memset(zb[0:1, 0:512], 1.0), inc="sV")
    tS = p.op("vector", lambda e: e.memset(S0[:, :], 0.0), inc="sV")
    tS2z = p.op("vector", lambda e: e.memset(S2[:, :], 0.0), inc="sV")
    tS = [tS, tS2z]

    def recip_fold(dstcol, srccol, wtok, tmp0, tmp1):
        t1 = p.op("vector", lambda e: e.tensor_scalar_add(
            out=stats[:, tmp0:tmp0 + KSL], in0=srccol[:, :], scalar1=1e-20),
            wait=wtok, inc="sV")
        t2 = p.op("vector", lambda e: e.reciprocal(
            out=stats[:, tmp1:tmp1 + KSL], in_=stats[:, tmp0:tmp0 + KSL]),
            wait=[t1], inc="sV")
        t3 = p.op("vector", lambda e: e.tensor_mul(
            out=dstcol[:, :], in0=dstcol[:, :], in1=stats[:, tmp1:tmp1 + KSL]),
            wait=[t2], inc="sV")
        return t3

    tbg = recip_fold(bgcol, gcol, [tg, tb], 0, 8)
    tbg2 = recip_fold(bg2col, g2col, [tg2, tb2], 16, 24)
    tbg = p.op("vector", lambda e: e.tensor_copy(out=bgcol_b[:, :], in_=bgcol[:, :]),
               wait=[tbg], inc="sV")
    tbg2 = p.op("vector", lambda e: e.tensor_copy(out=bg2col_b[:, :],
                                                  in_=bg2col[:, :]),
                wait=[tbg2], inc="sV")
    tgb = p.op("vector", lambda e: e.tensor_scalar_mul(
        out=g2bias[:, :], in0=g2col[:, :], scalar1=-128.0), wait=[tg2], inc="sV")

    def load_fold(wdst, wsrc, coltok, colap, start_wait=()):
        tlast = None
        for sl in range(KSL):
            stg = wstage[sl % 2]
            tl = p.dma("sync", stg[:, :], wsrc[128 * sl:128 * (sl + 1), :],
                       wait=p.wr(f"wstg{sl % 2}") + list(start_wait), q=f"qW{sl % 4}")
            if colap is None:
                tc = p.op("vector", lambda e, s=sl, g_=stg: e.tensor_copy(
                    out=wdst[:, D * s:D * (s + 1)], in_=g_[:, :]),
                    wait=[tl], inc="sV")
            else:
                tc = p.op("vector", lambda e, s=sl, g_=stg: e.tensor_scalar_mul(
                    out=wdst[:, D * s:D * (s + 1)], in0=g_[:, :],
                    scalar1=colap[:, s:s + 1]),
                    wait=[tl, coltok], inc="sV")
            p.rd(f"wstg{sl % 2}", tc)
            tlast = tc
        return tlast

    tWw = load_fold(mstage, Ww, None, None)
    tWxg = load_fold(wK, Wx, tg, gcol)
    bt0 = pool[3].ap().bitcast(BF16)
    tTx = None
    for jc in range(KSL):
        tt = None
        for s in range(KSL):
            tt = p.op("tensor", lambda e, s_=s, jc_=jc: e.transpose(
                bt0[:, 128 * s_:128 * (s_ + 1)],
                wK[:, D * s_ + 128 * jc_: D * s_ + 128 * (jc_ + 1)], idn[:, :]),
                wait=([tWxg, tU[5]] + p.wr("pl3")) if s == 0 else (),
                inc="sT" if s == KSL - 1 else None)
        tTx = p.op("vector", lambda e, jc_=jc: e.tensor_copy(
            out=wO[:, D * jc_:D * (jc_ + 1)], in_=bt0[:, :]),
            wait=[tt], inc="sV")
        p.use["pl3"] = [tTx]
    tM = None
    for sl in range(KSL):
        for hf in range(2):
            ps = pool[(sl * 2 + hf) % 2]
            tmm = mm_group(ps[:, :],
                           [wO[:, D * k + 128 * sl: D * k + 128 * (sl + 1)]
                            for k in range(KSL)],
                           [mstage[:, D * k + 512 * hf: D * k + 512 * (hf + 1)]
                            for k in range(KSL)],
                           wait=[tWw, tTx, p.use.get(f"pl{(sl * 2 + hf) % 2}")])
            tM = p.op("vector", lambda e, s=sl, h=hf, ps_=ps: e.tensor_copy(
                out=wM[:, D * s + 512 * h: D * s + 512 * (h + 1)], in_=ps_[:, :]),
                wait=[tmm], inc="sV")
            p.use[f"pl{(sl * 2 + hf) % 2}"] = [tM]

    tWk = load_fold(wK, Wk, tg, gcol, start_wait=[tM])
    tWv = load_fold(wV, Wv, tg, gcol)
    tWr = load_fold(wR, Wr, tg, gcol)
    tWo = load_fold(wO, Wo, None, None, start_wait=[tM])

    def row_mm(wten, wtok, ncols, dst32, dst32tok):
        tlast = None
        for ns in range(ncols // 512):
            ps = pool[2 + (ns % 2)]
            tmm = mm_group(ps[0:1, :],
                           [bgcol_b[:, k:k + 1] for k in range(KSL)],
                           [wten[:, D * k + 512 * ns: D * k + 512 * (ns + 1)]
                            for k in range(KSL)],
                           wait=[tbg, wtok, p.use.get(f"pl{2 + ns % 2}")])
            tv = p.op("vector", lambda e, n=ns, ps_=ps: e.tensor_copy(
                out=dst32[0:1, 512 * n:512 * (n + 1)], in_=ps_[0:1, :]),
                wait=[tmm, dst32tok], inc="sV")
            p.use[f"pl{2 + ns % 2}"] = [tv]
            tlast = tv
        return tlast

    t_rw = row_mm(wM, tM, D, rowtmp, None)
    t_rw2 = p.op("vector", lambda e: e.tensor_add(
        out=rowtmp[0:1, 1024:2048], in0=rowtmp[0:1, 0:1024], in1=bwrow[0:1, :]),
        wait=[t_rw, tbw], inc="sV")
    t_cw = p.op("vector", lambda e: e.tensor_copy(
        out=crow_w[0:1, :], in_=rowtmp[0:1, 1024:2048]), wait=[t_rw2], inc="sV")
    t_rr = row_mm(wR, tWr, D, rowtmp, t_cw)
    t_cr = p.op("vector", lambda e: e.tensor_copy(
        out=crow_r[0:1, :], in_=rowtmp[0:1, 0:1024]), wait=[t_rr], inc="sV")
    t_rk = row_mm(wK, tWk, D, rowtmp, t_cr)
    t_dk = p.dma("sync", ckrow_d, rowtmp[0:1, 0:1024], wait=[t_rk], q="qR")
    t_rv = row_mm(wV, tWv, D, rowtmp[0:1, 1024:2048], t_cw)
    t_dv = p.dma("sync", cvrow_d, rowtmp[0:1, 1024:2048], wait=[t_rv], q="qR")
    t_bk = p.dma("gpsimd", ckb[:, :], ckrow_d.broadcast_to([128, D]), wait=[t_dk], q="qRB")
    t_bv = p.dma("gpsimd", cvb[:, :], cvrow_d.broadcast_to([128, D]), wait=[t_dv], q="qRB")

    PRO_END = [("sV", p.cnt["sV"]), ("sT", p.cnt["sT"]), ("sS", p.cnt["sS"])]
    for nm in ("yfm0", "yfm1", "xmidb0", "xmidb1", "xn2b0", "xn2b1",
               "xn2f0", "xn2f1", "ktil0", "ktil1", "pevac0", "pevac1",
               "khtm0", "khtm1", "vtm0", "vtm1", "kfm0", "kfm1",
               "ktm0", "ktm1", "Ekh0", "Ekh1", "Eb0", "Eb1"):
        p.use.setdefault(nm, []).extend(PRO_END)
    tok_S_box = [tS]

    def emit_attn_chunk(c):
        tok_S = tok_S_box[0]
        pr = c % 2

        def gate():
            toks = [("sT", p.cnt["sT"]), ("sV", p.cnt["sV"]),
                    ("sS", p.cnt["sS"]), ("sP", p.cnt["sP"])]
            for nm in list(p.use.keys()):
                p.use[nm] = p.use[nm] + toks
        sc = 32 + 48 * pr
        xb = xbuf[c % 2]
        tx = p.dma("sync", xb[:, :], x[L * c:L * (c + 1), :],
                   wait=p.wr(f"xbuf{c % 2}"), q="qX")
        tsq = p.op("scalar", lambda e, xb_=xb: e.activation(
            out=sqd[:, :], in_=xb_[:, :], func=AF.Square,
            accum_out=stats[:, sc:sc + 1]),
            wait=[tx] + p.wr("sqd") + p.wr(f"st{sc}"), inc="sS")
        p.use["sqd"] = [tsq]
        tsm = p.op("vector", lambda e, xb_=xb: e.reduce_sum(
            out=stats[:, sc + 1:sc + 2], in_=xb_[:, :], axis=AXX),
            wait=[tx] + p.wr(f"st{sc + 1}"), inc="sV")
        t1 = p.op("vector", lambda e: e.tensor_scalar_mul(
            out=stats[:, sc + 2:sc + 3], in0=stats[:, sc + 1:sc + 2],
            scalar1=1.0 / D), wait=[tsm], inc="sV")
        t2 = p.op("vector", lambda e: e.tensor_mul(
            out=stats[:, sc + 3:sc + 4], in0=stats[:, sc + 2:sc + 3],
            in1=stats[:, sc + 2:sc + 3]), wait=[t1], inc="sV")
        t3 = p.op("vector", lambda e: e.tensor_scalar(
            out=stats[:, sc + 4:sc + 5], in0=stats[:, sc:sc + 1],
            scalar1=1.0 / D, scalar2=None, op0=ALU.mult), wait=[tsq], inc="sV")
        t4 = p.op("vector", lambda e: e.tensor_sub(
            out=stats[:, sc + 5:sc + 6], in0=stats[:, sc + 4:sc + 5],
            in1=stats[:, sc + 3:sc + 4]), wait=[t2, t3], inc="sV")
        t5 = p.op("vector", lambda e: e.tensor_scalar(
            out=stats[:, sc + 6:sc + 7], in0=stats[:, sc + 5:sc + 6],
            scalar1=(1.0 + EPS), scalar2=EPS * EPS, op0=ALU.mult, op1=ALU.add),
            wait=[t4], inc="sV")
        t6 = p.op("scalar", lambda e: e.activation(
            out=stats[:, sc + 7:sc + 8], in_=stats[:, sc + 6:sc + 7], func=AF.Ln),
            wait=[t5], inc="sS")
        t7 = p.op("scalar", lambda e: e.activation(
            out=stats[:, sc + 8:sc + 9], in_=stats[:, sc + 7:sc + 8], func=AF.Exp,
            scale=-0.5), wait=[t6], inc="sS")
        t8 = p.op("vector", lambda e: e.tensor_mul(
            out=stats[:, sc + 9:sc + 10], in0=stats[:, sc + 2:sc + 3],
            in1=stats[:, sc + 8:sc + 9]), wait=[t7], inc="sV")
        t9 = p.op("vector", lambda e: e.tensor_scalar_mul(
            out=stats[:, sc + 10:sc + 11], in0=stats[:, sc + 9:sc + 10],
            scalar1=-1.0), wait=[t8], inc="sV")
        p.rd(f"st{sc}", t3)
        p.rd(f"st{sc + 1}", t1)
        txg = p.op("gpsimd", lambda e, xb_=xb: e.tensor_mul(
            out=scr[0][:, :], in0=xb_[:, :],
            in1=stats[:, sc + 8:sc + 9].broadcast_to([128, D])),
            wait=[t9] + p.wr("scr0"), inc="sP")
        p.use["scr0"] = [txg]
        txn = p.op("gpsimd", lambda e: e.tensor_add(
            out=xn_tm[0][:, :], in0=scr[0][:, :],
            in1=stats[:, sc + 10:sc + 11].broadcast_to([128, D])),
            wait=[txg] + p.wr("xn_tm0"), inc="sP")
        p.rd("scr0", txn)
        bt = pool[0].ap().bitcast(BF16)
        ttr = None
        for sl in range(KSL):
            ttr = p.op("tensor", lambda e, s=sl: e.transpose(
                bt[:, 128 * s:128 * (s + 1)], xn_tm[0][:, 128 * s:128 * (s + 1)],
                idn[:, :]),
                wait=([txn, tU[5]] + p.wr("pl0")) if sl == 0 else (),
                inc="sT" if sl == KSL - 1 else None)
        txnf = p.op("vector", lambda e: e.tensor_copy(
            out=xn_fm[pr][:, :], in_=bt[:, :]),
            wait=[ttr] + p.wr(f"xn_fm{pr}"), inc="sV")
        p.use["pl0"] = [txnf]
        p.rd("xn_tm0", txnf)
        if sub < 2:
            gate()
            return

        def softplus_chain(wten, wtok, crow, crtok, spdst, spname, neg):
            tps = []
            for hf in range(2):
                ps = pool[1 + hf]
                tmm = mm_group(ps[:, :],
                               [xn_fm[pr][:, 128 * k:128 * (k + 1)]
                                for k in range(KSL)],
                               [wten[:, D * k + 512 * hf:D * k + 512 * (hf + 1)]
                                for k in range(KSL)],
                               wait=[txnf, wtok, crtok, tone] + p.wr(f"pl{1 + hf}"),
                               extra=None if v604 == 20 else
                               (ones[:, :], crow[0:1, 512 * hf:512 * (hf + 1)]))
                a = scr[hf]
                ta = p.op("scalar", lambda e, ps_=ps, a_=a: e.activation(
                    out=a_[:, 0:512], in_=ps_[:, :], func=AF.Abs),
                    wait=[tmm] + p.wr(f"scr{hf}"), inc="sS")
                te = p.op("scalar", lambda e, a_=a: e.activation(
                    out=a_[:, 512:1024], in_=a_[:, 0:512], func=AF.Exp, scale=-1.0),
                    wait=[ta], inc="sS")
                tad = p.op("vector", lambda e, a_=a: e.tensor_scalar_add(
                    out=a_[:, 0:512], in0=a_[:, 512:1024], scalar1=1.0),
                    wait=[te], inc="sV")
                tl = p.op("scalar", lambda e, a_=a: e.activation(
                    out=a_[:, 512:1024], in_=a_[:, 0:512], func=AF.Ln),
                    wait=[tad], inc="sS")
                if dbg and not neg:
                    tds_ = p.dma("sync",
                                 scrd_d[L * c:L * (c + 1),
                                        512 * hf:512 * (hf + 1)],
                                 a[:, 512:1024], wait=[tl], q="qNO")
                    p.rd(f"scr{hf}", tds_)
                op0 = ALU.min if neg else ALU.max
                op1 = ALU.subtract if neg else ALU.add
                tp = p.op("vector", lambda e, ps_=ps, a_=a, h=hf, o0=op0, o1=op1:
                          e.scalar_tensor_tensor(
                    out=spdst[:, 512 * h:512 * (h + 1)], in0=ps_[:, :], scalar=0.0,
                    in1=a_[:, 512:1024], op0=o0, op1=o1),
                    wait=[tl] + (p.wr(spname) if hf == 0 else []), inc="sV")
                p.use[f"scr{hf}"] = [tp]
                p.use[f"pl{1 + hf}"] = [tp]
                tps.append(tp)
            return tps

        tsp = softplus_chain(wM, tM, crow_w, t_cw, spb[pr], f"spb{pr}", False)
        tspr = softplus_chain(wR, tWr, crow_r, t_cr, sprb[pr], f"sprb{pr}", True)
        if dbg:
            tdd = p.dma("sync", spd_d[L * c:L * (c + 1), :], spb[pr][:, :],
                        wait=[tsp[1]], q="qMO")
            p.rd(f"spb{pr}", tdd)
            tdd = p.dma("sync", sprd_d[L * c:L * (c + 1), :], sprb[pr][:, :],
                        wait=[tspr[1]], q="qMO")
            p.rd(f"sprb{pr}", tdd)

        def proj(wten, wtok, fold, foldtok, dst, dstn):
            tvs = []
            for hf in range(2):
                ps = pool[1 + hf]
                tmm = mm_group(ps[:, :],
                               [xn_fm[pr][:, 128 * k:128 * (k + 1)]
                                for k in range(KSL)],
                               [wten[:, D * k + 512 * hf: D * k + 512 * (hf + 1)]
                                for k in range(KSL)],
                               wait=[txnf, wtok] + p.wr(f"pl{1 + hf}"))
                tv = p.op("vector", lambda e, h=hf, ps_=ps: e.tensor_add(
                    out=dst[:, 512 * h:512 * (h + 1)], in0=ps_[:, :],
                    in1=fold[:, 512 * h:512 * (h + 1)]),
                    wait=[tmm, foldtok] + (p.wr(dstn) if hf == 0 else []), inc="sV")
                p.use[f"pl{1 + hf}"] = [tv]
                tvs.append(tv)
            return tvs[-1]

        tkp = proj(wK, tWk, ckb, t_bk, ktm[pr], f"ktm{pr}")
        tvp = proj(wV, tWv, cvb, t_bv, vtm[pr], f"vtm{pr}")

        bt2 = pool[0].ap().bitcast(BF16)
        ttr2 = None
        for sl in range(KSL):
            ttr2 = p.op("tensor", lambda e, s=sl: e.transpose(
                bt2[:, 128 * s:128 * (s + 1)], ktm[pr][:, 128 * s:128 * (s + 1)],
                idn[:, :]),
                wait=([tkp] + p.wr("pl0")) if sl == 0 else (),
                inc="sT" if sl == KSL - 1 else None)
        tkf = p.op("vector", lambda e: e.tensor_copy(
            out=kfm[pr][:, :], in_=bt2[:, :]),
            wait=[ttr2] + p.wr(f"kfm{pr}"), inc="sV")
        p.use["pl0"] = [tkf]
        p.rd(f"xn_fm{pr}", ttr2)
        p.rd(f"ktm{pr}", ttr2)
        if sub < 3:
            gate()
            return

        teAB = []
        for sl in range(KSL):
            psA = pool[1 + (sl % 2)]
            psB = pool[3]
            tA = mm_group(psA[:, 0:NA],
                          [spb[pr][:, 128 * sl:128 * (sl + 1)]], [uA[:, :]],
                          wait=[tsp[1], tU[0]] + p.wr(f"pl{1 + (sl % 2)}"))
            tB1 = p.op("tensor", lambda e, s=sl, ps_=psB: e.matmul(
                ps_[:, 0:NB], spb[pr][:, 128 * s:128 * (s + 1)], uB[:, :],
                start=True, stop=False, skip_group_check=True),
                wait=[tsp[1], tU[1], tU[2]] + p.wr("pl3"))
            tB2 = p.op("tensor", lambda e, s=sl, ps_=psB: e.matmul(
                ps_[:, 0:NB], sprb[pr][:, 128 * s:128 * (s + 1)], uBr[:, :],
                start=False, stop=True, skip_group_check=True),
                wait=[tspr[1]], inc="sT")
            teA = p.op("scalar", lambda e, s=sl, ps_=psA: e.activation(
                out=eslice(Eb[pr], s, 0, NA), in_=ps_[:, 0:NA], func=AF.Exp),
                wait=[tA] + (p.wr(f"Eb{pr}") if sl == 0 else []), inc="sS")
            teB = p.op("scalar", lambda e, s=sl, ps_=psB: e.activation(
                out=eslice(Eb[pr], s, NA, ECOLS), in_=ps_[:, 0:NB], func=AF.Exp),
                wait=[tB2], inc="sS")
            p.use[f"pl{1 + (sl % 2)}"] = [teA]
            p.use["pl3"] = [teB]
            teAB.append(teB if sl == KSL - 1 else teA)
        tEdone = teAB[-1]
        if dbg:
            tdd = p.dma("sync", ebd_d[L * c:L * (c + 1), :], Eb[pr][:, :],
                        wait=[tEdone], q="qNO")
            p.rd(f"Eb{pr}", tdd)

        tkh1 = p.op("tensor", lambda e: e.matmul(
            pool[0].ap()[:, :], uKH[:, :], spb[pr][:, 0:512],
            start=True, stop=True, skip_group_check=True),
            wait=[tsp[1], tU[3]] + p.wr("pl0"), inc="sT")
        tek1 = p.op("scalar", lambda e: e.activation(
            out=Ekh[pr][:, 0:512], in_=pool[0].ap()[:, :], func=AF.Exp),
            wait=[tkh1] + p.wr(f"Ekh{pr}"), inc="sS")
        p.use["pl0"] = [tek1]
        tkh2 = p.op("tensor", lambda e: e.matmul(
            pool[0].ap()[:, :], uKH[:, :], spb[pr][:, 512:1024],
            start=True, stop=True, skip_group_check=True),
            wait=p.wr("pl0"), inc="sT")
        tek2 = p.op("scalar", lambda e: e.activation(
            out=Ekh[pr][:, 512:1024], in_=pool[0].ap()[:, :], func=AF.Exp),
            wait=[tkh2], inc="sS")
        p.use["pl0"] = [tek2]
        p.rd(f"spb{pr}", tkh2)
        p.rd(f"spb{pr}", tEdone)
        p.rd(f"sprb{pr}", tEdone)

        tkh = p.op("gpsimd", lambda e: e.tensor_mul(
            out=khtm[pr][:, :], in0=ktm[pr][:, :], in1=Ekh[pr][:, :]),
            wait=[tek1, tek2, tkp] + p.wr(f"khtm{pr}"), inc="sP")
        p.rd(f"ktm{pr}", tkh)
        p.rd(f"Ekh{pr}", tkh)
        if sub < 4:
            gate()
            return

        Scur = S0 if pr == 0 else S1
        Snxt = S1 if pr == 0 else S0
        Ebr = Eb[pr][:, :].rearrange("q (s n) -> q s n", s=KSL)
        t_i = None
        for slh in range(KSL):
            t_i = p.op("tensor", lambda e, s_=slh: e.matmul(
                outp[:, 128 * s_:128 * (s_ + 1)],
                S2[:, 128 * s_:128 * (s_ + 1)],
                Ebr[:, s_, NA + 128:NA + 256],
                start=(s_ in (0, 4)), stop=False, skip_group_check=True),
                wait=([tok_S, tEdone] + p.wr("outp")) if slh == 0 else (),
                inc="sT" if slh == KSL - 1 else None)
        p.rd("S2buf", t_i)
        p.rd(f"Eb{pr}", t_i)
        tok_pv = t_i
        if sub < 4.1:
            gate()
            return

        for i in range(NSUB):
            kb = ktil[i % 2]
            pe = pevac[i % 2]
            sr = KT_SZ[i]
            tkt_i = p.op("gpsimd", lambda e, i_=i, kb_=kb: e.tensor_mul(
                out=kb_[:, :].rearrange("q (s n) -> q s n", s=KSL)[:, :, 0:KT_SZ[i_]],
                in0=kfm[pr][:, :].rearrange("q (s n) -> q s n", s=KSL)
                    [:, :, 0:KT_SZ[i_]],
                in1=Eb[pr][:, :].rearrange("q (s n) -> q s n", s=KSL)
                    [:, :, KT_OFF[i_]:KT_OFF[i_] + KT_SZ[i_]]),
                wait=[tkf, tEdone] + p.wr(f"ktil{i % 2}"), inc="sP")
            p.rd(f"kfm{pr}", tkt_i)
            p.rd(f"Eb{pr}", tkt_i)
            if sub < 4.2:
                if i == NSUB - 1:
                    gate()
                    return
                continue
            tP = None
            heads = list(range(0, NH, 2)) + list(range(1, NH, 2))
            for hi, h in enumerate(heads):
                pb, slh = (h % 2) * 64, h // 2
                tgt = pool[3] if h % 2 == 0 else pool[4]
                tP = p.op("tensor", lambda e, h_=h, pb_=pb, s_=slh, i_=i, kb_=kb,
                          sr_=sr, tg_=tgt: e.matmul(
                    tg_.ap()[0:sr_, 32 * (h_ // 2):32 * (h_ // 2) + 32],
                    kb_[pb_:pb_ + 64, 128 * s_:128 * s_ + sr_],
                    Ebr[pb_:pb_ + 64, s_, NA + 32 * i_:NA + 32 * (i_ + 1)],
                    start=True, stop=True, skip_group_check=True),
                    wait=([tkt_i] + p.wr("pl3") + p.wr("pl4")) if hi == 0 else (),
                    inc="sT" if hi == NH - 1 else None)
            p.rd(f"ktil{i % 2}", tP)
            p.rd(f"Eb{pr}", tP)
            if sub < 4.3:
                if i == NSUB - 1:
                    gate()
                    return
                continue
            tpe0 = p.op("vector", lambda e, i_=i, pe_=pe, sr_=sr: e.tensor_tensor(
                out=pe_[0:sr_, :].rearrange("q (h two n) -> q h two n",
                                            two=2, n=32)[:, :, 0, :],
                in0=pool[3].ap()[0:sr_, 0:256].rearrange("q (h n) -> q h n", n=32),
                in1=msk[0:sr_, 32 * i_:32 * (i_ + 1)].unsqueeze(1).broadcast_to(
                    [sr_, NH // 2, 32]), op=ALU.mult),
                wait=[tP, tU[4]] + p.wr(f"pevac{i % 2}"), inc="sV")
            tpe1 = p.op("vector", lambda e, i_=i, pe_=pe, sr_=sr: e.tensor_tensor(
                out=pe_[0:sr_, :].rearrange("q (h two n) -> q h two n",
                                            two=2, n=32)[:, :, 1, :],
                in0=pool[4].ap()[0:sr_, 0:256].rearrange("q (h n) -> q h n", n=32),
                in1=msk[0:sr_, 32 * i_:32 * (i_ + 1)].unsqueeze(1).broadcast_to(
                    [sr_, NH // 2, 32]), op=ALU.mult),
                wait=[tP], inc="sV")
            p.use["pl3"] = [tpe0]
            p.use["pl4"] = [tpe1]
            tpe = [tpe0, tpe1]
            if sub < 4.4:
                if i == NSUB - 1:
                    gate()
                    return
                continue
            tpv = None
            for h in range(NH):
                pb, slh = (h % 2) * 64, h // 2
                tpv = p.op("tensor", lambda e, h_=h, pb_=pb, s_=slh, i_=i, pe_=pe,
                           sr_=sr: e.matmul(
                    outp[pb_:pb_ + 64,
                         128 * s_ + 32 * i_:128 * s_ + 32 * (i_ + 1)],
                    vtm[pr][0:sr_, 64 * h_:64 * (h_ + 1)],
                    pe_[0:sr_, 32 * h_:32 * (h_ + 1)],
                    start=False, stop=(i_ == NSUB - 1 and h_ == NH - 1),
                    skip_group_check=True),
                    wait=([tpe, tvp, tok_pv] if h == 0 else ()),
                    inc="sT" if h == NH - 1 else None)
            tok_pv = tpv
            p.rd(f"pevac{i % 2}", tpv)
            p.rd(f"vtm{pr}", tpv)

        tyev = p.op("vector", lambda e: e.tensor_copy(
            out=yfm[pr][:, :], in_=outp[:, :]),
            wait=[tok_pv] + p.wr(f"yfm{pr}"), inc="sV")
        p.use["outp"] = [tyev]
        if dbg:
            tdd = p.dma("sync", yfmd_d[L * c:L * (c + 1), :], yfm[pr][:, :],
                        wait=[tyev], q="qNO")
            p.rd(f"yfm{pr}", tdd)
        if sub < 5:
            gate()
            return

        tSa = None
        for h in range(NH):
            tSa = p.op("tensor", lambda e, h_=h: e.matmul(
                sadd[(h_ % 2) * 64:(h_ % 2) * 64 + 64,
                     64 * (h_ // 2):64 * (h_ // 2) + 64],
                khtm[pr][:, 64 * h_:64 * (h_ + 1)],
                vtm[pr][:, 64 * h_:64 * (h_ + 1)],
                start=True, stop=True, skip_group_check=True),
                wait=([tkh, tvp] + p.wr("sadd")) if h == 0 else (),
                inc="sT" if h == NH - 1 else None)
        p.rd(f"khtm{pr}", tSa)
        p.rd(f"vtm{pr}", tSa)
        tds = None
        for sl in range(KSL):
            tds = p.op("gpsimd", lambda e, s=sl, S_=Scur: e.tensor_mul(
                out=Sd[:, 64 * s:64 * (s + 1)], in0=S_[:, 64 * s:64 * (s + 1)],
                in1=eslice(Eb[pr], s, DELTA_C,
                           DELTA_C + 1).broadcast_to([128, 64])),
                wait=([tEdone, tok_S] + p.wr("Sd")) if sl == 0 else (),
                inc="sP" if sl == KSL - 1 else None)
        p.rd(f"Sbuf{0 if pr == 0 else 1}", tds)
        p.rd(f"Eb{pr}", tds)
        tS2 = p.op("vector", lambda e, S_=Snxt: e.tensor_add(
            out=S_[:, :], in0=Sd[:, :], in1=sadd[:, :]),
            wait=[tSa, tds] + p.wr(f"Sbuf{1 if pr == 0 else 0}"), inc="sV")
        p.use["Sd"] = [tS2]
        p.use["sadd"] = [tS2]
        tbd0 = p.op("vector", lambda e, S_=Snxt: e.tensor_copy(
            out=S2[0:64, :].rearrange("q (s n) -> q s n", n=128)[:, :, 0:64],
            in_=S_[0:64, :].rearrange("q (s n) -> q s n", n=64)),
            wait=[tS2] + p.wr("S2buf"), inc="sV")
        tbd1 = p.op("vector", lambda e, S_=Snxt: e.tensor_copy(
            out=S2[64:128, :].rearrange("q (s n) -> q s n", n=128)[:, :, 64:128],
            in_=S_[64:128, :].rearrange("q (s n) -> q s n", n=64)),
            wait=[tS2], inc="sV")
        tok_S = [tbd0, tbd1]
        if sub < 6:
            tok_S_box[0] = tok_S
            gate()
            return

        if sub < 6:
            tok_S_box[0] = tok_S
            gate()
            return
        att = {}
        for hf in range(2):
            ps = pool[1 + hf]
            tmm = mm_group(ps[:, :],
                           [yfm[pr][:, 128 * k:128 * (k + 1)] for k in range(KSL)],
                           [wO[:, D * k + 512 * hf: D * k + 512 * (hf + 1)]
                            for k in range(KSL)],
                           wait=[tyev, tWo] + p.wr(f"pl{1 + hf}"))
            tev = p.op("scalar", lambda e, ps_=ps, h=hf: e.activation(
                out=scr[h][:, 0:512], in_=ps_[:, :], func=AF.Identity,
                accum_out=stats[:, sc + 13 + h:sc + 14 + h]),
                wait=[tmm] + p.wr(f"scr{hf}") + p.wr(f"st{sc + 13 + hf}"),
                inc="sS")
            p.use[f"pl{1 + hf}"] = [tev]
            p.use[f"scr{hf}"] = [tev]
            tsqr = p.op("scalar", lambda e, h=hf: e.activation(
                out=sqd[:, 512 * h:512 * (h + 1)], in_=scr[h][:, 0:512],
                func=AF.Square, accum_out=stats[:, sc + 11 + h:sc + 12 + h]),
                wait=[tev] + p.wr("sqd") + p.wr(f"st{sc + 11 + hf}"), inc="sS")
            p.use["sqd"] = p.use.get("sqd", []) + [tsqr]
            att[hf] = (tmm, tsqr, tev, ps)
            p.rd(f"yfm{pr}", tmm)
            if sub < 6.15 and hf == 1:
                tok_S_box[0] = tok_S
                gate()
                return
        if sub < 6.25:
            tok_S_box[0] = tok_S
            gate()
            return
        u1 = p.op("vector", lambda e: e.tensor_add(
            out=stats[:, sc + 15:sc + 16], in0=stats[:, sc + 11:sc + 12],
            in1=stats[:, sc + 12:sc + 13]), wait=[att[0][1], att[1][1]], inc="sV")
        u2 = p.op("vector", lambda e: e.tensor_add(
            out=stats[:, sc + 16:sc + 17], in0=stats[:, sc + 13:sc + 14],
            in1=stats[:, sc + 14:sc + 15]), wait=[att[0][2], att[1][2]], inc="sV")
        p.rd(f"st{sc + 11}", u1)
        p.rd(f"st{sc + 12}", u1)
        p.rd(f"st{sc + 13}", u2)
        p.rd(f"st{sc + 14}", u2)
        u3 = p.op("vector", lambda e: e.tensor_scalar_mul(
            out=stats[:, sc + 17:sc + 18], in0=stats[:, sc + 16:sc + 17],
            scalar1=1.0 / D), wait=[u2], inc="sV")
        u4 = p.op("vector", lambda e: e.tensor_mul(
            out=stats[:, sc + 18:sc + 19], in0=stats[:, sc + 17:sc + 18],
            in1=stats[:, sc + 17:sc + 18]), wait=[u3], inc="sV")
        u5 = p.op("vector", lambda e: e.tensor_scalar(
            out=stats[:, sc + 19:sc + 20], in0=stats[:, sc + 15:sc + 16],
            scalar1=1.0 / D, scalar2=EPS, op0=ALU.mult, op1=ALU.add),
            wait=[u1], inc="sV")
        u6 = p.op("vector", lambda e: e.tensor_sub(
            out=stats[:, sc + 20:sc + 21], in0=stats[:, sc + 19:sc + 20],
            in1=stats[:, sc + 18:sc + 19]), wait=[u4, u5], inc="sV")
        u7 = p.op("scalar", lambda e: e.activation(
            out=stats[:, sc + 21:sc + 22], in_=stats[:, sc + 20:sc + 21],
            func=AF.Ln), wait=[u6], inc="sS")
        u8 = p.op("scalar", lambda e: e.activation(
            out=stats[:, sc + 22:sc + 23], in_=stats[:, sc + 21:sc + 22],
            func=AF.Exp, scale=-0.5), wait=[u7], inc="sS")
        u9 = p.op("vector", lambda e: e.tensor_mul(
            out=stats[:, sc + 23:sc + 24], in0=stats[:, sc + 17:sc + 18],
            in1=stats[:, sc + 22:sc + 23]), wait=[u8], inc="sV")
        u10 = p.op("vector", lambda e: e.tensor_scalar_mul(
            out=stats[:, sc + 24:sc + 25], in0=stats[:, sc + 23:sc + 24],
            scalar1=-1.0), wait=[u9], inc="sV")
        if sub < 6.35:
            tok_S_box[0] = tok_S
            gate()
            return
        txm = None
        for hf in range(2):
            ta1 = p.op("gpsimd", lambda e, h=hf: e.tensor_mul(
                out=scr[h][:, 512:1024], in0=scr[h][:, 0:512],
                in1=stats[:, sc + 22:sc + 23].broadcast_to([128, 512])),
                wait=[u10], inc="sP")
            ta2 = p.op("gpsimd", lambda e, h=hf: e.tensor_add(
                out=scr[h][:, 512:1024], in0=scr[h][:, 512:1024],
                in1=stats[:, sc + 24:sc + 25].broadcast_to([128, 512])),
                wait=[ta1], inc="sP")
            txm = p.op("vector", lambda e, xb_=xb, h=hf: e.tensor_add(
                out=xmidb[pr][:, 512 * h:512 * (h + 1)],
                in0=scr[h][:, 512:1024],
                in1=xb_[:, 512 * h:512 * (h + 1)]),
                wait=[ta2] + (p.wr(f"xmidb{pr}") if hf == 0 else []), inc="sV")
            p.rd(f"scr{hf}", txm)
            p.use[f"scr{hf}"] = p.use.get(f"scr{hf}", []) + [ta2]
            p.rd(f"xbuf{c % 2}", txm)
        tdx = p.dma("sync", xmid_d[L * c:L * (c + 1), :], xmidb[pr][:, :],
                    wait=[txm], q="qMO")
        p.rd(f"xmidb{pr}", tdx)
        if sub < 7:
            tok_S_box[0] = tok_S
            gate()
            return
        tsq3 = p.op("scalar", lambda e: e.activation(
            out=sqd[:, :], in_=xmidb[pr][:, :], func=AF.Square,
            accum_out=stats[:, sc + 25:sc + 26]),
            wait=[txm] + p.wr("sqd"), inc="sS")
        p.use["sqd"] = [tsq3]
        tsm3 = p.op("vector", lambda e: e.reduce_sum(
            out=stats[:, sc + 26:sc + 27], in_=xmidb[pr][:, :], axis=AXX),
            wait=[txm], inc="sV")
        v1 = p.op("vector", lambda e, c_=c: e.tensor_scalar_mul(
            out=m3c[:, 3 * c_:3 * c_ + 1], in0=stats[:, sc + 26:sc + 27],
            scalar1=1.0 / D), wait=[tsm3], inc="sV")
        v2 = p.op("vector", lambda e, c_=c: e.tensor_mul(
            out=stats[:, sc + 27:sc + 28], in0=m3c[:, 3 * c_:3 * c_ + 1],
            in1=m3c[:, 3 * c_:3 * c_ + 1]), wait=[v1], inc="sV")
        v3 = p.op("vector", lambda e: e.tensor_scalar_mul(
            out=stats[:, sc + 28:sc + 29], in0=stats[:, sc + 25:sc + 26],
            scalar1=1.0 / D), wait=[tsq3], inc="sV")
        v4 = p.op("vector", lambda e: e.tensor_sub(
            out=stats[:, sc + 29:sc + 30], in0=stats[:, sc + 28:sc + 29],
            in1=stats[:, sc + 27:sc + 28]), wait=[v2, v3], inc="sV")
        v5 = p.op("vector", lambda e: e.tensor_scalar_add(
            out=stats[:, sc + 30:sc + 31], in0=stats[:, sc + 29:sc + 30],
            scalar1=EPS), wait=[v4], inc="sV")
        v6 = p.op("scalar", lambda e: e.activation(
            out=stats[:, sc + 31:sc + 32], in_=stats[:, sc + 30:sc + 31],
            func=AF.Ln), wait=[v5], inc="sS")
        v7 = p.op("scalar", lambda e, c_=c: e.activation(
            out=m3c[:, 3 * c_ + 1:3 * c_ + 2], in_=stats[:, sc + 31:sc + 32],
            func=AF.Exp, scale=-0.5), wait=[v6], inc="sS")
        v8 = p.op("vector", lambda e, c_=c: e.tensor_mul(
            out=stats[:, sc + 32:sc + 33], in0=m3c[:, 3 * c_:3 * c_ + 1],
            in1=m3c[:, 3 * c_ + 1:3 * c_ + 2]), wait=[v7], inc="sV")
        v9 = p.op("vector", lambda e, c_=c: e.tensor_scalar_mul(
            out=m3c[:, 3 * c_ + 2:3 * c_ + 3], in0=stats[:, sc + 32:sc + 33],
            scalar1=-1.0), wait=[v8], inc="sV")
        va = p.op("vector", lambda e: e.tensor_scalar(
            out=stats[:, sc + 33:sc + 34], in0=stats[:, sc + 29:sc + 30],
            scalar1=(1.0 + EPS), scalar2=EPS * EPS, op0=ALU.mult, op1=ALU.add),
            wait=[v4], inc="sV")
        vb = p.op("scalar", lambda e: e.activation(
            out=stats[:, sc + 34:sc + 35], in_=stats[:, sc + 33:sc + 34],
            func=AF.Ln), wait=[va], inc="sS")
        vc = p.op("scalar", lambda e: e.activation(
            out=stats[:, sc + 35:sc + 36], in_=stats[:, sc + 34:sc + 35],
            func=AF.Exp, scale=-0.5), wait=[vb], inc="sS")
        vd = p.op("vector", lambda e, c_=c: e.tensor_mul(
            out=stats[:, sc + 36:sc + 37], in0=m3c[:, 3 * c_:3 * c_ + 1],
            in1=stats[:, sc + 35:sc + 36]), wait=[vc, v1], inc="sV")
        ve_ = p.op("vector", lambda e: e.tensor_scalar_mul(
            out=stats[:, sc + 37:sc + 38], in0=stats[:, sc + 36:sc + 37],
            scalar1=-1.0), wait=[vd], inc="sV")
        tx2g = p.op("gpsimd", lambda e: e.tensor_mul(
            out=scr[1][:, :], in0=xmidb[pr][:, :],
            in1=stats[:, sc + 35:sc + 36].broadcast_to([128, D])),
            wait=[ve_] + p.wr("scr1"), inc="sP")
        p.use["scr1"] = [tx2g]
        txn2 = p.op("gpsimd", lambda e: e.tensor_add(
            out=xn2b[pr][:, :], in0=scr[1][:, :],
            in1=stats[:, sc + 37:sc + 38].broadcast_to([128, D])),
            wait=[tx2g] + p.wr(f"xn2b{pr}"), inc="sP")
        p.rd("scr1", txn2)
        p.rd(f"xmidb{pr}", txn2)
        if sub < 8:
            tok_S_box[0] = tok_S
            gate()
            return
        bt3 = pool[0].ap().bitcast(BF16)
        ttr3 = None
        for sl in range(KSL):
            ttr3 = p.op("tensor", lambda e, s=sl: e.transpose(
                bt3[:, 128 * s:128 * (s + 1)], xn2b[pr][:, 128 * s:128 * (s + 1)],
                idn[:, :]),
                wait=([txn2] + p.wr("pl0")) if sl == 0 else (),
                inc="sT" if sl == KSL - 1 else None)
        p.rd(f"xn2b{pr}", ttr3)
        txn2f = p.op("vector", lambda e: e.tensor_copy(
            out=xn2f[pr][:, :], in_=bt3[:, :]),
            wait=[ttr3] + p.wr(f"xn2f{pr}"), inc="sV")
        p.use["pl0"] = [txn2f]
        tdn = p.dma("sync", xn2_d[L * c:L * (c + 1), :], xn2f[pr][:, :],
                    wait=[txn2f], q="qNO")
        p.rd(f"xn2f{pr}", tdn)
        if dbg and c == 0:
            p.dma("sync", scd_d[:, 8:16], g2col[:, :], wait=[tdn], q="qNO")
        if dbg:
            p.dma("sync", arnc_d[128 * c:128 * (c + 1), :],
                  arena[:, :].bitcast(F32)[:, 0:512],
                  wait=[tdn], q="qNO")
        if dbg and c == NCH - 1:
            p.dma("sync", scd_d[:, 16:24], g2col[:, :], wait=[tdn], q="qNO")
            p.dma("sync", arn1_d[:, :], arena[:, :].bitcast(F32)[:, 0:512],
                  wait=[tdn], q="qNO")
        tok_S_box[0] = tok_S

    if stage >= 2:
        for c in range(NCH):
            emit_attn_chunk(c)

    attn_done = [("sT", p.cnt["sT"]), ("sV", p.cnt["sV"]), ("sS", p.cnt["sS"]),
                 ("sP", p.cnt["sP"]), ("qMO", p.cnt.get("qMO", 0)),
                 ("qNO", p.cnt.get("qNO", 0)), ("qX", p.cnt.get("qX", 0))]
    attn_done = [t for t in attn_done if t[1] > 0]

    f32v = fw1[:, :].bitcast(F32)
    ts1 = p.dma("sync", f32v[0:8, 0:4096], s1, wait=attn_done, q="qC0")
    tc1 = p.op("vector", lambda e: e.tensor_copy(
        out=fw2[0:8, 0:4096], in_=f32v[0:8, 0:4096]),
        wait=[ts1] + attn_done, inc="sV")
    td1 = p.dma("sync", s1b_d, fw2[0:8, 0:4096], wait=[tc1], q="qR")
    ts2 = p.dma("sync", f32v[0:32, 4096:5120], s2, wait=attn_done, q="qC1")
    tc2 = p.op("vector", lambda e: e.tensor_copy(
        out=fw2[0:32, 4096:5120], in_=f32v[0:32, 4096:5120]),
        wait=[ts2] + attn_done, inc="sV")
    td2 = p.dma("sync", s2b_d, fw2[0:32, 4096:5120], wait=[tc2], q="qR")
    p.use["fw1st"] = [tc1, tc2]
    p.use["fw2st"] = [td1, td2]

    tb1r = p.dma("gpsimd", b1row[0:1, :], b1[None, :], wait=attn_done, q="qRB")
    tb2r = p.dma("sync", rowtmp[0:1, 0:1024], b2f[None, :], wait=attn_done, q="qC3")
    tcb2 = p.op("vector", lambda e: e.tensor_scalar_mul(
        out=cb2row[0:1, :], in0=rowtmp[0:1, 0:1024], scalar1=0.5),
        wait=[tb2r] + attn_done, inc="sV")
    p.use["rowtmp"] = [tcb2]

    def dequant_w1(colrange, extra_wait):
        tws = []
        for sl in range(KSL):
            b_ = sl % 2
            tq = p.dma("sync", qst[b_][:, 0:2048],
                       q1[128 * sl:128 * (sl + 1), colrange[0]:colrange[1]],
                       wait=p.wr(f"qst{b_}") + list(extra_wait), q=f"qQ{b_}")
            tsb = p.dma("sync", sbc[b_][:, 0:2048],
                        s1b_d[sl:sl + 1, colrange[0]:colrange[1]].broadcast_to(
                            [128, 2048]),
                        wait=[td1] + p.wr(f"sbc{b_}") + list(extra_wait), q=f"qSB{b_}")
            tw = None
            for half in range(2):
                sc_ = scr[half]
                tcs = p.op("gpsimd", lambda e, s=sl, bb=b_, h=half, sc2=sc_:
                           e.tensor_mul(
                    out=sc2[:, 0:1024],
                    in0=sbc[bb][:, 1024 * h:1024 * (h + 1)],
                    in1=g2col[:, s:s + 1].broadcast_to([128, 1024])),
                    wait=[tsb, tgb] + p.wr(f"scr{half}"), inc="sP")
                p.use[f"scr{half}"] = [tcs]
                tw = p.op("vector", lambda e, s=sl, bb=b_, h=half, sc2=sc_:
                          e.scalar_tensor_tensor(
                    out=fw1[:, 2048 * s + 1024 * h:2048 * s + 1024 * (h + 1)],
                    in0=qst[bb][:, 1024 * h:1024 * (h + 1)],
                    scalar=-128.0, op0=ALU.add,
                    in1=sc2[:, 0:1024], op1=ALU.mult),
                    wait=[tcs, tq] + (p.wr("fw1st") if half == 0 else []) +
                         ([] if sl > 1 else list(extra_wait)), inc="sV")
                p.rd(f"qst{b_}", tw)
                p.rd(f"sbc{b_}", tw)
                p.rd(f"scr{half}", tw)
                if dbg and colrange[0] == 0 and sl == KSL - 1 and half == 1:
                    td_ = p.dma("sync", qd_d[:, :], qst[b_][:, 0:2048],
                                wait=[tq], q="qNO")
                    p.rd(f"qst{b_}", td_)
            tws.append(tw)
        return tws

    def dequant_w2(rowrange, extra_wait):
        tws = []
        for sl in range(16):
            b_ = sl % 2
            gsl = rowrange[0] // 128 + sl
            tq = p.dma("sync", qst[b_][:, 0:1024],
                       q2[rowrange[0] + 128 * sl: rowrange[0] + 128 * (sl + 1), :],
                       wait=p.wr(f"qst{b_}") + list(extra_wait), q=f"qQ{b_}")
            tsb = p.dma("sync", sbc[b_][:, 0:1024],
                        s2b_d[gsl:gsl + 1, :].broadcast_to([128, 1024]),
                        wait=[td2] + p.wr(f"sbc{b_}") + list(extra_wait), q=f"qSB{b_}")
            sc_ = scr[b_]
            tts = p.op("vector", lambda e, bb=b_, sc2=sc_: e.tensor_scalar(
                out=sc2[:, 0:1024], in0=qst[bb][:, 0:1024],
                scalar1=-128.0, scalar2=0.5, op0=ALU.add, op1=ALU.mult),
                wait=[tq] + p.wr(f"scr{b_}") + list(extra_wait), inc="sV")
            p.rd(f"qst{b_}", tts)
            p.use[f"scr{b_}"] = [tts]
            tw = p.op("vector", lambda e, s=sl, bb=b_, sc2=sc_: e.tensor_mul(
                out=fw2[:, D * s:D * (s + 1)], in0=sc2[:, 0:1024],
                in1=sbc[bb][:, 0:1024]),
                wait=[tts, tsb] + p.wr("fw2st") +
                     ([] if sl > 1 else list(extra_wait)), inc="sV")
            p.rd(f"sbc{b_}", tw)
            p.rd(f"scr{b_}", tw)
            tws.append(tw)
        return tws

    def beta_row(dst_off, w1toks, extra_wait):
        tvs = []
        for ns in range(4):
            ps = pool[2 + (ns % 2)]
            tmm = mm_group(ps[0:1, :],
                           [bg2col_b[:, k:k + 1] for k in range(KSL)],
                           [fw1[:, 2048 * k + 512 * ns: 2048 * k + 512 * (ns + 1)]
                            for k in range(KSL)],
                           wait=[tbg2, w1toks[-1]] + p.wr(f"pl{2 + ns % 2}") +
                                list(extra_wait))
            tv = p.op("vector", lambda e, n=ns, ps_=ps, o=dst_off:
                      e.scalar_tensor_tensor(
                out=rowtmp[0:1, 512 * n:512 * (n + 1)], in0=ps_[0:1, :], scalar=1.0,
                in1=b1row[0:1, o + 512 * n:o + 512 * (n + 1)],
                op0=ALU.mult, op1=ALU.add),
                wait=[tmm, tb1r] + p.wr("rowtmp") + list(extra_wait), inc="sV")
            p.use[f"pl{2 + ns % 2}"] = [tv]
            tvs.append(tv)
        td = p.dma("sync", cb1_d[0:1, dst_off:dst_off + 2048],
                   rowtmp[0:1, 0:2048], wait=tvs, q="qR")
        p.rd("rowtmp", td)
        tcvt = p.dma("gpsimd", cb1r[0:1, dst_off:dst_off + 2048],
                     cb1_d[0:1, dst_off:dst_off + 2048], wait=[td], q="qRB")
        return tcvt

    if stage < 3:
        p.op("gpsimd", lambda e: e.engine_nop(),
             wait=attn_done + [("qW0", p.cnt.get("qW0", 0)),
                               ("qC0", p.cnt.get("qC0", 0))])
        with nc.Block() as block:
            p.finish(block)
        return nc
    tw1a = dequant_w1((0, 2048), attn_done)
    tw2a = dequant_w2((0, 2048), attn_done)
    tcb1a = beta_row(0, tw1a, [])
    if dbg:
        for j_ in range(4):
            td_ = p.dma("sync", f1d_d[:, 4096 * j_:4096 * (j_ + 1)],
                        fw1[:, 4096 * j_:4096 * (j_ + 1)],
                        wait=list(tw1a), q="qNO")
            p.rd("fw1st", td_)
        td_ = p.dma("sync", cbr_d[0:1, :], cb1r[0:1, :], wait=[tcb1a],
                    q="qNO")
        p.rd("cb1rb", td_)
        td_ = p.dma("sync", cbd_d[:, 0:8], g2col[:, :], wait=[tcb1a],
                    q="qNO")
        td_ = p.dma("sync", cbd_d[:, 8:16], g2bias[:, :], wait=[td_],
                    q="qNO")
        td_ = p.dma("sync", cbd_d[:, 16:24], gcol[:, :], wait=[td_],
                    q="qNO")
        td_ = p.dma("sync", zbd_d[0:1, :], zb[0:1, 0:512], wait=[tzb],
                    q="qNO")

    def ffn_pass(passb, w1toks, w2toks, cbtok):
        def emit_group(g0, gs):
            gi = (g0 // 4) % 2
            GT = 128 * gs
            txns = []
            for j in range(gs):
                c = g0 + j
                tx = p.dma("sync",
                           fxn2[gi][:, :].rearrange("q (s t) -> q s t", s=KSL)
                               [:, :, 128 * j:128 * (j + 1)],
                           xn2_d[L * c:L * (c + 1), :].rearrange(
                               "q (s t) -> q s t", s=KSL),
                           wait=(p.wr(f"fxn2{gi}") if j == 0 else []) +
                                [w1toks[-1], w2toks[-1]],
                           q=f"qNI{j % 2}")
                txns.append(tx)
            tsl_last = None
            for nch in range(16):
                ps = pool[nch % 4]
                tmm = mm_group(ps[:, 0:GT],
                               [fw1[:, 2048 * k + 128 * nch:
                                    2048 * k + 128 * (nch + 1)]
                                for k in range(KSL)],
                               [fxn2[gi][:, 512 * k:512 * k + GT]
                                for k in range(KSL)],
                               wait=txns + [cbtok, w1toks[-1], tzb] +
                                    p.wr(f"pl{nch % 4}") +
                                    (p.wr("fh") if nch == 0 else []),
                               extra=(cb1r[0:1, 2048 * passb + 128 * nch:
                                           2048 * passb + 128 * (nch + 1)],
                                      zb[0:1, 0:GT]))
                tsl_last = p.op("scalar", lambda e, n=nch, ps_=ps, G_=GT:
                                e.activation(
                    out=fh[:, 512 * n:512 * n + G_], in_=ps_[:, 0:G_],
                    func=AF.Silu),
                    wait=[tmm], inc="sS")
                p.use[f"pl{nch % 4}"] = [tsl_last]
            p.rd(f"fxn2{gi}", tsl_last)

            for j in range(gs):
                c = g0 + j
                b_ = c % 2
                q_ = c % 4
                tz = []
                for hf in range(2):
                    extra = (ones[:, :], cb2row[0:1, 512 * hf:512 * (hf + 1)]) \
                        if passb else None
                    tmm = mm_group(outp[:, 512 * hf:512 * (hf + 1)],
                                   [fh[:, 512 * k + 128 * j:512 * k + 128 * (j + 1)]
                                    for k in range(16)],
                                   [fw2[:, D * k + 512 * hf: D * k + 512 * (hf + 1)]
                                    for k in range(16)],
                                   wait=[tsl_last, tcb2, w2toks[-1], tone] +
                                        p.wr(f"fop{hf}"),
                                   extra=extra)
                    tz.append(tmm)
                p.rd("fh", tz[-1])
                if not passb:
                    tev = p.op("vector", lambda e, bb=b_: e.tensor_copy(
                        out=ft[bb][:, :], in_=outp[:, :]),
                        wait=tz + p.wr(f"ft{b_}"), inc="sV")
                    p.use["fop0"] = [tev]
                    p.use["fop1"] = [tev]
                    tdz = p.dma("sync", z2a_d[L * c:L * (c + 1), :], ft[b_][:, :],
                                wait=[tev], q="qZO")
                    p.rd(f"ft{b_}", tdz)
                else:
                    tza = p.dma("sync", fz2a[b_][:, :], z2a_d[L * c:L * (c + 1), :],
                                wait=p.wr(f"fz2a{b_}"), q="qZI")
                    txd = p.dma("sync", fxmid[q_][:, :],
                                xmid_d[L * c:L * (c + 1), :],
                                wait=p.wr(f"fxmid{q_}"), q="qMI")
                    tt1 = p.op("vector", lambda e, bb=b_, q2_=q_: e.tensor_add(
                        out=ft[q2_][:, :], in0=outp[:, :], in1=fz2a[bb][:, :]),
                        wait=tz + [tza] + p.wr(f"ft{q_}"), inc="sV")
                    p.use["fop0"] = [tt1]
                    p.use["fop1"] = [tt1]
                    p.rd(f"fz2a{b_}", tt1)
                    tp1 = p.op("gpsimd", lambda e, q2_=q_, c_=c: e.tensor_mul(
                        out=fo[c_ % 2][:, :], in0=fxmid[q2_][:, :],
                        in1=m3c[:, 3 * c_ + 1:3 * c_ + 2]
                            .broadcast_to([128, D])),
                        wait=[txd] + p.wr(f"fo{c % 2}"), inc="sP")
                    p.rd(f"fxmid{q_}", tp1)
                    tp2 = p.op("gpsimd", lambda e, c_=c: e.tensor_add(
                        out=fo[c_ % 2][:, :], in0=fo[c_ % 2][:, :],
                        in1=m3c[:, 3 * c_ + 2:3 * c_ + 3]
                            .broadcast_to([128, D])),
                        wait=[tp1], inc="sP")
                    tt2 = p.op("vector", lambda e, q2_=q_, c_=c: e.tensor_add(
                        out=ft[q2_][:, :], in0=ft[q2_][:, :],
                        in1=fo[c_ % 2][:, :]),
                        wait=[tt1, tp2], inc="sV")
                    p.rd(f"fo{c % 2}", tt2)
                    sc4 = 32 + 48 * (c % 2)
                    tq4 = p.op("scalar", lambda e, q2_=q_, s4=sc4: e.activation(
                        out=fsq[:, :], in_=ft[q2_][:, :], func=AF.Square,
                        accum_out=stats[:, s4:s4 + 1]),
                        wait=[tt2] + p.wr("fsq") + p.wr(f"fst{sc4}"), inc="sS")
                    p.use["fsq"] = [tq4]
                    tr4 = p.op("vector", lambda e, q2_=q_, s4=sc4: e.reduce_sum(
                        out=stats[:, s4 + 1:s4 + 2], in_=ft[q2_][:, :], axis=AXX),
                        wait=[tt2] + p.wr(f"fst{sc4}b"), inc="sV")
                    w1_ = p.op("vector", lambda e, s4=sc4: e.tensor_scalar_mul(
                        out=stats[:, s4 + 2:s4 + 3], in0=stats[:, s4 + 1:s4 + 2],
                        scalar1=1.0 / D), wait=[tr4], inc="sV")
                    w2_ = p.op("vector", lambda e, s4=sc4: e.tensor_mul(
                        out=stats[:, s4 + 3:s4 + 4], in0=stats[:, s4 + 2:s4 + 3],
                        in1=stats[:, s4 + 2:s4 + 3]), wait=[w1_], inc="sV")
                    w3_ = p.op("vector", lambda e, s4=sc4: e.tensor_scalar(
                        out=stats[:, s4 + 4:s4 + 5], in0=stats[:, s4:s4 + 1],
                        scalar1=1.0 / D, scalar2=EPS, op0=ALU.mult, op1=ALU.add),
                        wait=[tq4], inc="sV")
                    w4_ = p.op("vector", lambda e, s4=sc4: e.tensor_sub(
                        out=stats[:, s4 + 5:s4 + 6], in0=stats[:, s4 + 4:s4 + 5],
                        in1=stats[:, s4 + 3:s4 + 4]), wait=[w2_, w3_], inc="sV")
                    p.rd(f"fst{sc4}", w3_)
                    p.rd(f"fst{sc4}b", w1_)
                    w5_ = p.op("vector", lambda e, q2_=q_, s4=sc4: e.tensor_copy(
                        out=stats[:, 96 + 2 * q2_:97 + 2 * q2_],
                        in_=stats[:, s4 + 5:s4 + 6]),
                        wait=[w4_] + p.wr(f"fq{q_}"), inc="sV")
                    w6_ = p.op("vector", lambda e, q2_=q_, s4=sc4: e.tensor_copy(
                        out=stats[:, 97 + 2 * q2_:98 + 2 * q2_],
                        in_=stats[:, s4 + 2:s4 + 3]),
                        wait=[w1_, w5_], inc="sV")
                    p.use[f"fvq{q_}"] = [w6_]

        def emit_epi(c):
                q_ = c % 4
                vtok = p.wr(f"fvq{q_}")
                e1 = p.op("scalar", lambda e, q2_=q_: e.activation(
                    out=stats[:, 104 + 4 * q2_:105 + 4 * q2_],
                    in_=stats[:, 96 + 2 * q2_:97 + 2 * q2_], func=AF.Ln),
                    wait=vtok, inc="sS")
                e2 = p.op("scalar", lambda e, q2_=q_: e.activation(
                    out=stats[:, 105 + 4 * q2_:106 + 4 * q2_],
                    in_=stats[:, 104 + 4 * q2_:105 + 4 * q2_], func=AF.Exp,
                    scale=-0.5), wait=[e1], inc="sS")
                e3 = p.op("vector", lambda e, q2_=q_: e.tensor_mul(
                    out=stats[:, 106 + 4 * q2_:107 + 4 * q2_],
                    in0=stats[:, 97 + 2 * q2_:98 + 2 * q2_],
                    in1=stats[:, 105 + 4 * q2_:106 + 4 * q2_]), wait=[e2], inc="sV")
                e4 = p.op("vector", lambda e, q2_=q_: e.tensor_scalar_mul(
                    out=stats[:, 107 + 4 * q2_:108 + 4 * q2_],
                    in0=stats[:, 106 + 4 * q2_:107 + 4 * q2_], scalar1=-1.0),
                    wait=[e3], inc="sV")
                p.rd(f"fq{q_}", e3)
                to1 = p.op("gpsimd", lambda e, q2_=q_, c_=c: e.tensor_mul(
                    out=fo[c_ % 2][:, :], in0=ft[q2_][:, :],
                    in1=stats[:, 105 + 4 * q2_:106 + 4 * q2_]
                        .broadcast_to([128, D])),
                    wait=[e4] + p.wr(f"fo{c % 2}"), inc="sP")
                p.rd(f"ft{q_}", to1)
                to2 = p.op("gpsimd", lambda e, q2_=q_, c_=c: e.tensor_add(
                    out=fo[c_ % 2][:, :], in0=fo[c_ % 2][:, :],
                    in1=stats[:, 107 + 4 * q2_:108 + 4 * q2_]
                        .broadcast_to([128, D])),
                    wait=[to1], inc="sP")
                ob = p.op("vector", lambda e, q2_=q_, c_=c: e.tensor_add(
                    out=fo[c_ % 2][:, :], in0=fo[c_ % 2][:, :],
                    in1=fxmid[q2_][:, :]),
                    wait=[to2], inc="sV")
                p.rd(f"fxmid{q_}", ob)
                tdo = p.dma("sync", out[L * c:L * (c + 1), :], fo[c % 2][:, :],
                            q="sO" if c % 2 == 0 else "sO2", wait=[ob])
                p.rd(f"fo{c % 2}", tdo)

        for g0 in range(0, NCH, 4):
            gs = min(4, NCH - g0)
            emit_group(g0, gs)
            if passb:
                for c in range(g0, g0 + gs):
                    emit_epi(c)

    if stage < 4:
        p.op("gpsimd", lambda e: e.engine_nop(),
             wait=[("sT", p.cnt["sT"]), ("sV", p.cnt["sV"]), ("sS", p.cnt["sS"]),
                   ("qC0", p.cnt.get("qC0", 0))])
        with nc.Block() as block:
            p.finish(block)
        return nc
    ffn_pass(False, tw1a, tw2a, tcb1a)

    passA = [("sT", p.cnt["sT"]), ("sS", p.cnt["sS"]), ("sV", p.cnt["sV"]),
             ("qZO", p.cnt["qZO"])] + \
        [(q_, p.cnt[q_]) for q_ in ("qNI0", "qNI1") if q_ in p.cnt]
    tw1b = dequant_w1((2048, 4096), passA)
    tw2b = dequant_w2((2048, 4096), passA)
    tcb1b = beta_row(2048, tw1b, [])

    ffn_pass(True, tw1b, tw2b, tcb1b)

    p.op("gpsimd", lambda e: e.engine_nop(),
         wait=[(q_, p.cnt[q_]) for q_ in ("sO", "sO2") if q_ in p.cnt])

    with nc.Block() as block:
        p.finish(block)
    return nc


def kernel(L0_attn_x, L0_attn_ln_g, L0_attn_ln_b, L0_attn_Wx, L0_attn_Ww,
           L0_attn_bw, L0_attn_Wk, L0_attn_Wv, L0_attn_Wr, L0_attn_Wo,
           L0_ffn_ln_gamma, L0_ffn_ln_beta, L0_ffn_fc1_q, L0_ffn_fc1_scales,
           L0_ffn_fc1_bias, L0_ffn_fc2_q, L0_ffn_fc2_scales, L0_ffn_fc2_bias):
    consts = build_constants()
    nc = build(T=2048)
    base = dict(
        ln_g=np.ascontiguousarray(L0_attn_ln_g),
        ln_b=np.ascontiguousarray(L0_attn_ln_b),
        bw=np.ascontiguousarray(L0_attn_bw),
        Wx=np.ascontiguousarray(L0_attn_Wx), Ww=np.ascontiguousarray(L0_attn_Ww),
        Wk=np.ascontiguousarray(L0_attn_Wk), Wv=np.ascontiguousarray(L0_attn_Wv),
        Wr=np.ascontiguousarray(L0_attn_Wr), Wo=np.ascontiguousarray(L0_attn_Wo),
        g2=np.ascontiguousarray(L0_ffn_ln_gamma),
        b2v=np.ascontiguousarray(L0_ffn_ln_beta),
        q1=np.ascontiguousarray(L0_ffn_fc1_q),
        s1=np.ascontiguousarray(L0_ffn_fc1_scales),
        b1=np.ascontiguousarray(L0_ffn_fc1_bias),
        q2=np.ascontiguousarray(L0_ffn_fc2_q),
        s2=np.ascontiguousarray(L0_ffn_fc2_scales),
        b2f=np.ascontiguousarray(L0_ffn_fc2_bias),
        **consts,
    )
    in_maps = [dict(base, x=np.ascontiguousarray(L0_attn_x[b])) for b in range(8)]
    res = run_bass_kernel_spmd(nc, in_maps, core_ids=list(range(8)))
    return np.stack([r["out"] for r in res.results])
